# version 14
# baseline (speedup 1.0000x reference)
import numpy as np
import ml_dtypes

import concourse.bass as bass
import concourse.mybir as mybir
from concourse.bass_utils import run_bass_kernel_spmd
from concourse.bass_core_rust import IndirectOffsetOnAxis

F32 = mybir.dt.float32
BF16 = mybir.dt.bfloat16
I32 = mybir.dt.int32
AF = mybir.ActivationFunctionType
ALU = mybir.AluOpType
AX = mybir.AxisListType
BF = ml_dtypes.bfloat16

B, S, D = 32, 2048, 1024
H = 16
NCORE, NSTEP = 8, 4
NT = S // 128
GT = 512
CAP = 8192
PAD = 128
PLANE = 2 * (CAP + PAD)
ZROW = NCORE * 2 * CAP
ENGS = ("tensor", "vector", "scalar", "gpsimd", "sync")
DEBUG_OUTS = ()


def _tables():
    t = {}
    ah = 1.0 / 10000.0 ** np.linspace(0.0, 1.0, 32, dtype=np.float64)
    idx = np.arange(S, dtype=np.float64)
    cos_h = np.cos(idx[:, None] * ah[None, :])
    sin_h = np.sin(idx[:, None] * ah[None, :])
    t["cos4"] = np.tile(cos_h.T.astype(np.float32), (4, 1)).astype(BF)
    sgn = np.repeat(np.array([-1.0, 1.0, -1.0, 1.0], np.float32), 32)
    t["sin4"] = (np.tile(sin_h.T.astype(np.float32), (4, 1)) * sgn[:, None]).astype(BF)
    decay = np.log(1.0 - 2.0 ** (-5.0 - np.arange(H, dtype=np.float64)))
    g_negt = np.exp(-decay[:, None] * idx[None, :])
    gk = np.empty((8, 128, S), np.float32)
    for j in range(8):
        gk[j, 0:64] = (64.0 ** -0.5) * g_negt[2 * j]
        gk[j, 64:128] = (64.0 ** -0.5) * g_negt[2 * j + 1]
    t["gkrep"] = gk.astype(BF)
    masksum = (1.0 - np.exp(decay[:, None] * (idx[None, :] + 1))) / (1.0 - np.exp(decay[:, None]))
    c_tab = (np.exp(decay[:, None] * idx[None, :]) / np.sqrt(masksum)).astype(np.float32)
    ca = np.empty((128, 256), np.float32)
    for h in range(H):
        for s0 in range(16):
            ca[:, h * 16 + s0] = c_tab[h, s0 * 128:(s0 + 1) * 128]
    t["c_all"] = ca
    tt, ss = np.meshgrid(np.arange(128), np.arange(128), indexing="ij")
    t["tri"] = (tt <= ss).astype(np.float32).astype(BF)
    t["su"] = (tt < ss).astype(np.float32).astype(BF)
    t["id128b"] = np.eye(128, dtype=np.float32).astype(BF)
    t["id8f"] = np.eye(8, dtype=np.float32)
    t["iota128r"] = np.broadcast_to(np.arange(128, dtype=np.float32), (128, 128)).copy()
    t["iota8r"] = np.broadcast_to(np.arange(8, dtype=np.float32), (128, 8)).copy()
    t["rii8r"] = (99.0 - t["iota8r"]).copy()
    t["ones1x"] = np.ones((1, 128), np.float32).astype(BF)
    t["iotacol"] = np.arange(128, dtype=np.float32).reshape(128, 1)
    t["onescol"] = np.ones((128, 1), np.float32).astype(BF)
    idc = np.empty((128, GT, 2), np.float32)
    p = np.arange(128)
    for tile in range(GT):
        idc[:, tile, 0] = (tile % 2) * 128 + p
        idc[:, tile, 1] = tile // 2
    t["idcols"] = idc.astype(BF)
    return t


QDEPTH = 4


class Prog:
    def __init__(self, nc):
        self.nc = nc
        qsems = tuple(f"{q}{i}" for q in ("LY", "LG", "LS") for i in range(QDEPTH))
        self.names = ("MM", "VV", "SS", "CC") + qsems
        self.sems = {n: nc.alloc_semaphore(n) for n in self.names}
        self.cnt = {n: 0 for n in self.names}
        self.hw = {e: {n: 0 for n in self.names} for e in ENGS}
        self.qidx = {"LY": 0, "LG": 0, "LS": 0}

    def emit(self, eng, fn, waits=(), inc=None, force_self=False):
        w = []
        for s, v in waits:
            if v <= 0 or self.hw[eng][s] >= v:
                continue
            self.hw[eng][s] = v
            w.append((s, v))
        self.ops_list(eng).append((fn, w, inc))
        if inc is not None:
            self.cnt[inc[0]] += inc[1]
        return self.cnt[inc[0]] if inc else None

    def ops_list(self, eng):
        if not hasattr(self, "_ops"):
            self._ops = {e: [] for e in ENGS}
        return self._ops[eng]

    def snap(self):
        return dict(self.cnt)

    def qbar(self, *qs):
        out = []
        for q in qs:
            for i in range(QDEPTH):
                s = f"{q}{i}"
                out.append((s, self.cnt[s]))
        return out

    def wsnap(self, sn, skip=()):
        return [(s, v) for s, v in sn.items() if s in self.names and s not in skip]

    def realize(self):
        with self.nc.Block() as blk:
            for name in ENGS:
                def mk(nm):
                    def body(e):
                        for fn, w, inc in self._ops[nm]:
                            for s, v in w:
                                e.wait_ge(self.sems[s], v)
                            inst = fn(e)
                            if inc is not None:
                                inst.then_inc(self.sems[inc[0]], inc[1])
                    return body
                getattr(blk, name)(mk(name))


def build_program(stop=None, dbg_keys=None):
    global DEBUG_OUTS
    if dbg_keys is not None:
        DEBUG_OUTS = tuple(dbg_keys)
    T = _tables()
    nc = bass.Bass("TRN2", target_bir_lowering=False)

    def din(name, shape, dt=F32):
        return nc.dram_tensor(name, shape, dt, kind="ExternalInput").ap()
    x4 = din("x4", [NSTEP, S, D])
    wqp = din("wqp", [D, D]); bqp = din("bqp", [128, 8])
    wkp = din("wkp", [D, D]); bkp = din("bkp", [128, 8])
    wvi = din("wvi", [D, 2 * D]); bvi = din("bvi", [2 * D])
    wgi = din("wgi", [D, 2 * D]); bgi = din("bgi", [2 * D])
    woi = din("woi", [2 * D, D]); boi = din("boi", [D])
    rwi = din("rwi", [D, 8])
    w1c = din("w1c", [D, 4096]); w2c = din("w2c", [4096, D])
    fbi = din("fbi", [D])
    onehot = din("onehot", [128, 8])
    ownrows = din("ownrows", [128, 64], I32)
    out_c = nc.dram_tensor("out_c", [NSTEP, S, D], F32, kind="ExternalOutput").ap()
    dbg = {}
    for k in DEBUG_OUTS:
        shp = {"xf": [NSTEP * S, D], "logits": [NSTEP, 128, 128],
               "islot": [2, GT * 128], "ycmp": [2 * CAP, D]}[k]
        dt = {"xf": F32, "logits": F32, "islot": I32, "ycmp": BF16}[k]
        dbg[k] = nc.dram_tensor("dbg_" + k, shp, dt, kind="ExternalOutput").ap()

    def cinl(key):
        return nc.inline_tensor(T[key], name="c_" + key).ap()
    cos4_d, sin4_d, gk_d = cinl("cos4"), cinl("sin4"), cinl("gkrep")
    call_d, tri_d, su_d = cinl("c_all"), cinl("tri"), cinl("su")
    id128_d, id8_d = cinl("id128b"), cinl("id8f")
    i128r_d, i8r_d, rii8_d = cinl("iota128r"), cinl("iota8r"), cinl("rii8r")
    ones1_d, onesc_d, idc_d = cinl("ones1x"), cinl("onescol"), cinl("idcols")
    iotac_d = cinl("iotacol")

    def dsc(name, shape, dt):
        return nc.dram_tensor(name, shape, dt, kind="Internal").ap()
    xbf_s = dsc("xbf_s", [2, S, D], BF16)
    v_s = dsc("v_s", [S, 2 * D], BF16)
    g_s = dsc("g_s", [S, 2 * D], BF16)
    gT_s = dsc("gT_s", [2 * D, S], BF16)
    xf32_s = dsc("xf32_s", [NSTEP, S, D], F32)
    xfb_s = dsc("xfb_s", [NSTEP * S, D], BF16)
    lg_s = dsc("lg_s", [NSTEP, 128, 128], F32)
    xfAG = dsc("xfAG", [NCORE * NSTEP * S, D], BF16)
    lgAG = dsc("lgAG", [NSTEP * NCORE * 128, 128], F32)
    cstage = dsc("cstage", [GT * 16], F32)
    bstage = dsc("bstage", [GT * 16], F32)
    ctflat = [dsc("ctflat0", [GT * 128, 1], I32), dsc("ctflat1", [GT * 128, 1], I32)]
    rowstage = dsc("rowstage", [4096], F32)
    yrw_s = dsc("yrw_s", [NCORE * NSTEP * S, 4], F32)
    comp_s = dsc("comp_s", [2 * CAP, D], BF16)
    ycmp_s = dsc("ycmp_s", [2 * CAP, D], BF16)
    yAG = dsc("yAG", [ZROW, D], BF16)

    A = nc.alloc_sbuf_tensor("BIGA", [128, 16384], BF16)
    Bg = nc.alloc_sbuf_tensor("BIGB", [128, 16384], BF16)
    C = nc.alloc_sbuf_tensor("BIGC", [128, 16384], BF16)
    Dg = nc.alloc_sbuf_tensor("BIGD", [128, 16384], BF16)
    E = nc.alloc_sbuf_tensor("BIGE", [128, 16384], BF16)
    Fg = nc.alloc_sbuf_tensor("BIGF", [128, 8192], BF16, is_transpose_dest=True)
    WK = nc.alloc_sbuf_tensor("WK", [128, 2048], F32)
    WK2 = nc.alloc_sbuf_tensor("WK2", [128, 2048], F32, is_transpose_dest=True)
    REPB = nc.alloc_sbuf_tensor("REPB", [128, 2048], BF16)
    REPF = nc.alloc_sbuf_tensor("REPF", [128, 1024], F32)
    SM = nc.alloc_sbuf_tensor("SM", [128, 512], F32)
    SMB = nc.alloc_sbuf_tensor("SMB", [128, 640], BF16)
    IDC = nc.alloc_sbuf_tensor("IDC", [128, 1024], BF16)
    ISL = nc.alloc_sbuf_tensor("ISL", [128, 140], I32)
    OWN = nc.alloc_sbuf_tensor("OWN", [128, 64], I32)
    ONESF = nc.alloc_sbuf_tensor("ONESF", [1, 128], F32)
    P = [nc.alloc_psum_tensor(f"P{i}", [128, 512], F32) for i in range(8)]
    assert nc.sbuf_bytes_remaining >= 0

    SM_call = SM[:, 0:256]
    SM_i8 = SM[:, 256:264]
    SM_rii8 = SM[:, 264:272]
    SM_id8 = SM[:, 272:280]
    SM_bq = SM[:, 280:288]
    SM_bk = SM[:, 288:296]
    SM_oh = SM[:, 296:304]
    SM_i128 = SM[:, 304:432]
    SMc = lambda i: SM[:, 432 + i:433 + i]
    SM_iotac = SM[:, 448:449]
    SM_base = [SM[:, 452 + 12 * k:456 + 12 * k] for k in range(2)]
    SM_basep1 = [SM[:, 456 + 12 * k:460 + 12 * k] for k in range(2)]
    SM_cnt = [SM[:, 460 + 12 * k:464 + 12 * k] for k in range(2)]
    SMB_tri = SMB[:, 0:128]
    SMB_su = SMB[:, 128:256]
    SMB_id = SMB[:, 256:384]
    SMB_ones1 = SMB[0:1, 384:512]
    SMB_onesc = SMB[:, 512:513]
    SMB_rw = SMB[:, 520:584].rearrange("p (dt e) -> p dt e", e=8)

    pg = Prog(nc)
    ZT = ("MM", 0)

    def dma(eng, q, out, in_, waits=(), **kw):
        s = f"{q}{pg.qidx[q] % QDEPTH}"
        pg.qidx[q] += 1
        def f(e, o=out, i=in_, k=dict(kw)):
            return e.dma_start(out=o, in_=i, **k)
        pg.emit(eng, f, waits=list(waits) + [(s, pg.cnt[s])], inc=(s, 16))
        return (s, pg.cnt[s])

    def mm(out, lhsT, rhs, start, stop, waits=()):
        def f(e, o=out, l=lhsT, r=rhs, st=start, sp=stop):
            return e.matmul(o, l, r, start=st, stop=sp)
        return pg.emit("tensor", f, waits=waits, inc=("MM", 1))

    def tr(out, in_, ident, waits=()):
        def f(e, o=out, i=in_, d=ident):
            return e.transpose(o, i, d)
        return pg.emit("tensor", f, waits=waits, inc=("MM", 1))

    def vop(fn, waits=()):
        return pg.emit("vector", fn, waits=list(waits) + [("VV", pg.cnt["VV"])],
                       inc=("VV", 1), force_self=True)

    def sop(fn, waits=()):
        return pg.emit("scalar", fn, waits=list(waits) + [("SS", pg.cnt["SS"])],
                       inc=("SS", 1), force_self=True)

    def gop(fn, waits=(), inc=None):
        if inc is not None:
            pg.emit("gpsimd", fn, waits=waits, inc=inc)
            return ("CC", pg.cnt["CC"])
        s = f"LG{pg.qidx['LG'] % QDEPTH}"
        pg.qidx["LG"] += 1
        pg.emit("gpsimd", fn, waits=list(waits) + [(s, pg.cnt[s])], inc=(s, 16))
        return (s, pg.cnt[s])

    dma("sync", "LY", SM_call, call_d)
    dma("sync", "LY", SM_i8, i8r_d)
    dma("sync", "LY", SM_rii8, rii8_d)
    dma("sync", "LY", SM_id8[0:8, :], id8_d)
    dma("sync", "LY", SM_bq, bqp)
    dma("sync", "LY", SM_bk, bkp)
    dma("sync", "LY", SM_oh, onehot)
    dma("sync", "LY", SM_i128, i128r_d)
    dma("sync", "LY", SM_iotac, iotac_d)
    dma("sync", "LY", SMB_tri, tri_d)
    dma("sync", "LY", SMB_su, su_d)
    dma("sync", "LY", SMB_id, id128_d)
    dma("sync", "LY", SMB_ones1, ones1_d)
    dma("sync", "LY", SMB_onesc, onesc_d)
    dma("sync", "LY", IDC[:].rearrange("p (t two) -> p t two", two=2), idc_d)
    dma("sync", "LY", OWN[:], ownrows)
    lrw = dma("gpsimd", "LG", SMB_rw, rwi.rearrange("(dt p) e -> p dt e", p=128))
    zv = vop(lambda e: e.memset(WK[:], 0.0))
    vop(lambda e: e.memset(ONESF[:], 1.0))
    lbo = dma("sync", "LY", WK2[0:1, 0:1024], boi.rearrange("(a d) -> a d", a=1))
    ov1 = vop(lambda e: e.memset(WK2[0:1, 1024:1152], 1.0))
    rv0 = 0
    for sl in range(2):
        m = mm(P[6][:, 0:512], WK2[0:1, 1024:1152], WK2[0:1, sl * 512:(sl + 1) * 512],
               True, True, waits=[lbo, ("VV", ov1), ("VV", rv0)])
        rv0 = vop(lambda e, sl=sl: e.tensor_copy(REPF[:, sl * 512:(sl + 1) * 512], P[6][:, 0:512]),
                  waits=[("MM", m)])
    init_snap = pg.snap()

    xT = A[:].rearrange("p (dt s) -> p dt s", dt=8)
    qrT = Bg[:].rearrange("p (dt s) -> p dt s", dt=8)
    krT = C[:].rearrange("p (dt s) -> p dt s", dt=8)
    W2b = WK2[:].bitcast(BF16)
    qtmp = W2b[:, 0:2048]
    qsw = W2b[:, 2048:4096]
    F_gk = Fg[:, 0:2048]
    F_t2 = Fg[:, 2048:4096]
    F_cos = Fg[:, 4096:6144]
    F_sin = Fg[:, 6144:8192]
    wq_sb = E[:, 0:8192].rearrange("p (dt o) -> p dt o", dt=8)
    wk_sb = E[:, 8192:16384].rearrange("p (dt o) -> p dt o", dt=8)
    wvg_sb = E[:].rearrange("p (dt o) -> p dt o", dt=8)
    E_wo = E[:].rearrange("p (g d) -> p g d", g=16)
    D_xfb = Dg[:, 8192:9216]
    D_lgf = Dg[:, 9216:9472].bitcast(F32)
    D_xfT = [Bg[:, 12288 + i * 2048: 12288 + (i + 1) * 2048] for i in range(2)]

    prev_step = init_snap
    lcos = dma("sync", "LY", F_cos, cos4_d, waits=pg.wsnap(init_snap))
    lsin = dma("sync", "LY", F_sin, sin4_d)
    pre_x, pre_w = {}, {}
    xt_prev = []
    ov = 0
    tr_last = 0
    lg_marks = []
    ag_n = 0
    _sub = stop is not None and (stop in ('a0', 'b0', 'c0', 'd0', 'ef0', 'dskip') or stop.startswith('dh') or stop.startswith('dm') or stop in ('dep', 'dld', 'dsa', 'dva'))
    nstep_run = 0 if stop == 'p0' else (1 if _sub else NSTEP)
    for b in range(nstep_run):
        if b in pre_x:
            lgc, xt_l = pre_x.pop(b)
        else:
            lgc = dma("gpsimd", "LG", xbf_s[b % 2], x4[b], waits=pg.wsnap(prev_step, ("CC",)))
            xt_l = [dma("scalar", "LS", xT[:, j, :], xbf_s[b % 2][:, j * 128:(j + 1) * 128],
                        waits=[lgc], transpose=True) for j in range(8)]
        if b in pre_w:
            lq, lk = pre_w.pop(b)
        else:
            lq = dma("gpsimd", "LG", wq_sb, wqp.rearrange("(dt p) o -> p dt o", p=128),
                     waits=pg.wsnap(prev_step, ("CC",)))
            lk = dma("gpsimd", "LG", wk_sb, wkp.rearrange("(dt p) o -> p dt o", p=128))

        if stop == 'a0':
            continue
        pa_free_s = [0, 0]
        last_rot = 0
        for which in range(2):
            wsb, wld = (wq_sb, lq) if which == 0 else (wk_sb, lk)
            bcol = SM_bq if which == 0 else SM_bk
            dstT = qrT if which == 0 else krT
            for j in range(8):
                if which == 1:
                    lgk = dma("gpsimd", "LG", F_gk, gk_d[j],
                              waits=[("VV", last_rot)] +
                                    (pg.wsnap(prev_step, ("CC",)) if j == 0 else []))
                for sl in range(4):
                    ps = P[sl % 2]
                    for dt in range(8):
                        w = ([wld, ("VV", max(pa_free_s[sl % 2], last_rot))]
                             if dt == 0 else []) + [xt_l[dt]]
                        m = mm(ps[:, 0:512], wsb[:, dt, j * 128:(j + 1) * 128],
                               xT[:, dt, sl * 512:(sl + 1) * 512], dt == 0, dt == 7, waits=w)
                    pa_free_s[sl % 2] = vop(
                        lambda e, ps=ps, sl=sl, bc=bcol, j=j: e.tensor_scalar(
                            qtmp[:, sl * 512:(sl + 1) * 512], ps[:, 0:512],
                            bc[:, j:j + 1], None, op0=ALU.add),
                        waits=[("MM", m), ("VV", last_rot)])
                sse = max(pa_free_s)
                if which == 1:
                    gv = vop(lambda e: e.tensor_mul(qtmp[:], qtmp[:], F_gk[:]),
                             waits=[("VV", sse), lgk])
                else:
                    gv = 0
                w0 = [("VV", max(sse, gv)), lsin, lcos] + \
                     (pg.wsnap(prev_step, ("CC",)) if (which == 0 and j == 0) else [])
                vop(lambda e: e.tensor_copy(qsw[0:32, :], qtmp[32:64, :]), waits=w0)
                vop(lambda e: e.tensor_copy(qsw[32:64, :], qtmp[0:32, :]))
                vop(lambda e: e.tensor_copy(qsw[64:96, :], qtmp[96:128, :]))
                c4 = vop(lambda e: e.tensor_copy(qsw[96:128, :], qtmp[64:96, :]))
                vop(lambda e, dT=dstT, j=j: e.tensor_mul(dT[:, j, :], qtmp[:], F_cos[:]))
                t2 = vop(lambda e: e.tensor_mul(F_t2[:], qsw[:], F_sin[:]), waits=[("VV", c4)])
                last_rot = vop(lambda e, dT=dstT, j=j: e.tensor_add(dT[:, j, :], dT[:, j, :], F_t2[:]),
                               waits=[("VV", t2)])
        rot_done = pg.snap()

        if stop == 'b0':
            continue
        vg_last_store = ZT
        for which in range(2):
            wten, brow_d = (wvi, bvi) if which == 0 else (wgi, bgi)
            dst_d = v_s if which == 0 else g_s
            lw = dma("gpsimd", "LG", wvg_sb, wten.rearrange("(dt p) o -> p dt o", p=128),
                     waits=(pg.wsnap(rot_done) if which == 0 else
                            [("VV", pg.cnt["VV"]), ("MM", pg.cnt["MM"])]))
            lbr = dma("gpsimd", "LG", REPB[0:1, :], brow_d.rearrange("(a d) -> a d", a=1),
                      waits=pg.wsnap(rot_done))
            repv = 0
            for sl in range(4):
                m = mm(P[2][:, 0:512], SMB_ones1,
                       REPB[0:1, sl * 512:(sl + 1) * 512], True, True,
                       waits=[lbr, ("VV", repv)])
                repv = vop(lambda e, sl=sl: e.tensor_copy(REPB[:, sl * 512:(sl + 1) * 512], P[2][:, 0:512]),
                           waits=[("MM", m)])
            pv_free = [repv, repv]
            for tt in range(NT):
                ev = 0
                for dvs in range(4):
                    ps = P[dvs % 2]
                    for dt in range(8):
                        w = ([lw, ("VV", pv_free[dvs % 2])] if dt == 0 else []) + [xt_l[dt]]
                        m = mm(ps[:, 0:512], xT[:, dt, tt * 128:(tt + 1) * 128],
                               wvg_sb[:, dt, dvs * 512:(dvs + 1) * 512], dt == 0, dt == 7, waits=w)
                    ev = vop(lambda e, ps=ps, dvs=dvs: e.tensor_add(
                        qtmp[:, dvs * 512:(dvs + 1) * 512], ps[:, 0:512],
                        REPB[:, dvs * 512:(dvs + 1) * 512]),
                        waits=[("MM", m), vg_last_store])
                    pv_free[dvs % 2] = ev
                vg_last_store = dma("sync", "LY", dst_d[tt * 128:(tt + 1) * 128, :], qtmp[:],
                                    waits=[("VV", ev)])

        if stop == 'c0':
            continue
        vg_bar = pg.qbar("LY")
        lwo = dma("gpsimd", "LG", E_wo, woi.rearrange("(g p) d -> p g d", p=128),
                  waits=[("MM", pg.cnt["MM"])])
        if b + 1 < nstep_run:
            nb = b + 1
            plgc = dma("gpsimd", "LG", xbf_s[nb % 2], x4[nb], waits=list(xt_prev))
            pxt = [dma("scalar", "LS", xT[:, j, :], xbf_s[nb % 2][:, j * 128:(j + 1) * 128],
                       waits=[plgc, ("MM", pg.cnt["MM"])], transpose=True) for j in range(8)]
            pre_x[nb] = (plgc, pxt)
        xt_prev = xt_l
        D_vre4 = [Dg[:, hr * 2112:(hr + 1) * 2112].rearrange("p (tt c) -> p tt c", c=132)
                  for hr in range(4)]
        D_gh2 = [Dg[:, 8448 + i * 2048:8448 + (i + 1) * 2048].rearrange("p (q d) -> p q d", d=512)
                 for i in range(2)]
        D_owb = Dg[:, 12544:14592].rearrange("p (s0 d) -> p s0 d", d=128)
        D_st = [Dg[:, 14592 + i * 512:14592 + (i + 1) * 512] for i in range(3)]
        D_gacc = [Fg[:, i * 2048:(i + 1) * 2048] for i in range(2)]
        SG_all = W2b[:, 2048:4096]
        OrawR = WK[:, 384:900].rearrange("p (rs c) -> p rs c", c=129)
        R4 = WK[:, 96:100]
        gat_all = W2b[:, 0:2048]
        if b == 0:
            ov = vop(lambda e: e.memset(
                Dg[:, 0:8448].rearrange("p (x c) -> p x c", c=132)[:, :, 128:129], 1.0),
                waits=pg.wsnap(prev_step, ("CC",)))
        MM1P = (0, 1, 7)
        st_mm_free = [0, 0, 0]
        pa_s = [0, 0, 0]
        pa_v = [pg.cnt["VV"]] * 3
        slot_i = 0
        po_ep = [0, 0, 0, 0]
        vre_mm4 = [0, 0, 0, 0]
        gh_ss = [0, 0]
        gacc_fl = [[], []]
        pmt4 = [0, 0, 0, 0]
        vg2_4 = [0, 0, 0, 0]
        ghp = 0
        _dH = 0 if stop == 'dskip' else H
        if stop is not None and stop.startswith("dh"):
            _dH = int(stop[2:])
        tail_q = []
        for hq in range(0, H, 4):
            if hq >= _dH:
                break
            lvh = []
            for hr in range(4):
                h = hq + hr
                lvh.append(dma("sync", "LY", D_vre4[hr][:, :, 0:128],
                               v_s.rearrange("(tt p) o -> p tt o", p=128)[:, :, h * 128:(h + 1) * 128],
                               waits=(vg_bar if hq == 0 else []) + [("VV", ov),
                                     ("MM", pg.cnt["MM"] if hq == 0 else vre_mm4[hr])]))
            for g in range(4):
                s_lo = g * 4
                pp = ghp % 2
                lgh = dma("sync", "LY", D_gh2[pp][:, :, :],
                          g_s.rearrange("(tt p) o -> p tt o", p=128)[:, 4 * g:4 * g + 4,
                                                                    hq * 128:hq * 128 + 512],
                          waits=(vg_bar if ghp < 2 else []) + [("SS", gh_ss[pp])])
                ghp += 1

                def emit_mm2(ent, s_lo=s_lo):
                    sl3, t0, hr2, sv, vv = ent
                    lastm = 0
                    for s0 in range(max(s_lo, t0), s_lo + 4):
                        rs = s0 - s_lo
                        lastm = mm(P[2 + rs][:, 0:129],
                                   D_st[sl3][:, rs * 128:(rs + 1) * 128],
                                   D_vre4[hr2][:, t0, 0:129], t0 == 0, s0 == t0,
                                   waits=[("SS", sv),
                                          ("VV", max(vv, po_ep[rs] if t0 == 0 else 0)),
                                          lvh[hr2]])
                    st_mm_free[sl3] = lastm
                    vre_mm4[hr2] = max(vre_mm4[hr2], lastm)

                for hr in range(4):
                    h = hq + hr
                    if h >= _dH:
                        continue
                    j, half = divmod(h, 2)
                    pb = half * 64
                    pend = []
                    for t0 in range(s_lo + 4):
                        sl3 = slot_i % 3
                        slot_i += 1
                        ps = P[MM1P[sl3]]
                        m = mm(ps[:, 0:512], krT[pb:pb + 64, j, t0 * 128:(t0 + 1) * 128],
                               qrT[pb:pb + 64, j, g * 512:(g + 1) * 512], True, True,
                               waits=[("SS", pa_s[sl3]), ("VV", pa_v[sl3])])
                        rel = t0 - s_lo
                        sv = vv = 0
                        if rel < 0:
                            sv = sop(lambda e, ps=ps, sl=sl3: e.activation(
                                D_st[sl][:, 0:512], ps[:, 0:512], AF.Copy),
                                waits=[("MM", max(m, st_mm_free[sl3]))])
                        else:
                            c0 = rel * 128
                            vv = vop(lambda e, ps=ps, sl=sl3, c0=c0: e.tensor_mul(
                                D_st[sl][:, c0:c0 + 128], ps[:, c0:c0 + 128], SMB_tri),
                                waits=[("MM", max(m, st_mm_free[sl3]))])
                            if c0 + 128 < 512:
                                vv = vop(lambda e, ps=ps, sl=sl3, c0=c0: e.tensor_copy(
                                    D_st[sl][:, c0 + 128:512], ps[:, c0 + 128:512]))
                        pa_s[sl3] = max(pa_s[sl3], sv)
                        pa_v[sl3] = max(pa_v[sl3], vv)
                        pend.append((sl3, t0, hr, sv, vv))
                        if tail_q:
                            tail_q.pop(0)()
                        if len(pend) > 2:
                            emit_mm2(pend.pop(0))
                    while pend:
                        emit_mm2(pend.pop(0))
                    if tail_q:
                        tail_q.pop(0)()
                    mfin = pg.cnt["MM"]
                    for rs in range(4):
                        po_ep[rs] = vop(lambda e, rs=rs: e.tensor_copy(
                            OrawR[:, rs, :], P[2 + rs][:, 0:129]),
                            waits=[("MM", mfin)] if rs == 0 else [])
                    csl4 = SM_call[:, h * 16 + s_lo: h * 16 + s_lo + 4]
                    tc, tn, wc = WK[:, 104:108], WK[:, 108:112], WK[:, 100:104]
                    vop(lambda e: e.tensor_copy(
                        R4, OrawR[:, :, 128:129].rearrange("p a c -> p (a c)")))
                    vop(lambda e, csl4=csl4: e.tensor_tensor(tc, csl4, R4, op=ALU.mult))
                    vop(lambda e: e.tensor_scalar(tn, tc, -1.0, None, op0=ALU.mult))
                    vop(lambda e: e.tensor_tensor(tc, tc, tn, op=ALU.max))
                    vop(lambda e: e.tensor_scalar(tc, tc, 1.0, None, op0=ALU.max))
                    vop(lambda e: e.reciprocal(tc, tc))
                    vop(lambda e, csl4=csl4: e.tensor_mul(wc, csl4, tc))
                    for rs in range(4):
                        idx = hr * 4 + rs
                        mus = WK[:, idx:idx + 1]
                        sqs = WK[:, 16 + idx:17 + idx]
                        vop(lambda e, rs=rs, idx=idx, mus=mus: e.tensor_scalar(
                            D_owb[:, idx, :], OrawR[:, rs, 0:128], wc[:, rs:rs + 1], 0.0,
                            op0=ALU.mult, op1=ALU.add, accum_out=mus))
                        vop(lambda e, idx=idx, sqs=sqs: e.scalar_tensor_tensor(
                            WK[:, 128:256], D_owb[:, idx, :], 1.0, D_owb[:, idx, :],
                            op0=ALU.mult, op1=ALU.mult, accum_out=sqs))
                nm16, var16, rstd16, e16 = WK[:, 32:48], WK[:, 48:64], WK[:, 64:80], WK[:, 80:96]
                vop(lambda e: e.tensor_scalar(nm16, WK[:, 0:16], -1.0 / 128.0, None, op0=ALU.mult))
                vop(lambda e: e.tensor_tensor(e16, WK[:, 0:16], WK[:, 0:16], op=ALU.mult))
                vop(lambda e: e.tensor_scalar(e16, e16, -1.0 / 16384.0, None, op0=ALU.mult))
                vop(lambda e: e.scalar_tensor_tensor(
                    var16, WK[:, 16:32], 1.0 / 128.0, e16, op0=ALU.mult, op1=ALU.add))
                vv3 = vop(lambda e: e.tensor_scalar(var16, var16, 1e-5, None, op0=ALU.add))
                s2 = sop(lambda e: e.activation(rstd16, var16, AF.Ln), waits=[("VV", vv3)])
                s3 = sop(lambda e: e.activation(rstd16, rstd16, AF.Exp, scale=-0.5))
                gh_ss[pp] = s4 = sop(lambda e, pp=pp: e.activation(
                    SG_all, D_gh2[pp][:, :, :].rearrange("p q d -> p (q d)"), AF.Silu),
                    waits=[lgh, ("VV", pg.cnt["VV"])])
                pp2 = (ghp - 1) % 2
                first_p2 = True
                ucell = {"first": True}
                for hr in range(4):
                    h = hq + hr
                    if h >= _dH:
                        continue
                    for rs in range(4):
                        idx = hr * 4 + rs
                        ctr = WK[:, 256:384]
                        vop(lambda e, idx=idx: e.tensor_scalar(
                            ctr, D_owb[:, idx, :], nm16[:, idx:idx + 1], None, op0=ALU.add),
                            waits=[("SS", s3)] if first_p2 else [])
                        v13 = vop(lambda e, idx=idx, rs=rs, hr=hr: e.scalar_tensor_tensor(
                            gat_all[:, idx * 128:(idx + 1) * 128], ctr, rstd16[:, idx:idx + 1],
                            SG_all[:, rs * 512 + hr * 128: rs * 512 + (hr + 1) * 128],
                            op0=ALU.mult, op1=ALU.mult),
                            waits=[("SS", s4)] + ([("MM", tr_last)] if first_p2 else []))
                        first_p2 = False

                        def t_go(idx=idx, hr=hr, rs=rs, pp2=pp2, v13=v13, ucell=ucell):
                            nonlocal tr_last
                            sl4 = idx % 4
                            pt = P[6][:, sl4 * 64:(sl4 + 1) * 64].bitcast(BF16)
                            pmt4[sl4] = tr_last = tr(
                                pt, gat_all[:, idx * 128:(idx + 1) * 128], SMB_id,
                                waits=[("VV", v13), ("VV", max(vg2_4))])
                            wfl = []
                            if ucell["first"]:
                                ucell["first"] = False
                                wfl = gacc_fl[pp2]
                                gacc_fl[pp2] = []
                            vg2_4[sl4] = vop(lambda e, pt=pt, hr=hr, rs=rs, pp2=pp2: e.tensor_copy(
                                D_gacc[pp2][:, hr * 512 + rs * 128: hr * 512 + (rs + 1) * 128], pt),
                                waits=[("MM", pmt4[sl4])] + wfl)
                        tail_q.append(t_go)

                    def f_go(h=h, g=g, hr=hr, pp2=pp2):
                        gacc_fl[pp2].append(dma(
                            "sync", "LY", gT_s[h * 128:(h + 1) * 128, g * 512:(g + 1) * 512],
                            D_gacc[pp2][:, hr * 512:(hr + 1) * 512],
                            waits=[("VV", vg2_4[(hr * 4 + 3) % 4])]))
                    tail_q.append(f_go)

        while tail_q:
            tail_q.pop(0)()
        d_done = pg.snap()

        if stop == 'd0':
            continue
        D_gstat = [Dg[:, i * 2048:(i + 1) * 2048].rearrange("p (g s) -> p g s", s=128)
                   for i in range(2)]
        gst_mm = [0, 0]
        pc_v = 0
        xf_l2 = ZT
        prev_l1 = ZT
        prev_l2 = ZT
        for tt in range(NT):
            pp = tt % 2
            lg2 = dma("sync", "LY", D_gstat[pp],
                      gT_s[:, tt * 128:(tt + 1) * 128].rearrange("(g p) s -> p g s", p=128),
                      waits=[("MM", gst_mm[pp])] + pg.wsnap(d_done))
            lx = dma("sync", "LY", WK[:, 1024:2048], x4[b][tt * 128:(tt + 1) * 128, :])
            mlast = 0
            for dgt in range(16):
                for h2 in range(2):
                    w = ([lg2, lwo, ("VV", pc_v)] + pg.wsnap(d_done)) if dgt == 0 else ()
                    mlast = mm(P[4 + h2][:, 0:512], D_gstat[pp][:, dgt, :],
                               E_wo[:, dgt, h2 * 512:(h2 + 1) * 512], dgt == 0, dgt == 15,
                               waits=w)
            gst_mm[pp] = mlast
            rbuf = WK[:, 0:1024]
            vop(lambda e: e.tensor_add(rbuf[:, 0:512], P[4][:, 0:512], REPF[:, 0:512]),
                waits=[("MM", mlast), prev_l1] + pg.wsnap(d_done))
            pc_v = vop(lambda e: e.tensor_add(rbuf[:, 512:1024], P[5][:, 0:512], REPF[:, 512:1024]))
            e3 = vop(lambda e: e.tensor_add(rbuf, rbuf, WK[:, 1024:2048]), waits=[lx])
            nm, var, rstd, sqs = SMc(7), SMc(8), SMc(9), SMc(10)
            vop(lambda e, nm=nm: e.reduce_sum(nm, rbuf, axis=AX.X, negate=True), waits=[("VV", e3)])
            vop(lambda e, nm=nm: e.tensor_scalar(nm, nm, 1.0 / 1024.0, None, op0=ALU.mult))
            v3 = vop(lambda e, nm=nm: e.tensor_scalar(rbuf, rbuf, nm, None, op0=ALU.add))
            s1 = vop(lambda e, sqs=sqs: e.scalar_tensor_tensor(
                WK[:, 1024:2048], rbuf, 1.0, rbuf, op0=ALU.mult, op1=ALU.mult, accum_out=sqs),
                waits=[("VV", v3)])
            v4 = vop(lambda e, var=var, sqs=sqs: e.tensor_scalar(var, sqs, 1.0 / 1024.0, 1e-5,
                                                                 op0=ALU.mult, op1=ALU.add))
            s2 = sop(lambda e, rstd=rstd, var=var: e.activation(rstd, var, AF.Ln), waits=[("VV", v4)])
            s3 = sop(lambda e, rstd=rstd: e.activation(rstd, rstd, AF.Exp, scale=-0.5))
            v5 = vop(lambda e, rstd=rstd: e.tensor_scalar(rbuf, rbuf, rstd, None, op0=ALU.mult),
                     waits=[("SS", s3)])
            v6 = vop(lambda e: e.tensor_copy(D_xfb[:], rbuf), waits=[prev_l2])
            prev_l1 = dma("sync", "LY", xf32_s[b][tt * 128:(tt + 1) * 128, :], rbuf,
                          waits=[("VV", v5)])
            prev_l2 = xf_l2 = dma("sync", "LY",
                                  xfb_s[b * S + tt * 128: b * S + (tt + 1) * 128, :], D_xfb,
                                  waits=[("VV", v6)])
        if b + 1 < nstep_run:
            pre_w[b + 1] = (
                dma("gpsimd", "LG", wq_sb, wqp.rearrange("(dt p) o -> p dt o", p=128),
                    waits=[("MM", max(gst_mm))]),
                dma("gpsimd", "LG", wk_sb, wkp.rearrange("(dt p) o -> p dt o", p=128)))
        xfb_bar = pg.qbar("LY")
        xfT_mm = [0, 0]
        mm_d = pg.cnt["MM"]
        for dt in range(8):
            pp = dt % 2
            lt = dma("scalar", "LS", D_xfT[pp], xfb_s[b * S:(b + 1) * S, dt * 128:(dt + 1) * 128],
                     waits=xfb_bar + [("MM", xfT_mm[pp] if dt >= 2 else mm_d)], transpose=True)
            for sl in range(4):
                xfT_mm[pp] = mm(P[sl][0:8, 0:512], SMB_rw[:, dt, :],
                                D_xfT[pp][:, sl * 512:(sl + 1) * 512], dt == 0, dt == 7,
                                waits=[lt, lrw])
        mrt = pg.cnt["MM"]
        lgTrow = WK[0:8, 0:2048]
        lgv = 0
        for sl in range(4):
            lgv = vop(lambda e, sl=sl: e.tensor_copy(lgTrow[:, sl * 512:(sl + 1) * 512],
                                                     P[sl][0:8, 0:512]),
                      waits=[("MM", mrt)])
        lgsb = D_lgf[:, 0:128].rearrange("p (tt e) -> p tt e", e=8)
        lgev = 0
        for tt in range(NT):
            mt = tr(P[6][:, 0:8], lgTrow[:, tt * 128:(tt + 1) * 128], SM_id8[0:8, :],
                    waits=[("VV", lgv), ("VV", lgev)])
            lgev = vop(lambda e, tt=tt: e.tensor_copy(lgsb[:, tt, :], P[6][:, 0:8]),
                       waits=[("MM", mt)])
        l3 = dma("sync", "LY", lg_s[b], lgsb.rearrange("p tt e -> p (tt e)"), waits=[("VV", lgev)])
        if stop == 'ef0':
            continue
        gop(lambda e, b=b: e.collective_compute(
            "AllGather", ALU.bypass, replica_groups=[list(range(NCORE))],
            ins=[lg_s[b].opt()],
            outs=[lgAG[b * NCORE * 128:(b + 1) * NCORE * 128, :].opt()]),
            waits=[l3], inc=("CC", 1))
        gop(lambda e, b=b: e.collective_compute(
            "AllGather", ALU.bypass, replica_groups=[list(range(NCORE))],
            ins=[xfb_s[b * S:(b + 1) * S, :].opt()],
            outs=[xfAG[b * NCORE * S:(b + 1) * NCORE * S, :].opt()]),
            waits=xfb_bar, inc=("CC", 1))
        ag_n += 2
        prev_step = pg.snap()

    ag_attn = ag_n
    attn_snap = pg.snap()

    def _finish_early():
        sl_ = dma("sync", "LY", out_c[0][0:128, :], REPF[:], waits=pg.wsnap(pg.snap()))
        pg.emit("sync", lambda e: e.nop(), waits=[sl_])
        pg.realize()
        return nc

    if stop is not None and stop != "attn" and (
            stop in ("p0", "attn0", "a0", "b0", "c0", "d0", "ef0")
            or stop.startswith("dh") or stop.startswith("dm") or stop in ("dep", "dskip", "dld", "dsa", "dva")):
        return _finish_early()
    if "xf" in dbg:
        dma("sync", "LY", dbg["xf"], xf32_s.rearrange("b s d -> (b s) d"), waits=pg.wsnap(attn_snap))
    if "logits" in dbg:
        dma("sync", "LY", dbg["logits"].rearrange("b p x -> (b p) x"), lg_s.rearrange("b p x -> (b p) x"), waits=pg.wsnap(attn_snap))
    if stop == "attn":
        return _finish_early()

    L = C[:, 0:8192].bitcast(F32).rearrange("p (t e) -> p t e", e=8)
    ind1 = C[:, 8192:12288].rearrange("p (t e) -> p t e", e=8)
    ind2 = C[:, 12288:16384].rearrange("p (t e) -> p t e", e=8)
    lL_toks = []
    for bb in range(NSTEP):
        for rr in range(NCORE):
            lL_toks.append(dma("sync", "LY", L[:, (bb * 8 + rr) * 16:(bb * 8 + rr + 1) * 16, :],
                     lgAG[(bb * 8 + rr) * 128:(bb * 8 + rr + 1) * 128, :].rearrange(
                         "p (tw e) -> p tw e", e=8),
                     waits=([("CC", max(ag_attn - 1, 0))] + pg.wsnap(attn_snap, ("CC",))) if (bb == 0 and rr == 0) else ()))
    Dx = Dg[:].bitcast(F32)
    m1, m2 = Dx[:, 0:512], Dx[:, 512:1024]
    i1f, i2f = Dx[:, 1024:1536], Dx[:, 1536:2048]
    wt1, wt2 = Dx[:, 2048:2560], Dx[:, 2560:3072]
    scr = Dx[:, 3072:7168]
    scr2v = scr.rearrange("p (t e) -> p t e", e=8)
    se = Dx[:, 7168:7680]
    bc = lambda ap: ap.rearrange("p (t a) -> p t a", a=1).broadcast_to([128, 512, 8])
    i8bc = SM_i8.rearrange("p (a e) -> p a e", a=1).broadcast_to([128, 512, 8])
    rii8bc = SM_rii8.rearrange("p (a e) -> p a e", a=1).broadcast_to([128, 512, 8])
    ohbc = SM_oh.rearrange("p (a e) -> p a e", a=1).broadcast_to([128, 512, 8])
    vop(lambda e: e.reduce_max(m1, L, axis=AX.X), waits=lL_toks + pg.wsnap(attn_snap, ("CC",)))
    vop(lambda e: e.tensor_tensor(scr2v, L, bc(m1), op=ALU.is_equal))
    vop(lambda e: e.tensor_mul(scr2v, scr2v, rii8bc))
    vop(lambda e: e.reduce_max(i1f, scr2v, axis=AX.X))
    vop(lambda e: e.tensor_tensor(scr2v, L, bc(m1), op=ALU.is_equal))
    vop(lambda e: e.tensor_scalar(scr2v, scr2v, 1e30, None, op0=ALU.mult))
    vop(lambda e: e.tensor_tensor(scr2v, L, scr2v, op=ALU.subtract))
    vop(lambda e: e.reduce_max(m2, scr2v, axis=AX.X))
    vop(lambda e: e.tensor_tensor(scr2v, scr2v, bc(m2), op=ALU.is_equal))
    vop(lambda e: e.tensor_mul(scr2v, scr2v, rii8bc))
    vop(lambda e: e.reduce_max(i2f, scr2v, axis=AX.X))
    vop(lambda e: e.tensor_scalar(i1f, i1f, -1.0, 99.0, op0=ALU.mult, op1=ALU.add))
    vop(lambda e: e.tensor_scalar(i2f, i2f, -1.0, 99.0, op0=ALU.mult, op1=ALU.add))
    v14 = vop(lambda e: e.tensor_tensor(scr2v, L, bc(m1), op=ALU.subtract))
    s1 = sop(lambda e: e.activation(scr, scr, AF.Exp), waits=[("VV", v14)])
    vop(lambda e: e.reduce_sum(se, scr2v, axis=AX.X), waits=[("SS", s1)])
    vop(lambda e: e.reciprocal(wt1, se))
    v17 = vop(lambda e: e.tensor_tensor(wt2, m2, m1, op=ALU.subtract))
    s2 = sop(lambda e: e.activation(wt2, wt2, AF.Exp), waits=[("VV", v17)])
    vop(lambda e: e.tensor_mul(wt2, wt2, wt1), waits=[("SS", s2)])
    vop(lambda e: e.tensor_tensor(ind1, bc(i1f), i8bc, op=ALU.is_equal))
    ind_done = vop(lambda e: e.tensor_tensor(ind2, bc(i2f), i8bc, op=ALU.is_equal))

    ranks = A[:].bitcast(F32).rearrange("p (t e) -> p t e", e=16)
    rk_ev = 0
    cs_ev = 0
    cs_toks = []
    for Gi in range(4):
        mmax = 0
        for trel in range(128):
            tile = Gi * 128 + trel
            bank = trel // 32
            col = (trel % 32) * 16
            for kk, indv in enumerate((ind1, ind2)):
                w = ([("VV", ind_done), ("VV", rk_ev)] + pg.wsnap(attn_snap, ("CC",))) if (trel == 0 and kk == 0) else ()
                mmax = mm(P[bank][:, col + kk * 8: col + kk * 8 + 8], SMB_su,
                          indv[:, tile, :], True, True, waits=w)
        for q4 in range(4):
            rk_ev = vop(lambda e, Gi=Gi, q4=q4: e.tensor_copy(
                ranks[:, Gi * 128 + q4 * 32:Gi * 128 + (q4 + 1) * 32, :],
                P[q4][:, 0:512].rearrange("p (t e) -> p t e", e=16)),
                waits=[("MM", mmax)])
        for h16 in range(4):
            mc = 0
            for tr2 in range(32):
                tile = Gi * 128 + h16 * 32 + tr2
                for kk, indv in enumerate((ind1, ind2)):
                    w = [("VV", cs_ev)] if (tr2 == 0 and kk == 0) else ()
                    mc = mm(P[4][0:1, tr2 * 16 + kk * 8: tr2 * 16 + kk * 8 + 8],
                            SMB_onesc, indv[:, tile, :], True, True, waits=w)
            cs_ev = vop(lambda e, h16=h16: e.tensor_copy(WK[0:1, h16 * 512:(h16 + 1) * 512],
                                                         P[4][0:1, 0:512]),
                        waits=[("MM", mc), (cs_toks[-1] if cs_toks else ("MM", 0))])
        cs_toks.append(dma("sync", "LY", cstage[Gi * 2048:(Gi + 1) * 2048].rearrange("(a b) -> a b", a=1),
                           WK[0:1, 0:2048], waits=[("VV", cs_ev)]))
    cs2 = REPB[:, 0:64]
    lc2 = dma("gpsimd", "LG", cs2.rearrange("p (t1 ke) -> p t1 ke", ke=16),
              cstage.rearrange("(t1 p ke) -> p t1 ke", p=128, ke=16),
              waits=cs_toks + pg.wsnap(attn_snap, ("CC",)))
    mb = mm(P[5][:, 0:64], SMB_su, cs2, True, False, waits=[lc2])
    mtb = mm(P[4][0:1, 0:64], SMB_onesc, cs2, True, True)
    tb, bp = WK2[0:1, 64:128], WK2[0:1, 128:192]
    vtb = vop(lambda e: e.tensor_copy(tb, P[4][0:1, 0:64]), waits=[("MM", mtb)])
    vop(lambda e: e.memset(bp[:, 0:16], 0.0))
    vop(lambda e: e.tensor_copy(bp[:, 16:32], tb[:, 0:16]), waits=[("VV", vtb)])
    vop(lambda e: e.tensor_add(bp[:, 32:48], bp[:, 16:32], tb[:, 16:32]))
    vb3 = vop(lambda e: e.tensor_add(bp[:, 48:64], bp[:, 32:48], tb[:, 32:48]))
    mb2 = mm(P[5][:, 0:64], ONESF[:], bp, False, True, waits=[("VV", vb3)])
    vbase = vop(lambda e: e.tensor_copy(WK2[:, 192:256], P[5][:, 0:64]), waits=[("MM", mb2)])
    lbs = dma("sync", "LY", bstage.rearrange("(t1 p ke) -> p t1 ke", p=128, ke=16),
              WK2[:, 192:256].rearrange("p (t1 ke) -> p t1 ke", ke=16), waits=[("VV", vbase)])
    radd_ev = 0
    prev_lbr2 = ("MM", 0)
    for Gi in range(4):
        lbr2 = dma("sync", "LY", WK2[0:1, 0:2048],
                   bstage[Gi * 2048:(Gi + 1) * 2048].rearrange("(a b) -> a b", a=1),
                   waits=[lbs, prev_lbr2, ("MM", pg.cnt["MM"])])
        prev_lbr2 = lbr2
        for q4 in range(4):
            mq = mm(P[q4][:, 0:512], ONESF[:], WK2[0:1, q4 * 512:(q4 + 1) * 512],
                    True, True, waits=[lbr2, ("VV", rk_ev), ("VV", radd_ev)])
            radd_ev = vop(lambda e, Gi=Gi, q4=q4: e.tensor_add(
                A[:].bitcast(F32)[:, Gi * 2048 + q4 * 512:Gi * 2048 + (q4 + 1) * 512],
                A[:].bitcast(F32)[:, Gi * 2048 + q4 * 512:Gi * 2048 + (q4 + 1) * 512],
                P[q4][:, 0:512]), waits=[("MM", mq)])
    ranks_done = radd_ev

    kept_k, vv_, pp_, jj_ = WK2[:, 0:512], WK2[:, 512:1024], WK2[:, 1024:1536], WK2[:, 1536:2048]
    yr, wfin = se, m2
    stash = WK[:, 0:2048].rearrange("p (t f) -> p t f", f=4)
    yrw_ev = 0
    for kk, (indv, iv, wv_) in enumerate(((ind1, i1f, wt1), (ind2, i2f, wt2))):
        rk = Dx[:, 7680:8192]
        vop(lambda e, indv=indv, kk=kk: e.tensor_tensor(
            scr2v, ranks[:, :, kk * 8:(kk + 1) * 8], indv, op=ALU.mult),
            waits=[("VV", ranks_done), ("VV", yrw_ev)])
        vop(lambda e, rk=rk: e.reduce_sum(rk, scr2v, axis=AX.X))
        vop(lambda e, rk=rk: e.tensor_scalar(kept_k, rk, 8192.0, None, op0=ALU.is_lt))
        vop(lambda e, rk=rk, kk=kk: e.tensor_scalar(vv_, rk, float(kk * CAP), None, op0=ALU.add))
        vi32 = scr[:, 0:512].bitcast(I32)
        pi32 = scr[:, 512:1024].bitcast(I32)
        vop(lambda e, vi32=vi32: e.tensor_copy(vi32, vv_))
        vop(lambda e, vi32=vi32, pi32=pi32: e.tensor_scalar(pi32, vi32, 511, None, op0=ALU.bitwise_and))
        vop(lambda e, pi32=pi32: e.tensor_copy(pp_, pi32))
        vop(lambda e, vi32=vi32, pi32=pi32: e.tensor_scalar(pi32, vi32, 9, None, op0=ALU.arith_shift_right))
        vop(lambda e, pi32=pi32: e.tensor_copy(jj_, pi32))
        vop(lambda e: e.tensor_scalar(yr, jj_, 4096.0, None, op0=ALU.mult))
        vop(lambda e, iv=iv: e.scalar_tensor_tensor(yr, iv, 512.0, yr, op0=ALU.mult, op1=ALU.add))
        vop(lambda e: e.tensor_add(yr, yr, pp_))
        vop(lambda e: e.tensor_mul(yr, yr, kept_k))
        vop(lambda e, wv_=wv_: e.tensor_mul(wfin, wv_, kept_k))
        vop(lambda e, kk=kk: e.tensor_copy(stash[:, 0:512, kk:kk + 1],
                                           yr.rearrange("p (t a) -> p t a", a=1)))
        yrw_ev = vop(lambda e, kk=kk: e.tensor_copy(stash[:, 0:512, 2 + kk:3 + kk],
                                                    wfin.rearrange("p (t a) -> p t a", a=1)))
    lyrw = dma("sync", "LY", yrw_s.rearrange("(p t) f -> p t f", p=128), stash,
               waits=[("VV", yrw_ev)])

    i128bc4 = SM_i128.rearrange("p (a r) -> p a r", a=1).broadcast_to([128, 4, 128])
    PTb = E[:, 0:512].rearrange("p (t r) -> p t r", r=128)
    CLO, CHI = E[:, 1024:1536], E[:, 1536:2048]
    CT = WK2[:].bitcast(BF16)
    prev_kk = ("MM", 0)
    sent_v = vop(lambda e: e.memset(SMc(15)[0:1, :], 999999.0), waits=[("VV", yrw_ev)])
    col_toks = []
    for kk, indv in enumerate((ind1, ind2)):
        roe, sel = Dx[:, 7680:8192], Dx[:, 0:512]
        va = vop(lambda e, kk=kk: e.tensor_tensor(
            scr2v, ranks[:, :, kk * 8:(kk + 1) * 8], ohbc, op=ALU.mult),
            waits=[("VV", yrw_ev), lyrw, prev_kk])
        vop(lambda e, roe=roe: e.reduce_sum(roe, scr2v, axis=AX.X))
        vop(lambda e, indv=indv: e.tensor_tensor(scr2v, indv, ohbc, op=ALU.mult))
        selv = vop(lambda e, sel=sel: e.reduce_sum(sel, scr2v, axis=AX.X))
        lrow = dma("sync", "LY", rowstage[kk * 1024:kk * 1024 + 512].rearrange("(a b) -> a b", a=1),
                   roe[0:1, :], waits=[("VV", selv)])
        lsen = dma("sync", "LY", rowstage[kk * 1024 + 512:kk * 1024 + 513].rearrange("(a b) -> a b", a=1),
                   SMc(15)[0:1, :], waits=[("VV", sent_v)])
        for j in range(4):
            col_toks.append(dma("sync", "LY", SM_base[kk][:, j:j + 1],
                                rowstage[kk * 1024 + j * 128:kk * 1024 + (j + 1) * 128].rearrange("(p a) -> p a", a=1),
                                waits=[lrow]))
            col_toks.append(dma("sync", "LY", SM_basep1[kk][:, j:j + 1],
                                rowstage[kk * 1024 + j * 128 + 1:kk * 1024 + (j + 1) * 128 + 1].rearrange("(p a) -> p a", a=1),
                                waits=[lrow, lsen]))
        vop(lambda e, kk=kk: e.tensor_sub(SM_cnt[kk], SM_basep1[kk], SM_base[kk]),
            waits=col_toks[-8:])
        rof = Dx[:, 512:1024]
        mK = mm(P[6][:, 0:512], ONESF[:], roe[0:1, 0:512], True, True, waits=[("VV", selv)])
        vf = vop(lambda e, roe=roe, rof=rof: e.tensor_tensor(rof, roe, P[6][:, 0:512], op=ALU.subtract),
                 waits=[("MM", mK)])
        rofb, selb = C[:, 0:512], C[:, 512:1024]
        vop(lambda e, rof=rof: e.tensor_copy(rofb, rof), waits=[("VV", vf)])
        vh2 = vop(lambda e, sel=sel: e.tensor_copy(selb, sel))
        pt_ev = 0
        clo_ev = 0
        for g4 in range(128):
            rofs = rofb.rearrange("p (t a) -> p t a", a=1)[:, g4 * 4:(g4 + 1) * 4, :].broadcast_to([128, 4, 128])
            sels = selb.rearrange("p (t a) -> p t a", a=1)[:, g4 * 4:(g4 + 1) * 4, :].broadcast_to([128, 4, 128])
            vop(lambda e, rofs=rofs: e.tensor_tensor(PTb, i128bc4, rofs, op=ALU.is_equal),
                waits=[("VV", vh2), ("MM", pg.cnt["MM"])])
            pt_ev = vop(lambda e, sels=sels: e.tensor_tensor(PTb, PTb, sels, op=ALU.mult))
            mc2 = 0
            for t4 in range(4):
                tile = g4 * 4 + t4
                col = (tile % 32) * 4
                mc2 = mm(P[7][:, col:col + 2], PTb[:, t4, :], IDC[:, tile * 2:tile * 2 + 2],
                         True, True, waits=[("VV", pt_ev), ("VV", clo_ev if (tile % 32) == 0 else 0)])
            if g4 % 8 == 7:
                f0 = (g4 // 8) * 32
                pv = P[7][:, 0:128].rearrange("p (t f) -> p t f", f=4)
                clo_ev = vop(lambda e, f0=f0, pv=pv: e.tensor_copy(
                    CLO[:, f0:f0 + 32].rearrange("p (t a) -> p t a", a=1),
                    pv[:, 0:32, 0:1]), waits=[("MM", mc2)])
                clo_ev = vop(lambda e, f0=f0, pv=pv: e.tensor_copy(
                    CHI[:, f0:f0 + 32].rearrange("p (t a) -> p t a", a=1),
                    pv[:, 0:32, 1:2]))
        ct_prev = 0
        for g2 in range(4):
            for plane, srcb in enumerate((CLO, CHI)):
                mt = tr(P[6][:, 0:64].bitcast(BF16), srcb[:, g2 * 128:(g2 + 1) * 128], SMB_id,
                        waits=[("VV", clo_ev), ("VV", ct_prev)])
                ct_prev = vop(lambda e, g2=g2, plane=plane: e.tensor_copy(
                    CT[:, (plane * 4 + g2) * 128:(plane * 4 + g2 + 1) * 128],
                    P[6][:, 0:64].bitcast(BF16)), waits=[("MM", mt)])
        ctc = WK[:, 0:512].rearrange("p (g r) -> p g r", r=128)
        ctlo = CT[:, 0:512].rearrange("p (g r) -> p g r", r=128)
        cthi = CT[:, 512:1024].rearrange("p (g r) -> p g r", r=128)
        vcm = vop(lambda e: e.scalar_tensor_tensor(ctc, cthi, 256.0, ctlo, op0=ALU.mult, op1=ALU.add),
                  waits=[("VV", ct_prev), lyrw])
        cti = WK[:, 512:1024].bitcast(I32).rearrange("p (g r) -> p g r", r=128)
        vci = vop(lambda e: e.tensor_copy(cti, ctc), waits=[("VV", vcm)])
        prev_kk = dma("sync", "LY", ctflat[kk].rearrange("(g p r) a -> p g (r a)", p=128, r=128),
                      cti, waits=[("VV", vci)])
        col_toks.append(prev_kk)
    idx_done = pg.snap()
    if stop == "index":
        return _finish_early()

    w1lo = Bg[:].rearrange("p (dt f) -> p dt f", dt=4)
    w1hi = E[:].rearrange("p (dt f) -> p dt f", dt=4)
    w2sb = C[:].rearrange("p (ft d) -> p ft d", ft=16)
    w2sb2 = A[:].rearrange("p (ft d) -> p ft d", ft=16)
    lw1a = dma("gpsimd", "LG", w1lo, w1c[0:512, :].rearrange("(dt p) f -> p dt f", p=128),
               waits=pg.wsnap(idx_done, ("CC",)))
    lw1b = dma("gpsimd", "LG", w1hi, w1c[512:1024, :].rearrange("(dt p) f -> p dt f", p=128),
               waits=pg.wsnap(idx_done, ("CC",)))
    lw2a = dma("gpsimd", "LG", w2sb, w2c[0:2048, :].rearrange("(ft p) d -> p ft d", p=128),
               waits=pg.wsnap(idx_done, ("CC",)))
    lw2b = dma("gpsimd", "LG", w2sb2, w2c[2048:4096, :].rearrange("(ft p) d -> p ft d", p=128),
               waits=pg.wsnap(idx_done, ("CC",)))

    def w1slice(dt, f0, f1):
        return (w1lo if dt < 4 else w1hi)[:, dt % 4, f0:f1]

    def w2slice(ft, d0, d1):
        return (w2sb if ft < 16 else w2sb2)[:, ft % 16, d0:d1]

    hT_sb = Dg[:].rearrange("p (ft s) -> p ft s", ft=32)
    xeT = [Fg[:, i * 4096:(i + 1) * 4096].rearrange("p (dt s) -> p dt s", dt=8) for i in range(2)]
    gb = W2b[:, 0:2048]
    NCH = 32
    gat_l = [ZT, ZT, ZT, ZT]
    yag_toks = []
    xe_mm = [0, 0]
    xet_marks = {}
    FS = dict(ctm=0, cti=0, h_ev=0, y_ev=0, hT_mm=0)

    def ffn_gather(ch):
        kk = ch // 16
        j0 = (ch % 16) * 4
        pp = ch % 2
        cst_toks = []
        for q in range(4):
            jslot = j0 + q
            icol, icol2 = q * 2, q * 2 + 1
            tj = WK[:, 0:128]
            vtj = vop(lambda e, jslot=jslot, tj=tj: e.tensor_scalar(
                tj, SM_i128, float(jslot * 128), None, op0=ALU.add),
                waits=(col_toks if (ch == 0 and q == 0) else [("MM", FS['ctm'])]))
            tjb = tj.rearrange("p (a s) -> p a s", a=1).broadcast_to([128, 4, 128])
            bbc = SM_base[kk].rearrange("p (t a) -> p t a", a=1).broadcast_to([128, 4, 128])
            bp1c = SM_basep1[kk].rearrange("p (t a) -> p t a", a=1).broadcast_to([128, 4, 128])
            cntc = SM_cnt[kk].rearrange("p (t a) -> p t a", a=1).broadcast_to([128, 4, 128])
            B4 = W2b[:, 2048:2560].rearrange("p (t s) -> p t s", s=128)
            C4 = W2b[:, 2560:3072].rearrange("p (t s) -> p t s", s=128)
            vb4 = vop(lambda e, tjb=tjb, bbc=bbc, B4=B4: e.tensor_tensor(B4, tjb, bbc, op=ALU.is_ge))
            vc4 = vop(lambda e, tjb=tjb, bp1c=bp1c, C4=C4: e.tensor_tensor(C4, tjb, bp1c, op=ALU.is_ge))
            vc5 = vop(lambda e, C4=C4, cntc=cntc: e.tensor_tensor(C4, C4, cntc, op=ALU.mult))
            for t1 in range(4):
                mB = mm(P[4][:, 0:1], B4[:, t1, :], SMB_onesc, t1 == 0, t1 == 3,
                        waits=([("VV", vc5), ("VV", FS['cti'])] if t1 == 0 else ()))
                FS['ctm'] = mm(P[5][:, 0:1], C4[:, t1, :], SMB_onesc, t1 == 0, t1 == 3)
            cidx = WK[:, 128:129]
            vop(lambda e, cidx=cidx: e.tensor_scalar(cidx, P[4][:, 0:1], 128.0, -128.0,
                                                     op0=ALU.mult, op1=ALU.add),
                waits=[("MM", FS['ctm'])])
            vop(lambda e, cidx=cidx: e.scalar_tensor_tensor(
                cidx, SM_iotac, 1.0, cidx, op0=ALU.mult, op1=ALU.add))
            vop(lambda e, cidx=cidx, jslot=jslot: e.tensor_scalar(
                cidx, cidx, float(jslot * 128), None, op0=ALU.add))
            vop(lambda e, cidx=cidx: e.tensor_tensor(cidx, cidx, P[5][:, 0:1], op=ALU.subtract))
            vop(lambda e, cidx=cidx: e.tensor_scalar(cidx, cidx, 0.0, 65535.0,
                                                     op0=ALU.max, op1=ALU.min))
            FS['cti'] = vop(lambda e, cidx=cidx, icol=icol: e.tensor_copy(ISL[:, icol:icol + 1], cidx),
                            waits=[gat_l[2 + q % 2]])
            gq1 = gop(lambda e, icol=icol, icol2=icol2, kk=kk: e.indirect_dma_start(
                out=ISL[:, icol2:icol2 + 1], out_offset=None, in_=ctflat[kk],
                in_offset=IndirectOffsetOnAxis(ISL[:, icol:icol + 1], 0)),
                waits=[("VV", FS['cti'])])
            gq = gop(lambda e, icol2=icol2, q=q: e.indirect_dma_start(
                out=gb[:, (q % 2) * 1024:(q % 2) * 1024 + 1024],
                out_offset=None, in_=xfAG,
                in_offset=IndirectOffsetOnAxis(ISL[:, icol2:icol2 + 1], 0)),
                waits=[gq1, ("CC", ag_attn), gat_l[q % 2]])
            gat_l[2 + q % 2] = gq
            gat_l[q % 2] = dma("sync", "LY", comp_s[ch * 512 + q * 128:ch * 512 + (q + 1) * 128, :],
                               gb[:, (q % 2) * 1024:(q % 2) * 1024 + 1024], waits=[gq])
            cst_toks.append(gat_l[q % 2])
        xm = []
        for dt in range(8):
            xm.append(dma("scalar", "LS", xeT[pp][:, dt, :],
                          comp_s[ch * 512:(ch + 1) * 512, dt * 128:(dt + 1) * 128],
                          waits=cst_toks + [("MM", xe_mm[pp])],
                          transpose=True))
        xet_marks[ch] = xm

    def ffn_compute(ch):
        nonlocal ag_n
        pp = ch % 2
        for fs in range(32):
            psp = P[fs % 2]
            for dt in range(8):
                w = (xet_marks[ch] + [lw1a, lw1b, ("SS", FS['h_ev'])]) if dt == 0 else ()
                m = mm(psp[:, 0:512], w1slice(dt, fs * 128, (fs + 1) * 128),
                       xeT[pp][:, dt, :], dt == 0, dt == 7, waits=w)
            xe_mm[pp] = m
            FS['h_ev'] = sop(lambda e, fs=fs, psp=psp: e.activation(
                hT_sb[:, fs, :], psp[:, 0:512], AF.Gelu_apprx_tanh),
                waits=[("MM", m), ("MM", FS['hT_mm'] if fs < 2 else 0)])
        yst_l = [ZT, ZT]
        for st in range(4):
            yp = st % 2
            for dh in range(2):
                psy = P[2 + (st * 2 + dh) % 2]
                for fs in range(32):
                    w = ([("SS", FS['h_ev']), lw2a, lw2b, ("VV", FS['y_ev']),
                          yst_l[yp]] if fs == 0 else ())
                    m = mm(psy[:, 0:512], hT_sb[:, fs, st * 128:(st + 1) * 128],
                           w2slice(fs, dh * 512, (dh + 1) * 512), fs == 0, fs == 31, waits=w)
                FS['hT_mm'] = m
                FS['y_ev'] = vop(lambda e, dh=dh, psy=psy, yp=yp: e.tensor_copy(
                    REPB[:, yp * 1024 + dh * 512: yp * 1024 + (dh + 1) * 512],
                    psy[:, 0:512]), waits=[("MM", m)])
            yst_l[yp] = dma("sync", "LY",
                            ycmp_s[ch * 512 + st * 128:ch * 512 + (st + 1) * 128, :],
                            REPB[:, yp * 1024:(yp + 1) * 1024], waits=[("VV", FS['y_ev'])])
            yag_toks.append(yst_l[yp])
        gop(lambda e, ch=ch: e.collective_compute(
            "AllGather", ALU.bypass, replica_groups=[list(range(NCORE))],
            ins=[ycmp_s[ch * 512:(ch + 1) * 512, :].opt()],
            outs=[yAG[ch * NCORE * 512:(ch + 1) * NCORE * 512, :].opt()]),
            waits=list(yag_toks), inc=("CC", 1))
        ag_n += 1
        yag_toks.clear()

    IDCi = IDC[:].bitcast(I32)
    fpre = {}

    def pre_stage(jt):
        rw = WK2[:, 1536 + 4 * jt:1540 + 4 * jt]
        g1 = gop(lambda e, jt=jt, rw=rw: e.indirect_dma_start(
            out=rw, out_offset=None, in_=yrw_s,
            in_offset=IndirectOffsetOnAxis(OWN[:, jt:jt + 1], 0)),
            waits=[lyrw] + (pg.wsnap(idx_done, ("CC",)) if jt == 0 else []))
        r1v = vop(lambda e, rw=rw, jt=jt: e.tensor_copy(IDCi[:, 2 * jt:2 * jt + 1], rw[:, 0:1]),
                  waits=[g1] + (pg.wsnap(idx_done, ("CC",)) if jt == 0 else []))
        r2v = vop(lambda e, rw=rw, jt=jt: e.tensor_copy(IDCi[:, 2 * jt + 1:2 * jt + 2], rw[:, 1:2]))
        fpre[jt] = (r1v, r2v)

    ffn_gather(0)
    for ch in range(NCH):
        if ch + 1 < NCH:
            ffn_gather(ch + 1)
        ffn_compute(ch)
        pre_stage(2 * ch)
        pre_stage(2 * ch + 1)
    y_ev = FS['y_ev']
    if "ycmp" in dbg:
        dma("sync", "LY", dbg["ycmp"], ycmp_s, waits=pg.qbar("LY") + [("VV", y_ev)])
    ffn_snap = pg.snap()
    if stop == "ffn":
        return _finish_early()

    lfb = dma("sync", "LY", WK2[0:1, 0:1024], fbi.rearrange("(a d) -> a d", a=1),
              waits=pg.wsnap(ffn_snap, ("CC",)))
    ovf = vop(lambda e: e.memset(WK2[0:1, 1024:1152], 1.0), waits=pg.wsnap(ffn_snap, ("CC",)))
    repf_ev = 0
    for sl in range(2):
        m = mm(P[6][:, 0:512], WK2[0:1, 1024:1152], WK2[0:1, sl * 512:(sl + 1) * 512],
               True, True, waits=[lfb, ("VV", ovf), ("VV", repf_ev)])
        repf_ev = vop(lambda e, sl=sl: e.tensor_copy(REPF[:, sl * 512:(sl + 1) * 512], P[6][:, 0:512]),
                      waits=[("MM", m)])
    NP = 4
    SKEW = 2
    Af = A[:].bitcast(F32)
    Bf = Bg[:].bitcast(F32)
    ffw = [("MM", pg.cnt["MM"]), ("VV", pg.cnt["VV"]), ("SS", pg.cnt["SS"])]
    st_l = [ZT] * NP
    fmarks = {}

    def fin_gather(jt):
        b, tt = divmod(jt, NT)
        p = jt % NP
        first = [] if jt >= NP else ffw
        yb = Dg[:, p * 2048:(p + 1) * 2048]
        xb = Af[:, p * 1024:(p + 1) * 1024]
        g2 = gop(lambda e, yb=yb, jt=jt: e.indirect_dma_start(
            out=yb[:, 0:1024], out_offset=None, in_=yAG,
            in_offset=IndirectOffsetOnAxis(IDCi[:, 2 * jt:2 * jt + 1], 0)),
            waits=[("VV", fpre[jt][0]), ("CC", ag_n), st_l[p]] + first)
        g3 = gop(lambda e, yb=yb, jt=jt: e.indirect_dma_start(
            out=yb[:, 1024:2048], out_offset=None, in_=yAG,
            in_offset=IndirectOffsetOnAxis(IDCi[:, 2 * jt + 1:2 * jt + 2], 0)),
            waits=[("VV", fpre[jt][1])])
        lx2 = dma("sync", "LY", xb, xf32_s[b][tt * 128:(tt + 1) * 128, :],
                  waits=[st_l[p]] + first)
        fmarks[jt] = (g2, g3, lx2)

    def fin_compute(jt):
        b, tt = divmod(jt, NT)
        p = jt % NP
        g2, g3, lx2 = fmarks.pop(jt)
        rw = WK2[:, 1536 + 4 * jt:1540 + 4 * jt]
        yb = Dg[:, p * 2048:(p + 1) * 2048]
        xb = Af[:, p * 1024:(p + 1) * 1024]
        acc = Bf[:, p * 1024:(p + 1) * 1024]
        a1 = vop(lambda e, acc=acc, yb=yb, rw=rw: e.scalar_tensor_tensor(
            acc, yb[:, 0:1024], rw[:, 2:3], REPF[:], op0=ALU.mult, op1=ALU.add),
            waits=[g2, g3, ("VV", repf_ev)])
        a2 = vop(lambda e, acc=acc, yb=yb, rw=rw: e.scalar_tensor_tensor(
            acc, yb[:, 1024:2048], rw[:, 3:4], acc, op0=ALU.mult, op1=ALU.add))
        a3 = vop(lambda e, acc=acc, xb=xb: e.tensor_add(acc, acc, xb), waits=[lx2])
        nm, var, rstd, sqs = SMc(p * 4), SMc(p * 4 + 1), SMc(p * 4 + 2), SMc(p * 4 + 3)
        vop(lambda e, nm=nm, acc=acc: e.reduce_sum(nm, acc, axis=AX.X, negate=True),
            waits=[("VV", a3)])
        vop(lambda e, nm=nm: e.tensor_scalar(nm, nm, 1.0 / 1024.0, None, op0=ALU.mult))
        v3 = vop(lambda e, nm=nm, acc=acc: e.tensor_scalar(acc, acc, nm, None, op0=ALU.add))
        s1 = vop(lambda e, sqs=sqs, acc=acc, xb=xb: e.scalar_tensor_tensor(
            xb, acc, 1.0, acc, op0=ALU.mult, op1=ALU.mult, accum_out=sqs),
            waits=[("VV", v3)])
        v4 = vop(lambda e, var=var, sqs=sqs: e.tensor_scalar(var, sqs, 1.0 / 1024.0, 1e-5,
                                                             op0=ALU.mult, op1=ALU.add))
        s2 = sop(lambda e, rstd=rstd, var=var: e.activation(rstd, var, AF.Ln), waits=[("VV", v4)])
        s3 = sop(lambda e, rstd=rstd: e.activation(rstd, rstd, AF.Exp, scale=-0.5))
        v5 = vop(lambda e, rstd=rstd, acc=acc: e.tensor_scalar(acc, acc, rstd, None, op0=ALU.mult),
                 waits=[("SS", s3)])
        st_l[p] = dma("sync", "LY", out_c[b][tt * 128:(tt + 1) * 128, :], acc, waits=[("VV", v5)])

    NTILE = NSTEP * NT
    for jt in range(NTILE):
        fin_gather(jt)
        if jt >= SKEW:
            fin_compute(jt - SKEW)
    for jt in range(NTILE - SKEW, NTILE):
        fin_compute(jt)
    pg.emit("sync", lambda e: e.nop(), waits=list(st_l))

    pg.realize()
    return nc


_NC_CACHE = None


def kernel(_stop=None, _dbg=None, **inputs):
    global _NC_CACHE
    perm = np.concatenate([np.arange(0, 64, 2), np.arange(1, 64, 2)])
    PERM = np.concatenate([h * 64 + perm for h in range(H)])
    f32 = np.float32
    Wq = np.ascontiguousarray(inputs["L0_attn_Wq"][:, PERM], dtype=f32)
    bq = np.ascontiguousarray(np.asarray(inputs["L0_attn_bq"], dtype=f32)[PERM].reshape(8, 128).T)
    Wk = np.ascontiguousarray(inputs["L0_attn_Wk"][:, PERM], dtype=f32)
    bk = np.ascontiguousarray(np.asarray(inputs["L0_attn_bk"], dtype=f32)[PERM].reshape(8, 128).T)
    x = np.asarray(inputs["L0_attn_x"], dtype=f32)
    if _stop is None and _dbg is None:
        if _NC_CACHE is None:
            _NC_CACHE = build_program()
        nc = _NC_CACHE
    else:
        nc = build_program(stop=_stop, dbg_keys=_dbg or ())
    in_maps = []
    for c in range(NCORE):
        batches = [c + 8 * bb for bb in range(NSTEP)]
        oh = np.zeros((128, 8), f32)
        oh[:, c] = 1.0
        tglob = np.concatenate([(np.arange(S, dtype=np.int64) + bb * S) for bb in batches])
        ownpos = ((tglob % 128) * 512 + tglob // 128).astype(np.int32)
        own = np.ascontiguousarray(ownpos.reshape(64, 128).T)
        in_maps.append({
            "x4": np.ascontiguousarray(x[batches]),
            "wqp": Wq, "bqp": bq, "wkp": Wk, "bkp": bk,
            "wvi": np.ascontiguousarray(inputs["L0_attn_Wv"], dtype=f32),
            "bvi": np.ascontiguousarray(inputs["L0_attn_bv"], dtype=f32),
            "wgi": np.ascontiguousarray(inputs["L0_attn_Wg"], dtype=f32),
            "bgi": np.ascontiguousarray(inputs["L0_attn_bg"], dtype=f32),
            "woi": np.ascontiguousarray(inputs["L0_attn_Wo"], dtype=f32),
            "boi": np.ascontiguousarray(inputs["L0_attn_bo"], dtype=f32),
            "rwi": np.ascontiguousarray(inputs["L0_ffn_router_w"], dtype=f32),
            "w1c": np.ascontiguousarray(inputs["L0_ffn_w1"][c], dtype=f32),
            "w2c": np.ascontiguousarray(inputs["L0_ffn_w2"][c], dtype=f32),
            "fbi": np.ascontiguousarray(np.asarray(inputs["L0_ffn_bias"], dtype=f32).reshape(D)),
            "onehot": oh,
            "ownrows": own,
        })
    res = run_bass_kernel_spmd(nc, in_maps, core_ids=list(range(NCORE)))
    if _stop is not None or _dbg:
        return res
    out = np.empty((B, S, D), np.float32)
    for c in range(NCORE):
        oc = res.results[c]["out_c"]
        for bb in range(NSTEP):
            out[c + 8 * bb] = oc[bb]
    return out


# version 15
# speedup vs baseline: 1.0123x; 1.0123x over previous
import numpy as np
import ml_dtypes

import concourse.bass as bass
import concourse.mybir as mybir
from concourse.bass_utils import run_bass_kernel_spmd
from concourse.bass_core_rust import IndirectOffsetOnAxis

F32 = mybir.dt.float32
BF16 = mybir.dt.bfloat16
I32 = mybir.dt.int32
AF = mybir.ActivationFunctionType
ALU = mybir.AluOpType
AX = mybir.AxisListType
BF = ml_dtypes.bfloat16

B, S, D = 32, 2048, 1024
H = 16
NCORE, NSTEP = 8, 4
NT = S // 128
GT = 512
CAP = 8192
PAD = 128
PLANE = 2 * (CAP + PAD)
ZROW = NCORE * 2 * CAP
ENGS = ("tensor", "vector", "scalar", "gpsimd", "sync")
DEBUG_OUTS = ()


def _tables():
    t = {}
    ah = 1.0 / 10000.0 ** np.linspace(0.0, 1.0, 32, dtype=np.float64)
    idx = np.arange(S, dtype=np.float64)
    cos_h = np.cos(idx[:, None] * ah[None, :])
    sin_h = np.sin(idx[:, None] * ah[None, :])
    t["cos4"] = np.tile(cos_h.T.astype(np.float32), (4, 1)).astype(BF)
    sgn = np.repeat(np.array([-1.0, 1.0, -1.0, 1.0], np.float32), 32)
    t["sin4"] = (np.tile(sin_h.T.astype(np.float32), (4, 1)) * sgn[:, None]).astype(BF)
    decay = np.log(1.0 - 2.0 ** (-5.0 - np.arange(H, dtype=np.float64)))
    g_negt = np.exp(-decay[:, None] * idx[None, :])
    gk = np.empty((8, 128, S), np.float32)
    for j in range(8):
        gk[j, 0:64] = (64.0 ** -0.5) * g_negt[2 * j]
        gk[j, 64:128] = (64.0 ** -0.5) * g_negt[2 * j + 1]
    t["gkrep"] = gk.astype(BF)
    masksum = (1.0 - np.exp(decay[:, None] * (idx[None, :] + 1))) / (1.0 - np.exp(decay[:, None]))
    c_tab = (np.exp(decay[:, None] * idx[None, :]) / np.sqrt(masksum)).astype(np.float32)
    ca = np.empty((128, 256), np.float32)
    for h in range(H):
        for s0 in range(16):
            ca[:, h * 16 + s0] = c_tab[h, s0 * 128:(s0 + 1) * 128]
    t["c_all"] = ca
    tt, ss = np.meshgrid(np.arange(128), np.arange(128), indexing="ij")
    t["tri"] = (tt <= ss).astype(np.float32).astype(BF)
    t["su"] = (tt < ss).astype(np.float32).astype(BF)
    t["id128b"] = np.eye(128, dtype=np.float32).astype(BF)
    t["id8f"] = np.eye(8, dtype=np.float32)
    t["iota128r"] = np.broadcast_to(np.arange(128, dtype=np.float32), (128, 128)).copy()
    t["iota8r"] = np.broadcast_to(np.arange(8, dtype=np.float32), (128, 8)).copy()
    t["rii8r"] = (99.0 - t["iota8r"]).copy()
    t["ones1x"] = np.ones((1, 128), np.float32).astype(BF)
    t["iotacol"] = np.arange(128, dtype=np.float32).reshape(128, 1)
    t["onescol"] = np.ones((128, 1), np.float32).astype(BF)
    idc = np.empty((128, GT, 2), np.float32)
    p = np.arange(128)
    for tile in range(GT):
        idc[:, tile, 0] = (tile % 2) * 128 + p
        idc[:, tile, 1] = tile // 2
    t["idcols"] = idc.astype(BF)
    return t


QDEPTH = 4


class Prog:
    def __init__(self, nc):
        self.nc = nc
        qsems = tuple(f"{q}{i}" for q in ("LY", "LG", "LS") for i in range(QDEPTH))
        self.names = ("MM", "VV", "SS", "CC") + qsems
        self.sems = {n: nc.alloc_semaphore(n) for n in self.names}
        self.cnt = {n: 0 for n in self.names}
        self.hw = {e: {n: 0 for n in self.names} for e in ENGS}
        self.qidx = {"LY": 0, "LG": 0, "LS": 0}

    def emit(self, eng, fn, waits=(), inc=None, force_self=False):
        w = []
        for s, v in waits:
            if v <= 0 or self.hw[eng][s] >= v:
                continue
            self.hw[eng][s] = v
            w.append((s, v))
        self.ops_list(eng).append((fn, w, inc))
        if inc is not None:
            self.cnt[inc[0]] += inc[1]
        return self.cnt[inc[0]] if inc else None

    def ops_list(self, eng):
        if not hasattr(self, "_ops"):
            self._ops = {e: [] for e in ENGS}
        return self._ops[eng]

    def snap(self):
        return dict(self.cnt)

    def qbar(self, *qs):
        out = []
        for q in qs:
            for i in range(QDEPTH):
                s = f"{q}{i}"
                out.append((s, self.cnt[s]))
        return out

    def wsnap(self, sn, skip=()):
        return [(s, v) for s, v in sn.items() if s in self.names and s not in skip]

    def realize(self):
        with self.nc.Block() as blk:
            for name in ENGS:
                def mk(nm):
                    def body(e):
                        for fn, w, inc in self._ops[nm]:
                            for s, v in w:
                                e.wait_ge(self.sems[s], v)
                            inst = fn(e)
                            if inc is not None:
                                inst.then_inc(self.sems[inc[0]], inc[1])
                    return body
                getattr(blk, name)(mk(name))


def build_program(stop=None, dbg_keys=None):
    global DEBUG_OUTS
    if dbg_keys is not None:
        DEBUG_OUTS = tuple(dbg_keys)
    T = _tables()
    nc = bass.Bass("TRN2", target_bir_lowering=False)

    def din(name, shape, dt=F32):
        return nc.dram_tensor(name, shape, dt, kind="ExternalInput").ap()
    x4 = din("x4", [NSTEP, S, D])
    wqp = din("wqp", [D, D]); bqp = din("bqp", [128, 8])
    wkp = din("wkp", [D, D]); bkp = din("bkp", [128, 8])
    wvi = din("wvi", [D, 2 * D]); bvi = din("bvi", [2 * D])
    wgi = din("wgi", [D, 2 * D]); bgi = din("bgi", [2 * D])
    woi = din("woi", [2 * D, D]); boi = din("boi", [D])
    rwi = din("rwi", [D, 8])
    w1c = din("w1c", [D, 4096]); w2c = din("w2c", [4096, D])
    fbi = din("fbi", [D])
    onehot = din("onehot", [128, 8])
    ownrows = din("ownrows", [128, 64], I32)
    out_c = nc.dram_tensor("out_c", [NSTEP, S, D], F32, kind="ExternalOutput").ap()
    dbg = {}
    for k in DEBUG_OUTS:
        shp = {"xf": [NSTEP * S, D], "logits": [NSTEP, 128, 128],
               "islot": [2, GT * 128], "ycmp": [2 * CAP, D]}[k]
        dt = {"xf": F32, "logits": F32, "islot": I32, "ycmp": BF16}[k]
        dbg[k] = nc.dram_tensor("dbg_" + k, shp, dt, kind="ExternalOutput").ap()

    def cinl(key):
        return nc.inline_tensor(T[key], name="c_" + key).ap()
    cos4_d, sin4_d, gk_d = cinl("cos4"), cinl("sin4"), cinl("gkrep")
    call_d, tri_d, su_d = cinl("c_all"), cinl("tri"), cinl("su")
    id128_d, id8_d = cinl("id128b"), cinl("id8f")
    i128r_d, i8r_d, rii8_d = cinl("iota128r"), cinl("iota8r"), cinl("rii8r")
    ones1_d, onesc_d, idc_d = cinl("ones1x"), cinl("onescol"), cinl("idcols")
    iotac_d = cinl("iotacol")

    def dsc(name, shape, dt):
        return nc.dram_tensor(name, shape, dt, kind="Internal").ap()
    xbf_s = dsc("xbf_s", [2, S, D], BF16)
    v_s = dsc("v_s", [S, 2 * D], BF16)
    g_s = dsc("g_s", [S, 2 * D], BF16)
    gT_s = dsc("gT_s", [2 * D, S], BF16)
    xf32_s = dsc("xf32_s", [NSTEP, S, D], F32)
    xfb_s = dsc("xfb_s", [NSTEP * S, D], BF16)
    lg_s = dsc("lg_s", [NSTEP, 128, 128], F32)
    xfAG = dsc("xfAG", [NCORE * NSTEP * S, D], BF16)
    lgAG = dsc("lgAG", [NSTEP * NCORE * 128, 128], F32)
    cstage = dsc("cstage", [GT * 16], F32)
    bstage = dsc("bstage", [GT * 16], F32)
    ctflat = [dsc("ctflat0", [GT * 128, 1], I32), dsc("ctflat1", [GT * 128, 1], I32)]
    rowstage = dsc("rowstage", [4096], F32)
    yrw_s = dsc("yrw_s", [NCORE * NSTEP * S, 4], F32)
    comp_s = dsc("comp_s", [2 * CAP, D], BF16)
    ycmp_s = dsc("ycmp_s", [2 * CAP, D], BF16)
    yAG = dsc("yAG", [ZROW, D], BF16)

    A = nc.alloc_sbuf_tensor("BIGA", [128, 16384], BF16)
    Bg = nc.alloc_sbuf_tensor("BIGB", [128, 16384], BF16)
    C = nc.alloc_sbuf_tensor("BIGC", [128, 16384], BF16)
    Dg = nc.alloc_sbuf_tensor("BIGD", [128, 16384], BF16)
    E = nc.alloc_sbuf_tensor("BIGE", [128, 16384], BF16)
    Fg = nc.alloc_sbuf_tensor("BIGF", [128, 8192], BF16, is_transpose_dest=True)
    WK = nc.alloc_sbuf_tensor("WK", [128, 2048], F32)
    WK2 = nc.alloc_sbuf_tensor("WK2", [128, 2048], F32, is_transpose_dest=True)
    REPB = nc.alloc_sbuf_tensor("REPB", [128, 2048], BF16)
    REPF = nc.alloc_sbuf_tensor("REPF", [128, 1024], F32)
    SM = nc.alloc_sbuf_tensor("SM", [128, 512], F32)
    SMB = nc.alloc_sbuf_tensor("SMB", [128, 640], BF16)
    IDC = nc.alloc_sbuf_tensor("IDC", [128, 1024], BF16)
    ISL = nc.alloc_sbuf_tensor("ISL", [128, 140], I32)
    OWN = nc.alloc_sbuf_tensor("OWN", [128, 64], I32)
    ONESF = nc.alloc_sbuf_tensor("ONESF", [1, 128], F32)
    P = [nc.alloc_psum_tensor(f"P{i}", [128, 512], F32) for i in range(8)]
    assert nc.sbuf_bytes_remaining >= 0

    SM_call = SM[:, 0:256]
    SM_i8 = SM[:, 256:264]
    SM_rii8 = SM[:, 264:272]
    SM_id8 = SM[:, 272:280]
    SM_bq = SM[:, 280:288]
    SM_bk = SM[:, 288:296]
    SM_oh = SM[:, 296:304]
    SM_i128 = SM[:, 304:432]
    SMc = lambda i: SM[:, 432 + i:433 + i]
    SM_iotac = SM[:, 448:449]
    SM_base = [SM[:, 452 + 12 * k:456 + 12 * k] for k in range(2)]
    SM_basep1 = [SM[:, 456 + 12 * k:460 + 12 * k] for k in range(2)]
    SM_cnt = [SM[:, 460 + 12 * k:464 + 12 * k] for k in range(2)]
    SMB_tri = SMB[:, 0:128]
    SMB_su = SMB[:, 128:256]
    SMB_id = SMB[:, 256:384]
    SMB_ones1 = SMB[0:1, 384:512]
    SMB_onesc = SMB[:, 512:513]
    SMB_rw = SMB[:, 520:584].rearrange("p (dt e) -> p dt e", e=8)

    pg = Prog(nc)
    ZT = ("MM", 0)

    def dma(eng, q, out, in_, waits=(), **kw):
        s = f"{q}{pg.qidx[q] % QDEPTH}"
        pg.qidx[q] += 1
        def f(e, o=out, i=in_, k=dict(kw)):
            return e.dma_start(out=o, in_=i, **k)
        pg.emit(eng, f, waits=list(waits) + [(s, pg.cnt[s])], inc=(s, 16))
        return (s, pg.cnt[s])

    def mm(out, lhsT, rhs, start, stop, waits=()):
        def f(e, o=out, l=lhsT, r=rhs, st=start, sp=stop):
            return e.matmul(o, l, r, start=st, stop=sp)
        return pg.emit("tensor", f, waits=waits, inc=("MM", 1))

    def tr(out, in_, ident, waits=()):
        def f(e, o=out, i=in_, d=ident):
            return e.transpose(o, i, d)
        return pg.emit("tensor", f, waits=waits, inc=("MM", 1))

    def vop(fn, waits=()):
        return pg.emit("vector", fn, waits=list(waits) + [("VV", pg.cnt["VV"])],
                       inc=("VV", 1), force_self=True)

    def sop(fn, waits=()):
        return pg.emit("scalar", fn, waits=list(waits) + [("SS", pg.cnt["SS"])],
                       inc=("SS", 1), force_self=True)

    def gop(fn, waits=(), inc=None):
        if inc is not None:
            pg.emit("gpsimd", fn, waits=waits, inc=inc)
            return ("CC", pg.cnt["CC"])
        s = f"LG{pg.qidx['LG'] % QDEPTH}"
        pg.qidx["LG"] += 1
        pg.emit("gpsimd", fn, waits=list(waits) + [(s, pg.cnt[s])], inc=(s, 16))
        return (s, pg.cnt[s])

    dma("sync", "LY", SM_call, call_d)
    dma("sync", "LY", SM_i8, i8r_d)
    dma("sync", "LY", SM_rii8, rii8_d)
    dma("sync", "LY", SM_id8[0:8, :], id8_d)
    dma("sync", "LY", SM_bq, bqp)
    dma("sync", "LY", SM_bk, bkp)
    dma("sync", "LY", SM_oh, onehot)
    dma("sync", "LY", SM_i128, i128r_d)
    dma("sync", "LY", SM_iotac, iotac_d)
    dma("sync", "LY", SMB_tri, tri_d)
    dma("sync", "LY", SMB_su, su_d)
    dma("sync", "LY", SMB_id, id128_d)
    dma("sync", "LY", SMB_ones1, ones1_d)
    dma("sync", "LY", SMB_onesc, onesc_d)
    dma("sync", "LY", IDC[:].rearrange("p (t two) -> p t two", two=2), idc_d)
    dma("sync", "LY", OWN[:], ownrows)
    lrw = dma("gpsimd", "LG", SMB_rw, rwi.rearrange("(dt p) e -> p dt e", p=128))
    zv = vop(lambda e: e.memset(WK[:], 0.0))
    vop(lambda e: e.memset(ONESF[:], 1.0))
    lbo = dma("sync", "LY", WK2[0:1, 0:1024], boi.rearrange("(a d) -> a d", a=1))
    ov1 = vop(lambda e: e.memset(WK2[0:1, 1024:1152], 1.0))
    rv0 = 0
    for sl in range(2):
        m = mm(P[6][:, 0:512], WK2[0:1, 1024:1152], WK2[0:1, sl * 512:(sl + 1) * 512],
               True, True, waits=[lbo, ("VV", ov1), ("VV", rv0)])
        rv0 = vop(lambda e, sl=sl: e.tensor_copy(REPF[:, sl * 512:(sl + 1) * 512], P[6][:, 0:512]),
                  waits=[("MM", m)])
    init_snap = pg.snap()

    xT = A[:].rearrange("p (dt s) -> p dt s", dt=8)
    qrT = Bg[:].rearrange("p (dt s) -> p dt s", dt=8)
    krT = C[:].rearrange("p (dt s) -> p dt s", dt=8)
    W2b = WK2[:].bitcast(BF16)
    qtmp = W2b[:, 0:2048]
    qsw = W2b[:, 2048:4096]
    F_gk = Fg[:, 0:2048]
    F_t2 = Fg[:, 2048:4096]
    F_cos = Fg[:, 4096:6144]
    F_sin = Fg[:, 6144:8192]
    wq_sb = E[:, 0:8192].rearrange("p (dt o) -> p dt o", dt=8)
    wk_sb = E[:, 8192:16384].rearrange("p (dt o) -> p dt o", dt=8)
    wvg_sb = E[:].rearrange("p (dt o) -> p dt o", dt=8)
    E_wo = E[:].rearrange("p (g d) -> p g d", g=16)
    D_xfb = Dg[:, 8192:9216]
    D_lgf = Dg[:, 9216:9472].bitcast(F32)
    D_xfT = [Bg[:, 12288 + i * 2048: 12288 + (i + 1) * 2048] for i in range(2)]

    prev_step = init_snap
    lcos = dma("sync", "LY", F_cos, cos4_d, waits=pg.wsnap(init_snap))
    lsin = dma("sync", "LY", F_sin, sin4_d)
    pre_x, pre_w = {}, {}
    xt_prev = []
    ov = 0
    tr_last = 0
    lg_marks = []
    ag_n = 0
    _sub = stop is not None and (stop in ('a0', 'b0', 'c0', 'd0', 'ef0', 'dskip') or stop.startswith('dh') or stop.startswith('dm') or stop in ('dep', 'dld', 'dsa', 'dva'))
    nstep_run = 0 if stop == 'p0' else (1 if _sub else NSTEP)
    for b in range(nstep_run):
        if b in pre_x:
            lgc, xt_l = pre_x.pop(b)
        else:
            lgc = dma("gpsimd", "LG", xbf_s[b % 2], x4[b], waits=pg.wsnap(prev_step, ("CC",)))
            xt_l = [dma("scalar", "LS", xT[:, j, :], xbf_s[b % 2][:, j * 128:(j + 1) * 128],
                        waits=[lgc], transpose=True) for j in range(8)]
        if b in pre_w:
            lq, lk = pre_w.pop(b)
        else:
            lq = dma("gpsimd", "LG", wq_sb, wqp.rearrange("(dt p) o -> p dt o", p=128),
                     waits=pg.wsnap(prev_step, ("CC",)))
            lk = dma("gpsimd", "LG", wk_sb, wkp.rearrange("(dt p) o -> p dt o", p=128))

        if stop == 'a0':
            continue
        pa_free_s = [0, 0]
        last_rot = 0
        for which in range(2):
            wsb, wld = (wq_sb, lq) if which == 0 else (wk_sb, lk)
            bcol = SM_bq if which == 0 else SM_bk
            dstT = qrT if which == 0 else krT
            for j in range(8):
                if which == 1:
                    lgk = dma("gpsimd", "LG", F_gk, gk_d[j],
                              waits=[("VV", last_rot)] +
                                    (pg.wsnap(prev_step, ("CC",)) if j == 0 else []))
                for sl in range(4):
                    ps = P[sl % 2]
                    for dt in range(8):
                        w = ([wld, ("VV", max(pa_free_s[sl % 2], last_rot))]
                             if dt == 0 else []) + [xt_l[dt]]
                        m = mm(ps[:, 0:512], wsb[:, dt, j * 128:(j + 1) * 128],
                               xT[:, dt, sl * 512:(sl + 1) * 512], dt == 0, dt == 7, waits=w)
                    pa_free_s[sl % 2] = vop(
                        lambda e, ps=ps, sl=sl, bc=bcol, j=j: e.tensor_scalar(
                            qtmp[:, sl * 512:(sl + 1) * 512], ps[:, 0:512],
                            bc[:, j:j + 1], None, op0=ALU.add),
                        waits=[("MM", m), ("VV", last_rot)])
                sse = max(pa_free_s)
                if which == 1:
                    gv = vop(lambda e: e.tensor_mul(qtmp[:], qtmp[:], F_gk[:]),
                             waits=[("VV", sse), lgk])
                else:
                    gv = 0
                w0 = [("VV", max(sse, gv)), lsin, lcos] + \
                     (pg.wsnap(prev_step, ("CC",)) if (which == 0 and j == 0) else [])
                vop(lambda e: e.tensor_copy(qsw[0:32, :], qtmp[32:64, :]), waits=w0)
                vop(lambda e: e.tensor_copy(qsw[32:64, :], qtmp[0:32, :]))
                vop(lambda e: e.tensor_copy(qsw[64:96, :], qtmp[96:128, :]))
                c4 = vop(lambda e: e.tensor_copy(qsw[96:128, :], qtmp[64:96, :]))
                vop(lambda e, dT=dstT, j=j: e.tensor_mul(dT[:, j, :], qtmp[:], F_cos[:]))
                t2 = vop(lambda e: e.tensor_mul(F_t2[:], qsw[:], F_sin[:]), waits=[("VV", c4)])
                last_rot = vop(lambda e, dT=dstT, j=j: e.tensor_add(dT[:, j, :], dT[:, j, :], F_t2[:]),
                               waits=[("VV", t2)])
        rot_done = pg.snap()

        if stop == 'b0':
            continue
        vg_last_store = ZT
        for which in range(2):
            wten, brow_d = (wvi, bvi) if which == 0 else (wgi, bgi)
            dst_d = v_s if which == 0 else g_s
            lw = dma("gpsimd", "LG", wvg_sb, wten.rearrange("(dt p) o -> p dt o", p=128),
                     waits=(pg.wsnap(rot_done) if which == 0 else
                            [("VV", pg.cnt["VV"]), ("MM", pg.cnt["MM"])]))
            lbr = dma("gpsimd", "LG", REPB[0:1, :], brow_d.rearrange("(a d) -> a d", a=1),
                      waits=pg.wsnap(rot_done))
            repv = 0
            for sl in range(4):
                m = mm(P[2][:, 0:512], SMB_ones1,
                       REPB[0:1, sl * 512:(sl + 1) * 512], True, True,
                       waits=[lbr, ("VV", repv)])
                repv = vop(lambda e, sl=sl: e.tensor_copy(REPB[:, sl * 512:(sl + 1) * 512], P[2][:, 0:512]),
                           waits=[("MM", m)])
            pv_free = [repv, repv]
            for tt in range(NT):
                ev = 0
                for dvs in range(4):
                    ps = P[dvs % 2]
                    for dt in range(8):
                        w = ([lw, ("VV", pv_free[dvs % 2])] if dt == 0 else []) + [xt_l[dt]]
                        m = mm(ps[:, 0:512], xT[:, dt, tt * 128:(tt + 1) * 128],
                               wvg_sb[:, dt, dvs * 512:(dvs + 1) * 512], dt == 0, dt == 7, waits=w)
                    ev = vop(lambda e, ps=ps, dvs=dvs: e.tensor_add(
                        qtmp[:, dvs * 512:(dvs + 1) * 512], ps[:, 0:512],
                        REPB[:, dvs * 512:(dvs + 1) * 512]),
                        waits=[("MM", m), vg_last_store])
                    pv_free[dvs % 2] = ev
                vg_last_store = dma("sync", "LY", dst_d[tt * 128:(tt + 1) * 128, :], qtmp[:],
                                    waits=[("VV", ev)])

        if stop == 'c0':
            continue
        vg_bar = pg.qbar("LY")
        lwo = dma("gpsimd", "LG", E_wo, woi.rearrange("(g p) d -> p g d", p=128),
                  waits=[("MM", pg.cnt["MM"])])
        if b + 1 < nstep_run:
            nb = b + 1
            plgc = dma("gpsimd", "LG", xbf_s[nb % 2], x4[nb], waits=list(xt_prev))
            pxt = [dma("scalar", "LS", xT[:, j, :], xbf_s[nb % 2][:, j * 128:(j + 1) * 128],
                       waits=[plgc, ("MM", pg.cnt["MM"])], transpose=True) for j in range(8)]
            pre_x[nb] = (plgc, pxt)
        xt_prev = xt_l
        D_vre4 = [Dg[:, hr * 2112:(hr + 1) * 2112].rearrange("p (tt c) -> p tt c", c=132)
                  for hr in range(4)]
        D_gh2 = [Dg[:, 8448 + i * 2048:8448 + (i + 1) * 2048].rearrange("p (q d) -> p q d", d=512)
                 for i in range(2)]
        D_owb = Dg[:, 12544:14592].rearrange("p (s0 d) -> p s0 d", d=128)
        D_st = [Dg[:, 14592 + i * 512:14592 + (i + 1) * 512] for i in range(3)]
        D_gacc = [Fg[:, i * 2048:(i + 1) * 2048] for i in range(2)]
        SG_all = W2b[:, 2048:4096]
        OrawR = WK[:, 384:900].rearrange("p (rs c) -> p rs c", c=129)
        R4 = WK[:, 96:100]
        gat_all = W2b[:, 0:2048]
        if b == 0:
            ov = vop(lambda e: e.memset(
                Dg[:, 0:8448].rearrange("p (x c) -> p x c", c=132)[:, :, 128:129], 1.0),
                waits=pg.wsnap(prev_step, ("CC",)))
        MM1P = (0, 1, 7)
        st_mm_free = [0, 0, 0]
        pa_s = [0, 0, 0]
        pa_v = [pg.cnt["VV"]] * 3
        slot_i = 0
        po_ep = [0, 0, 0, 0]
        vre_mm4 = [0, 0, 0, 0]
        gh_ss = [0, 0]
        gacc_fl = [[], []]
        pmt4 = [0, 0, 0, 0]
        vg2_4 = [0, 0, 0, 0]
        ghp = 0
        _dH = 0 if stop == 'dskip' else H
        if stop is not None and stop.startswith("dh"):
            _dH = int(stop[2:])
        tail_q = []
        for hq in range(0, H, 4):
            if hq >= _dH:
                break
            lvh = []
            for hr in range(4):
                h = hq + hr
                lvh.append(dma("sync", "LY", D_vre4[hr][:, :, 0:128],
                               v_s.rearrange("(tt p) o -> p tt o", p=128)[:, :, h * 128:(h + 1) * 128],
                               waits=(vg_bar if hq == 0 else []) + [("VV", ov),
                                     ("MM", pg.cnt["MM"] if hq == 0 else vre_mm4[hr])]))
            for g in range(4):
                s_lo = g * 4
                pp = ghp % 2
                lgh = dma("sync", "LY", D_gh2[pp][:, :, :],
                          g_s.rearrange("(tt p) o -> p tt o", p=128)[:, 4 * g:4 * g + 4,
                                                                    hq * 128:hq * 128 + 512],
                          waits=(vg_bar if ghp < 2 else []) + [("SS", gh_ss[pp])])
                ghp += 1

                def emit_mm2(ent, s_lo=s_lo):
                    sl3, t0, hr2, sv, vv = ent
                    lastm = 0
                    for s0 in range(max(s_lo, t0), s_lo + 4):
                        rs = s0 - s_lo
                        lastm = mm(P[2 + rs][:, 0:129],
                                   D_st[sl3][:, rs * 128:(rs + 1) * 128],
                                   D_vre4[hr2][:, t0, 0:129], t0 == 0, s0 == t0,
                                   waits=[("SS", sv),
                                          ("VV", max(vv, po_ep[rs] if t0 == 0 else 0)),
                                          lvh[hr2]])
                    st_mm_free[sl3] = lastm
                    vre_mm4[hr2] = max(vre_mm4[hr2], lastm)

                for hr in range(4):
                    h = hq + hr
                    if h >= _dH:
                        continue
                    j, half = divmod(h, 2)
                    pb = half * 64
                    pend = []
                    for t0 in range(s_lo + 4):
                        sl3 = slot_i % 3
                        slot_i += 1
                        ps = P[MM1P[sl3]]
                        m = mm(ps[:, 0:512], krT[pb:pb + 64, j, t0 * 128:(t0 + 1) * 128],
                               qrT[pb:pb + 64, j, g * 512:(g + 1) * 512], True, True,
                               waits=[("SS", pa_s[sl3]), ("VV", pa_v[sl3])])
                        rel = t0 - s_lo
                        sv = vv = 0
                        if rel < 0:
                            sv = sop(lambda e, ps=ps, sl=sl3: e.activation(
                                D_st[sl][:, 0:512], ps[:, 0:512], AF.Copy),
                                waits=[("MM", max(m, st_mm_free[sl3]))])
                        else:
                            c0 = rel * 128
                            vv = vop(lambda e, ps=ps, sl=sl3, c0=c0: e.tensor_mul(
                                D_st[sl][:, c0:c0 + 128], ps[:, c0:c0 + 128], SMB_tri),
                                waits=[("MM", max(m, st_mm_free[sl3]))])
                            if c0 + 128 < 512:
                                vv = vop(lambda e, ps=ps, sl=sl3, c0=c0: e.tensor_copy(
                                    D_st[sl][:, c0 + 128:512], ps[:, c0 + 128:512]))
                        pa_s[sl3] = max(pa_s[sl3], sv)
                        pa_v[sl3] = max(pa_v[sl3], vv)
                        pend.append((sl3, t0, hr, sv, vv))
                        if tail_q:
                            tail_q.pop(0)()
                        if len(pend) > 2:
                            emit_mm2(pend.pop(0))
                    while pend:
                        emit_mm2(pend.pop(0))
                    if tail_q:
                        tail_q.pop(0)()
                    mfin = pg.cnt["MM"]
                    for rs in range(4):
                        po_ep[rs] = vop(lambda e, rs=rs: e.tensor_copy(
                            OrawR[:, rs, :], P[2 + rs][:, 0:129]),
                            waits=[("MM", mfin)] if rs == 0 else [])
                    csl4 = SM_call[:, h * 16 + s_lo: h * 16 + s_lo + 4]
                    tc, tn, wc = WK[:, 104:108], WK[:, 108:112], WK[:, 100:104]
                    vop(lambda e: e.tensor_copy(
                        R4, OrawR[:, :, 128:129].rearrange("p a c -> p (a c)")))
                    vop(lambda e, csl4=csl4: e.tensor_tensor(tc, csl4, R4, op=ALU.mult))
                    vop(lambda e: e.tensor_scalar(tn, tc, -1.0, None, op0=ALU.mult))
                    vop(lambda e: e.tensor_tensor(tc, tc, tn, op=ALU.max))
                    vop(lambda e: e.tensor_scalar(tc, tc, 1.0, None, op0=ALU.max))
                    vop(lambda e: e.reciprocal(tc, tc))
                    vop(lambda e, csl4=csl4: e.tensor_mul(wc, csl4, tc))
                    for rs in range(4):
                        idx = hr * 4 + rs
                        mus = WK[:, idx:idx + 1]
                        sqs = WK[:, 16 + idx:17 + idx]
                        vop(lambda e, rs=rs, idx=idx, mus=mus: e.tensor_scalar(
                            D_owb[:, idx, :], OrawR[:, rs, 0:128], wc[:, rs:rs + 1], 0.0,
                            op0=ALU.mult, op1=ALU.add, accum_out=mus))
                        vop(lambda e, idx=idx, sqs=sqs: e.scalar_tensor_tensor(
                            WK[:, 128:256], D_owb[:, idx, :], 1.0, D_owb[:, idx, :],
                            op0=ALU.mult, op1=ALU.mult, accum_out=sqs))
                nm16, var16, rstd16, e16 = WK[:, 32:48], WK[:, 48:64], WK[:, 64:80], WK[:, 80:96]
                vop(lambda e: e.tensor_scalar(nm16, WK[:, 0:16], -1.0 / 128.0, None, op0=ALU.mult))
                vop(lambda e: e.tensor_tensor(e16, WK[:, 0:16], WK[:, 0:16], op=ALU.mult))
                vop(lambda e: e.tensor_scalar(e16, e16, -1.0 / 16384.0, None, op0=ALU.mult))
                vop(lambda e: e.scalar_tensor_tensor(
                    var16, WK[:, 16:32], 1.0 / 128.0, e16, op0=ALU.mult, op1=ALU.add))
                vv3 = vop(lambda e: e.tensor_scalar(var16, var16, 1e-5, None, op0=ALU.add))
                s2 = sop(lambda e: e.activation(rstd16, var16, AF.Ln), waits=[("VV", vv3)])
                s3 = sop(lambda e: e.activation(rstd16, rstd16, AF.Exp, scale=-0.5))
                gh_ss[pp] = s4 = sop(lambda e, pp=pp: e.activation(
                    SG_all, D_gh2[pp][:, :, :].rearrange("p q d -> p (q d)"), AF.Silu),
                    waits=[lgh, ("VV", pg.cnt["VV"])])
                pp2 = (ghp - 1) % 2
                first_p2 = True
                ucell = {"first": True}
                for hr in range(4):
                    h = hq + hr
                    if h >= _dH:
                        continue
                    for rs in range(4):
                        idx = hr * 4 + rs
                        ctr = WK[:, 256:384]
                        vop(lambda e, idx=idx: e.tensor_scalar(
                            ctr, D_owb[:, idx, :], nm16[:, idx:idx + 1], None, op0=ALU.add),
                            waits=[("SS", s3)] if first_p2 else [])
                        v13 = vop(lambda e, idx=idx, rs=rs, hr=hr: e.scalar_tensor_tensor(
                            gat_all[:, idx * 128:(idx + 1) * 128], ctr, rstd16[:, idx:idx + 1],
                            SG_all[:, rs * 512 + hr * 128: rs * 512 + (hr + 1) * 128],
                            op0=ALU.mult, op1=ALU.mult),
                            waits=[("SS", s4)] + ([("MM", tr_last)] if first_p2 else []))
                        first_p2 = False

                        def t_go(idx=idx, hr=hr, rs=rs, pp2=pp2, v13=v13, ucell=ucell):
                            nonlocal tr_last
                            sl4 = idx % 4
                            pt = P[6][:, sl4 * 64:(sl4 + 1) * 64].bitcast(BF16)
                            pmt4[sl4] = tr_last = tr(
                                pt, gat_all[:, idx * 128:(idx + 1) * 128], SMB_id,
                                waits=[("VV", v13), ("VV", max(vg2_4))])
                            wfl = []
                            if ucell["first"]:
                                ucell["first"] = False
                                wfl = gacc_fl[pp2]
                                gacc_fl[pp2] = []
                            vg2_4[sl4] = vop(lambda e, pt=pt, hr=hr, rs=rs, pp2=pp2: e.tensor_copy(
                                D_gacc[pp2][:, hr * 512 + rs * 128: hr * 512 + (rs + 1) * 128], pt),
                                waits=[("MM", pmt4[sl4])] + wfl)
                        tail_q.append(t_go)

                    def f_go(h=h, g=g, hr=hr, pp2=pp2):
                        gacc_fl[pp2].append(dma(
                            "sync", "LY", gT_s[h * 128:(h + 1) * 128, g * 512:(g + 1) * 512],
                            D_gacc[pp2][:, hr * 512:(hr + 1) * 512],
                            waits=[("VV", vg2_4[(hr * 4 + 3) % 4])]))
                    tail_q.append(f_go)

        while tail_q:
            tail_q.pop(0)()
        d_done = pg.snap()

        if stop == 'd0':
            continue
        D_gstat = [Dg[:, i * 2048:(i + 1) * 2048].rearrange("p (g s) -> p g s", s=128)
                   for i in range(2)]
        gst_mm = [0, 0]
        pc_v = 0
        xf_l2 = ZT
        prev_l1 = ZT
        prev_l2 = ZT
        for tt in range(NT):
            pp = tt % 2
            lg2 = dma("sync", "LY", D_gstat[pp],
                      gT_s[:, tt * 128:(tt + 1) * 128].rearrange("(g p) s -> p g s", p=128),
                      waits=[("MM", gst_mm[pp])] + pg.wsnap(d_done))
            lx = dma("sync", "LY", WK[:, 1024:2048], x4[b][tt * 128:(tt + 1) * 128, :])
            mlast = 0
            for dgt in range(16):
                for h2 in range(2):
                    w = ([lg2, lwo, ("VV", pc_v)] + pg.wsnap(d_done)) if dgt == 0 else ()
                    mlast = mm(P[4 + h2][:, 0:512], D_gstat[pp][:, dgt, :],
                               E_wo[:, dgt, h2 * 512:(h2 + 1) * 512], dgt == 0, dgt == 15,
                               waits=w)
            gst_mm[pp] = mlast
            rbuf = WK[:, 0:1024]
            vop(lambda e: e.tensor_add(rbuf[:, 0:512], P[4][:, 0:512], REPF[:, 0:512]),
                waits=[("MM", mlast), prev_l1] + pg.wsnap(d_done))
            pc_v = vop(lambda e: e.tensor_add(rbuf[:, 512:1024], P[5][:, 0:512], REPF[:, 512:1024]))
            e3 = vop(lambda e: e.tensor_add(rbuf, rbuf, WK[:, 1024:2048]), waits=[lx])
            nm, var, rstd, sqs = SMc(7), SMc(8), SMc(9), SMc(10)
            vop(lambda e, nm=nm: e.reduce_sum(nm, rbuf, axis=AX.X, negate=True), waits=[("VV", e3)])
            vop(lambda e, nm=nm: e.tensor_scalar(nm, nm, 1.0 / 1024.0, None, op0=ALU.mult))
            v3 = vop(lambda e, nm=nm: e.tensor_scalar(rbuf, rbuf, nm, None, op0=ALU.add))
            s1 = vop(lambda e, sqs=sqs: e.scalar_tensor_tensor(
                WK[:, 1024:2048], rbuf, 1.0, rbuf, op0=ALU.mult, op1=ALU.mult, accum_out=sqs),
                waits=[("VV", v3)])
            v4 = vop(lambda e, var=var, sqs=sqs: e.tensor_scalar(var, sqs, 1.0 / 1024.0, 1e-5,
                                                                 op0=ALU.mult, op1=ALU.add))
            s2 = sop(lambda e, rstd=rstd, var=var: e.activation(rstd, var, AF.Ln), waits=[("VV", v4)])
            s3 = sop(lambda e, rstd=rstd: e.activation(rstd, rstd, AF.Exp, scale=-0.5))
            v5 = vop(lambda e, rstd=rstd: e.tensor_scalar(rbuf, rbuf, rstd, None, op0=ALU.mult),
                     waits=[("SS", s3)])
            v6 = vop(lambda e: e.tensor_copy(D_xfb[:], rbuf), waits=[prev_l2])
            prev_l1 = dma("sync", "LY", xf32_s[b][tt * 128:(tt + 1) * 128, :], rbuf,
                          waits=[("VV", v5)])
            prev_l2 = xf_l2 = dma("sync", "LY",
                                  xfb_s[b * S + tt * 128: b * S + (tt + 1) * 128, :], D_xfb,
                                  waits=[("VV", v6)])
        if b + 1 < nstep_run:
            pre_w[b + 1] = (
                dma("gpsimd", "LG", wq_sb, wqp.rearrange("(dt p) o -> p dt o", p=128),
                    waits=[("MM", max(gst_mm))]),
                dma("gpsimd", "LG", wk_sb, wkp.rearrange("(dt p) o -> p dt o", p=128)))
        xfb_bar = pg.qbar("LY")
        xfT_mm = [0, 0]
        mm_d = pg.cnt["MM"]
        for dt in range(8):
            pp = dt % 2
            lt = dma("scalar", "LS", D_xfT[pp], xfb_s[b * S:(b + 1) * S, dt * 128:(dt + 1) * 128],
                     waits=xfb_bar + [("MM", xfT_mm[pp] if dt >= 2 else mm_d)], transpose=True)
            for sl in range(4):
                xfT_mm[pp] = mm(P[sl][0:8, 0:512], SMB_rw[:, dt, :],
                                D_xfT[pp][:, sl * 512:(sl + 1) * 512], dt == 0, dt == 7,
                                waits=[lt, lrw])
        mrt = pg.cnt["MM"]
        lgTrow = WK[0:8, 0:2048]
        lgv = 0
        for sl in range(4):
            lgv = vop(lambda e, sl=sl: e.tensor_copy(lgTrow[:, sl * 512:(sl + 1) * 512],
                                                     P[sl][0:8, 0:512]),
                      waits=[("MM", mrt)])
        lgsb = D_lgf[:, 0:128].rearrange("p (tt e) -> p tt e", e=8)
        lgev = 0
        for tt in range(NT):
            mt = tr(P[6][:, 0:8], lgTrow[:, tt * 128:(tt + 1) * 128], SM_id8[0:8, :],
                    waits=[("VV", lgv), ("VV", lgev)])
            lgev = vop(lambda e, tt=tt: e.tensor_copy(lgsb[:, tt, :], P[6][:, 0:8]),
                       waits=[("MM", mt)])
        l3 = dma("sync", "LY", lg_s[b], lgsb.rearrange("p tt e -> p (tt e)"), waits=[("VV", lgev)])
        if stop == 'ef0':
            continue
        gop(lambda e, b=b: e.collective_compute(
            "AllGather", ALU.bypass, replica_groups=[list(range(NCORE))],
            ins=[lg_s[b].opt()],
            outs=[lgAG[b * NCORE * 128:(b + 1) * NCORE * 128, :].opt()]),
            waits=[l3], inc=("CC", 1))
        gop(lambda e, b=b: e.collective_compute(
            "AllGather", ALU.bypass, replica_groups=[list(range(NCORE))],
            ins=[xfb_s[b * S:(b + 1) * S, :].opt()],
            outs=[xfAG[b * NCORE * S:(b + 1) * NCORE * S, :].opt()]),
            waits=xfb_bar, inc=("CC", 1))
        ag_n += 2
        prev_step = pg.snap()

    ag_attn = ag_n
    attn_snap = pg.snap()

    def _finish_early():
        sl_ = dma("sync", "LY", out_c[0][0:128, :], REPF[:], waits=pg.wsnap(pg.snap()))
        pg.emit("sync", lambda e: e.nop(), waits=[sl_])
        pg.realize()
        return nc

    if stop is not None and stop != "attn" and (
            stop in ("p0", "attn0", "a0", "b0", "c0", "d0", "ef0")
            or stop.startswith("dh") or stop.startswith("dm") or stop in ("dep", "dskip", "dld", "dsa", "dva")):
        return _finish_early()
    if "xf" in dbg:
        dma("sync", "LY", dbg["xf"], xf32_s.rearrange("b s d -> (b s) d"), waits=pg.wsnap(attn_snap))
    if "logits" in dbg:
        dma("sync", "LY", dbg["logits"].rearrange("b p x -> (b p) x"), lg_s.rearrange("b p x -> (b p) x"), waits=pg.wsnap(attn_snap))
    if stop == "attn":
        return _finish_early()

    L = C[:, 0:8192].bitcast(F32).rearrange("p (t e) -> p t e", e=8)
    ind1 = C[:, 8192:12288].rearrange("p (t e) -> p t e", e=8)
    ind2 = C[:, 12288:16384].rearrange("p (t e) -> p t e", e=8)
    lL_toks = []
    for bb in range(NSTEP):
        for rr in range(NCORE):
            lL_toks.append(dma("sync", "LY", L[:, (bb * 8 + rr) * 16:(bb * 8 + rr + 1) * 16, :],
                     lgAG[(bb * 8 + rr) * 128:(bb * 8 + rr + 1) * 128, :].rearrange(
                         "p (tw e) -> p tw e", e=8),
                     waits=([("CC", max(ag_attn - 1, 0))] + pg.wsnap(attn_snap, ("CC",))) if (bb == 0 and rr == 0) else ()))
    Dx = Dg[:].bitcast(F32)
    m1, m2 = Dx[:, 0:512], Dx[:, 512:1024]
    i1f, i2f = Dx[:, 1024:1536], Dx[:, 1536:2048]
    wt1, wt2 = Dx[:, 2048:2560], Dx[:, 2560:3072]
    scr = Dx[:, 3072:7168]
    scr2v = scr.rearrange("p (t e) -> p t e", e=8)
    se = Dx[:, 7168:7680]
    bc = lambda ap: ap.rearrange("p (t a) -> p t a", a=1).broadcast_to([128, 512, 8])
    i8bc = SM_i8.rearrange("p (a e) -> p a e", a=1).broadcast_to([128, 512, 8])
    rii8bc = SM_rii8.rearrange("p (a e) -> p a e", a=1).broadcast_to([128, 512, 8])
    ohbc = SM_oh.rearrange("p (a e) -> p a e", a=1).broadcast_to([128, 512, 8])
    vop(lambda e: e.reduce_max(m1, L, axis=AX.X), waits=lL_toks + pg.wsnap(attn_snap, ("CC",)))
    vop(lambda e: e.tensor_tensor(scr2v, L, bc(m1), op=ALU.is_equal))
    vop(lambda e: e.tensor_mul(scr2v, scr2v, rii8bc))
    vop(lambda e: e.reduce_max(i1f, scr2v, axis=AX.X))
    vop(lambda e: e.tensor_tensor(scr2v, L, bc(m1), op=ALU.is_equal))
    vop(lambda e: e.tensor_scalar(scr2v, scr2v, 1e30, None, op0=ALU.mult))
    vop(lambda e: e.tensor_tensor(scr2v, L, scr2v, op=ALU.subtract))
    vop(lambda e: e.reduce_max(m2, scr2v, axis=AX.X))
    vop(lambda e: e.tensor_tensor(scr2v, scr2v, bc(m2), op=ALU.is_equal))
    vop(lambda e: e.tensor_mul(scr2v, scr2v, rii8bc))
    vop(lambda e: e.reduce_max(i2f, scr2v, axis=AX.X))
    vop(lambda e: e.tensor_scalar(i1f, i1f, -1.0, 99.0, op0=ALU.mult, op1=ALU.add))
    vop(lambda e: e.tensor_scalar(i2f, i2f, -1.0, 99.0, op0=ALU.mult, op1=ALU.add))
    v14 = vop(lambda e: e.tensor_tensor(scr2v, L, bc(m1), op=ALU.subtract))
    s1 = sop(lambda e: e.activation(scr, scr, AF.Exp), waits=[("VV", v14)])
    vop(lambda e: e.reduce_sum(se, scr2v, axis=AX.X), waits=[("SS", s1)])
    vop(lambda e: e.reciprocal(wt1, se))
    v17 = vop(lambda e: e.tensor_tensor(wt2, m2, m1, op=ALU.subtract))
    s2 = sop(lambda e: e.activation(wt2, wt2, AF.Exp), waits=[("VV", v17)])
    vop(lambda e: e.tensor_mul(wt2, wt2, wt1), waits=[("SS", s2)])
    vop(lambda e: e.tensor_tensor(ind1, bc(i1f), i8bc, op=ALU.is_equal))
    ind_done = vop(lambda e: e.tensor_tensor(ind2, bc(i2f), i8bc, op=ALU.is_equal))

    ranks = A[:].bitcast(F32).rearrange("p (t e) -> p t e", e=16)
    rk_ev = 0
    cs_ev = 0
    cs_toks = []
    for Gi in range(4):
        mmax = 0
        for trel in range(128):
            tile = Gi * 128 + trel
            bank = trel // 32
            col = (trel % 32) * 16
            for kk, indv in enumerate((ind1, ind2)):
                w = ([("VV", ind_done), ("VV", rk_ev)] + pg.wsnap(attn_snap, ("CC",))) if (trel == 0 and kk == 0) else ()
                mmax = mm(P[bank][:, col + kk * 8: col + kk * 8 + 8], SMB_su,
                          indv[:, tile, :], True, True, waits=w)
        for q4 in range(4):
            rk_ev = vop(lambda e, Gi=Gi, q4=q4: e.tensor_copy(
                ranks[:, Gi * 128 + q4 * 32:Gi * 128 + (q4 + 1) * 32, :],
                P[q4][:, 0:512].rearrange("p (t e) -> p t e", e=16)),
                waits=[("MM", mmax)])
        for h16 in range(4):
            mc = 0
            for tr2 in range(32):
                tile = Gi * 128 + h16 * 32 + tr2
                for kk, indv in enumerate((ind1, ind2)):
                    w = [("VV", cs_ev)] if (tr2 == 0 and kk == 0) else ()
                    mc = mm(P[4][0:1, tr2 * 16 + kk * 8: tr2 * 16 + kk * 8 + 8],
                            SMB_onesc, indv[:, tile, :], True, True, waits=w)
            cs_ev = vop(lambda e, h16=h16: e.tensor_copy(WK[0:1, h16 * 512:(h16 + 1) * 512],
                                                         P[4][0:1, 0:512]),
                        waits=[("MM", mc), (cs_toks[-1] if cs_toks else ("MM", 0))])
        cs_toks.append(dma("sync", "LY", cstage[Gi * 2048:(Gi + 1) * 2048].rearrange("(a b) -> a b", a=1),
                           WK[0:1, 0:2048], waits=[("VV", cs_ev)]))
    cs2 = REPB[:, 0:64]
    lc2 = dma("gpsimd", "LG", cs2.rearrange("p (t1 ke) -> p t1 ke", ke=16),
              cstage.rearrange("(t1 p ke) -> p t1 ke", p=128, ke=16),
              waits=cs_toks + pg.wsnap(attn_snap, ("CC",)))
    mb = mm(P[5][:, 0:64], SMB_su, cs2, True, False, waits=[lc2])
    mtb = mm(P[4][0:1, 0:64], SMB_onesc, cs2, True, True)
    tb, bp = WK2[0:1, 64:128], WK2[0:1, 128:192]
    vtb = vop(lambda e: e.tensor_copy(tb, P[4][0:1, 0:64]), waits=[("MM", mtb)])
    vop(lambda e: e.memset(bp[:, 0:16], 0.0))
    vop(lambda e: e.tensor_copy(bp[:, 16:32], tb[:, 0:16]), waits=[("VV", vtb)])
    vop(lambda e: e.tensor_add(bp[:, 32:48], bp[:, 16:32], tb[:, 16:32]))
    vb3 = vop(lambda e: e.tensor_add(bp[:, 48:64], bp[:, 32:48], tb[:, 32:48]))
    mb2 = mm(P[5][:, 0:64], ONESF[:], bp, False, True, waits=[("VV", vb3)])
    vbase = vop(lambda e: e.tensor_copy(WK2[:, 192:256], P[5][:, 0:64]), waits=[("MM", mb2)])
    lbs = dma("sync", "LY", bstage.rearrange("(t1 p ke) -> p t1 ke", p=128, ke=16),
              WK2[:, 192:256].rearrange("p (t1 ke) -> p t1 ke", ke=16), waits=[("VV", vbase)])
    radd_ev = 0
    prev_lbr2 = ("MM", 0)
    for Gi in range(4):
        lbr2 = dma("sync", "LY", WK2[0:1, 0:2048],
                   bstage[Gi * 2048:(Gi + 1) * 2048].rearrange("(a b) -> a b", a=1),
                   waits=[lbs, prev_lbr2, ("MM", pg.cnt["MM"])])
        prev_lbr2 = lbr2
        for q4 in range(4):
            mq = mm(P[q4][:, 0:512], ONESF[:], WK2[0:1, q4 * 512:(q4 + 1) * 512],
                    True, True, waits=[lbr2, ("VV", rk_ev), ("VV", radd_ev)])
            radd_ev = vop(lambda e, Gi=Gi, q4=q4: e.tensor_add(
                A[:].bitcast(F32)[:, Gi * 2048 + q4 * 512:Gi * 2048 + (q4 + 1) * 512],
                A[:].bitcast(F32)[:, Gi * 2048 + q4 * 512:Gi * 2048 + (q4 + 1) * 512],
                P[q4][:, 0:512]), waits=[("MM", mq)])
    ranks_done = radd_ev

    kept_k, vv_, pp_, jj_ = WK2[:, 0:512], WK2[:, 512:1024], WK2[:, 1024:1536], WK2[:, 1536:2048]
    yr, wfin = se, m2
    stash = WK[:, 0:2048].rearrange("p (t f) -> p t f", f=4)
    yrw_ev = 0
    for kk, (indv, iv, wv_) in enumerate(((ind1, i1f, wt1), (ind2, i2f, wt2))):
        rk = Dx[:, 7680:8192]
        vop(lambda e, indv=indv, kk=kk: e.tensor_tensor(
            scr2v, ranks[:, :, kk * 8:(kk + 1) * 8], indv, op=ALU.mult),
            waits=[("VV", ranks_done), ("VV", yrw_ev)])
        vop(lambda e, rk=rk: e.reduce_sum(rk, scr2v, axis=AX.X))
        vop(lambda e, rk=rk: e.tensor_scalar(kept_k, rk, 8192.0, None, op0=ALU.is_lt))
        vop(lambda e, rk=rk, kk=kk: e.tensor_scalar(vv_, rk, float(kk * CAP), None, op0=ALU.add))
        vi32 = scr[:, 0:512].bitcast(I32)
        pi32 = scr[:, 512:1024].bitcast(I32)
        vop(lambda e, vi32=vi32: e.tensor_copy(vi32, vv_))
        vop(lambda e, vi32=vi32, pi32=pi32: e.tensor_scalar(pi32, vi32, 2047, None, op0=ALU.bitwise_and))
        vop(lambda e, pi32=pi32: e.tensor_copy(pp_, pi32))
        vop(lambda e, vi32=vi32, pi32=pi32: e.tensor_scalar(pi32, vi32, 11, None, op0=ALU.arith_shift_right))
        vop(lambda e, pi32=pi32: e.tensor_copy(jj_, pi32))
        vop(lambda e: e.tensor_scalar(yr, jj_, 16384.0, None, op0=ALU.mult))
        vop(lambda e, iv=iv: e.scalar_tensor_tensor(yr, iv, 2048.0, yr, op0=ALU.mult, op1=ALU.add))
        vop(lambda e: e.tensor_add(yr, yr, pp_))
        vop(lambda e: e.tensor_mul(yr, yr, kept_k))
        vop(lambda e, wv_=wv_: e.tensor_mul(wfin, wv_, kept_k))
        vop(lambda e, kk=kk: e.tensor_copy(stash[:, 0:512, kk:kk + 1],
                                           yr.rearrange("p (t a) -> p t a", a=1)))
        yrw_ev = vop(lambda e, kk=kk: e.tensor_copy(stash[:, 0:512, 2 + kk:3 + kk],
                                                    wfin.rearrange("p (t a) -> p t a", a=1)))
    lyrw = dma("sync", "LY", yrw_s.rearrange("(p t) f -> p t f", p=128), stash,
               waits=[("VV", yrw_ev)])

    i128bc4 = SM_i128.rearrange("p (a r) -> p a r", a=1).broadcast_to([128, 4, 128])
    PTb = E[:, 0:512].rearrange("p (t r) -> p t r", r=128)
    CLO, CHI = E[:, 1024:1536], E[:, 1536:2048]
    CT = WK2[:].bitcast(BF16)
    prev_kk = ("MM", 0)
    sent_v = vop(lambda e: e.memset(SMc(15)[0:1, :], 999999.0), waits=[("VV", yrw_ev)])
    col_toks = []
    for kk, indv in enumerate((ind1, ind2)):
        roe, sel = Dx[:, 7680:8192], Dx[:, 0:512]
        va = vop(lambda e, kk=kk: e.tensor_tensor(
            scr2v, ranks[:, :, kk * 8:(kk + 1) * 8], ohbc, op=ALU.mult),
            waits=[("VV", yrw_ev), lyrw, prev_kk])
        vop(lambda e, roe=roe: e.reduce_sum(roe, scr2v, axis=AX.X))
        vop(lambda e, indv=indv: e.tensor_tensor(scr2v, indv, ohbc, op=ALU.mult))
        selv = vop(lambda e, sel=sel: e.reduce_sum(sel, scr2v, axis=AX.X))
        lrow = dma("sync", "LY", rowstage[kk * 1024:kk * 1024 + 512].rearrange("(a b) -> a b", a=1),
                   roe[0:1, :], waits=[("VV", selv)])
        lsen = dma("sync", "LY", rowstage[kk * 1024 + 512:kk * 1024 + 513].rearrange("(a b) -> a b", a=1),
                   SMc(15)[0:1, :], waits=[("VV", sent_v)])
        for j in range(4):
            col_toks.append(dma("sync", "LY", SM_base[kk][:, j:j + 1],
                                rowstage[kk * 1024 + j * 128:kk * 1024 + (j + 1) * 128].rearrange("(p a) -> p a", a=1),
                                waits=[lrow]))
            col_toks.append(dma("sync", "LY", SM_basep1[kk][:, j:j + 1],
                                rowstage[kk * 1024 + j * 128 + 1:kk * 1024 + (j + 1) * 128 + 1].rearrange("(p a) -> p a", a=1),
                                waits=[lrow, lsen]))
        vop(lambda e, kk=kk: e.tensor_sub(SM_cnt[kk], SM_basep1[kk], SM_base[kk]),
            waits=col_toks[-8:])
        rof = Dx[:, 512:1024]
        mK = mm(P[6][:, 0:512], ONESF[:], roe[0:1, 0:512], True, True, waits=[("VV", selv)])
        vf = vop(lambda e, roe=roe, rof=rof: e.tensor_tensor(rof, roe, P[6][:, 0:512], op=ALU.subtract),
                 waits=[("MM", mK)])
        rofb, selb = C[:, 0:512], C[:, 512:1024]
        vop(lambda e, rof=rof: e.tensor_copy(rofb, rof), waits=[("VV", vf)])
        vh2 = vop(lambda e, sel=sel: e.tensor_copy(selb, sel))
        pt_ev = 0
        clo_ev = 0
        for g4 in range(128):
            rofs = rofb.rearrange("p (t a) -> p t a", a=1)[:, g4 * 4:(g4 + 1) * 4, :].broadcast_to([128, 4, 128])
            sels = selb.rearrange("p (t a) -> p t a", a=1)[:, g4 * 4:(g4 + 1) * 4, :].broadcast_to([128, 4, 128])
            vop(lambda e, rofs=rofs: e.tensor_tensor(PTb, i128bc4, rofs, op=ALU.is_equal),
                waits=[("VV", vh2), ("MM", pg.cnt["MM"])])
            pt_ev = vop(lambda e, sels=sels: e.tensor_tensor(PTb, PTb, sels, op=ALU.mult))
            mc2 = 0
            for t4 in range(4):
                tile = g4 * 4 + t4
                col = (tile % 32) * 4
                mc2 = mm(P[7][:, col:col + 2], PTb[:, t4, :], IDC[:, tile * 2:tile * 2 + 2],
                         True, True, waits=[("VV", pt_ev), ("VV", clo_ev if (tile % 32) == 0 else 0)])
            if g4 % 8 == 7:
                f0 = (g4 // 8) * 32
                pv = P[7][:, 0:128].rearrange("p (t f) -> p t f", f=4)
                clo_ev = vop(lambda e, f0=f0, pv=pv: e.tensor_copy(
                    CLO[:, f0:f0 + 32].rearrange("p (t a) -> p t a", a=1),
                    pv[:, 0:32, 0:1]), waits=[("MM", mc2)])
                clo_ev = vop(lambda e, f0=f0, pv=pv: e.tensor_copy(
                    CHI[:, f0:f0 + 32].rearrange("p (t a) -> p t a", a=1),
                    pv[:, 0:32, 1:2]))
        ct_prev = 0
        for g2 in range(4):
            for plane, srcb in enumerate((CLO, CHI)):
                mt = tr(P[6][:, 0:64].bitcast(BF16), srcb[:, g2 * 128:(g2 + 1) * 128], SMB_id,
                        waits=[("VV", clo_ev), ("VV", ct_prev)])
                ct_prev = vop(lambda e, g2=g2, plane=plane: e.tensor_copy(
                    CT[:, (plane * 4 + g2) * 128:(plane * 4 + g2 + 1) * 128],
                    P[6][:, 0:64].bitcast(BF16)), waits=[("MM", mt)])
        ctc = WK[:, 0:512].rearrange("p (g r) -> p g r", r=128)
        ctlo = CT[:, 0:512].rearrange("p (g r) -> p g r", r=128)
        cthi = CT[:, 512:1024].rearrange("p (g r) -> p g r", r=128)
        vcm = vop(lambda e: e.scalar_tensor_tensor(ctc, cthi, 256.0, ctlo, op0=ALU.mult, op1=ALU.add),
                  waits=[("VV", ct_prev), lyrw])
        cti = WK[:, 512:1024].bitcast(I32).rearrange("p (g r) -> p g r", r=128)
        vci = vop(lambda e: e.tensor_copy(cti, ctc), waits=[("VV", vcm)])
        prev_kk = dma("sync", "LY", ctflat[kk].rearrange("(g p r) a -> p g (r a)", p=128, r=128),
                      cti, waits=[("VV", vci)])
        col_toks.append(prev_kk)
    idx_done = pg.snap()
    if stop == "index":
        return _finish_early()

    w1lo = Bg[:].rearrange("p (dt f) -> p dt f", dt=4)
    w1hi = E[:].rearrange("p (dt f) -> p dt f", dt=4)
    w2sb = C[:].rearrange("p (ft d) -> p ft d", ft=16)
    w2sb2 = A[:].rearrange("p (ft d) -> p ft d", ft=16)
    lw1a = dma("gpsimd", "LG", w1lo, w1c[0:512, :].rearrange("(dt p) f -> p dt f", p=128),
               waits=pg.wsnap(idx_done, ("CC",)))
    lw1b = dma("gpsimd", "LG", w1hi, w1c[512:1024, :].rearrange("(dt p) f -> p dt f", p=128),
               waits=pg.wsnap(idx_done, ("CC",)))
    lw2a = dma("gpsimd", "LG", w2sb, w2c[0:2048, :].rearrange("(ft p) d -> p ft d", p=128),
               waits=pg.wsnap(idx_done, ("CC",)))
    lw2b = dma("gpsimd", "LG", w2sb2, w2c[2048:4096, :].rearrange("(ft p) d -> p ft d", p=128),
               waits=pg.wsnap(idx_done, ("CC",)))

    def w1slice(dt, f0, f1):
        return (w1lo if dt < 4 else w1hi)[:, dt % 4, f0:f1]

    def w2slice(ft, d0, d1):
        return (w2sb if ft < 16 else w2sb2)[:, ft % 16, d0:d1]

    hT_sb = Dg[:].rearrange("p (ft s) -> p ft s", ft=32)
    xeT = [Fg[:, i * 4096:(i + 1) * 4096].rearrange("p (dt s) -> p dt s", dt=8) for i in range(2)]
    gb = W2b[:, 0:2048]
    NCH = 32
    gat_l = [ZT, ZT, ZT, ZT]
    yag_toks = []
    xe_mm = [0, 0]
    xet_marks = {}
    FS = dict(ctm=0, cti=0, h_ev=0, y_ev=0, hT_mm=0)

    def ffn_gather(ch):
        kk = ch // 16
        j0 = (ch % 16) * 4
        pp = ch % 2
        cst_toks = []
        for q in range(4):
            jslot = j0 + q
            icol, icol2 = q * 2, q * 2 + 1
            tj = WK[:, 0:128]
            vtj = vop(lambda e, jslot=jslot, tj=tj: e.tensor_scalar(
                tj, SM_i128, float(jslot * 128), None, op0=ALU.add),
                waits=(col_toks if (ch == 0 and q == 0) else [("MM", FS['ctm'])]))
            tjb = tj.rearrange("p (a s) -> p a s", a=1).broadcast_to([128, 4, 128])
            bbc = SM_base[kk].rearrange("p (t a) -> p t a", a=1).broadcast_to([128, 4, 128])
            bp1c = SM_basep1[kk].rearrange("p (t a) -> p t a", a=1).broadcast_to([128, 4, 128])
            cntc = SM_cnt[kk].rearrange("p (t a) -> p t a", a=1).broadcast_to([128, 4, 128])
            B4 = W2b[:, 2048:2560].rearrange("p (t s) -> p t s", s=128)
            C4 = W2b[:, 2560:3072].rearrange("p (t s) -> p t s", s=128)
            vb4 = vop(lambda e, tjb=tjb, bbc=bbc, B4=B4: e.tensor_tensor(B4, tjb, bbc, op=ALU.is_ge))
            vc4 = vop(lambda e, tjb=tjb, bp1c=bp1c, C4=C4: e.tensor_tensor(C4, tjb, bp1c, op=ALU.is_ge))
            vc5 = vop(lambda e, C4=C4, cntc=cntc: e.tensor_tensor(C4, C4, cntc, op=ALU.mult))
            for t1 in range(4):
                mB = mm(P[4][:, 0:1], B4[:, t1, :], SMB_onesc, t1 == 0, t1 == 3,
                        waits=([("VV", vc5), ("VV", FS['cti'])] if t1 == 0 else ()))
                FS['ctm'] = mm(P[5][:, 0:1], C4[:, t1, :], SMB_onesc, t1 == 0, t1 == 3)
            cidx = WK[:, 128:129]
            vop(lambda e, cidx=cidx: e.tensor_scalar(cidx, P[4][:, 0:1], 128.0, -128.0,
                                                     op0=ALU.mult, op1=ALU.add),
                waits=[("MM", FS['ctm'])])
            vop(lambda e, cidx=cidx: e.scalar_tensor_tensor(
                cidx, SM_iotac, 1.0, cidx, op0=ALU.mult, op1=ALU.add))
            vop(lambda e, cidx=cidx, jslot=jslot: e.tensor_scalar(
                cidx, cidx, float(jslot * 128), None, op0=ALU.add))
            vop(lambda e, cidx=cidx: e.tensor_tensor(cidx, cidx, P[5][:, 0:1], op=ALU.subtract))
            vop(lambda e, cidx=cidx: e.tensor_scalar(cidx, cidx, 0.0, 65535.0,
                                                     op0=ALU.max, op1=ALU.min))
            FS['cti'] = vop(lambda e, cidx=cidx, icol=icol: e.tensor_copy(ISL[:, icol:icol + 1], cidx),
                            waits=[gat_l[2 + q % 2]])
            gq1 = gop(lambda e, icol=icol, icol2=icol2, kk=kk: e.indirect_dma_start(
                out=ISL[:, icol2:icol2 + 1], out_offset=None, in_=ctflat[kk],
                in_offset=IndirectOffsetOnAxis(ISL[:, icol:icol + 1], 0)),
                waits=[("VV", FS['cti'])])
            gq = gop(lambda e, icol2=icol2, q=q: e.indirect_dma_start(
                out=gb[:, (q % 2) * 1024:(q % 2) * 1024 + 1024],
                out_offset=None, in_=xfAG,
                in_offset=IndirectOffsetOnAxis(ISL[:, icol2:icol2 + 1], 0)),
                waits=[gq1, ("CC", ag_attn), gat_l[q % 2]])
            gat_l[2 + q % 2] = gq
            gat_l[q % 2] = dma("sync", "LY", comp_s[ch * 512 + q * 128:ch * 512 + (q + 1) * 128, :],
                               gb[:, (q % 2) * 1024:(q % 2) * 1024 + 1024], waits=[gq])
            cst_toks.append(gat_l[q % 2])
        xm = []
        for dt in range(8):
            xm.append(dma("scalar", "LS", xeT[pp][:, dt, :],
                          comp_s[ch * 512:(ch + 1) * 512, dt * 128:(dt + 1) * 128],
                          waits=cst_toks + [("MM", xe_mm[pp])],
                          transpose=True))
        xet_marks[ch] = xm

    def ffn_compute(ch):
        nonlocal ag_n
        pp = ch % 2
        for fs in range(32):
            psp = P[fs % 2]
            for dt in range(8):
                w = (xet_marks[ch] + [lw1a, lw1b, ("SS", FS['h_ev'])]) if dt == 0 else ()
                m = mm(psp[:, 0:512], w1slice(dt, fs * 128, (fs + 1) * 128),
                       xeT[pp][:, dt, :], dt == 0, dt == 7, waits=w)
            xe_mm[pp] = m
            FS['h_ev'] = sop(lambda e, fs=fs, psp=psp: e.activation(
                hT_sb[:, fs, :], psp[:, 0:512], AF.Gelu_apprx_tanh),
                waits=[("MM", m), ("MM", FS['hT_mm'] if fs < 2 else 0)])
        yst_l = [ZT, ZT]
        for st in range(4):
            yp = st % 2
            for dh in range(2):
                psy = P[2 + (st * 2 + dh) % 2]
                for fs in range(32):
                    w = ([("SS", FS['h_ev']), lw2a, lw2b, ("VV", FS['y_ev']),
                          yst_l[yp]] if fs == 0 else ())
                    m = mm(psy[:, 0:512], hT_sb[:, fs, st * 128:(st + 1) * 128],
                           w2slice(fs, dh * 512, (dh + 1) * 512), fs == 0, fs == 31, waits=w)
                FS['hT_mm'] = m
                FS['y_ev'] = vop(lambda e, dh=dh, psy=psy, yp=yp: e.tensor_copy(
                    REPB[:, yp * 1024 + dh * 512: yp * 1024 + (dh + 1) * 512],
                    psy[:, 0:512]), waits=[("MM", m)])
            yst_l[yp] = dma("sync", "LY",
                            ycmp_s[ch * 512 + st * 128:ch * 512 + (st + 1) * 128, :],
                            REPB[:, yp * 1024:(yp + 1) * 1024], waits=[("VV", FS['y_ev'])])
            yag_toks.append(yst_l[yp])
        if ch % 4 == 3:
            jag = ch // 4
            gop(lambda e, jag=jag: e.collective_compute(
                "AllGather", ALU.bypass, replica_groups=[list(range(NCORE))],
                ins=[ycmp_s[jag * 2048:(jag + 1) * 2048, :].opt()],
                outs=[yAG[jag * NCORE * 2048:(jag + 1) * NCORE * 2048, :].opt()]),
                waits=list(yag_toks), inc=("CC", 1))
            ag_n += 1
            yag_toks.clear()

    IDCi = IDC[:].bitcast(I32)
    fpre = {}

    def pre_stage(jt):
        rw = WK2[:, 1536 + 4 * jt:1540 + 4 * jt]
        g1 = gop(lambda e, jt=jt, rw=rw: e.indirect_dma_start(
            out=rw, out_offset=None, in_=yrw_s,
            in_offset=IndirectOffsetOnAxis(OWN[:, jt:jt + 1], 0)),
            waits=[lyrw] + (pg.wsnap(idx_done, ("CC",)) if jt == 0 else []))
        r1v = vop(lambda e, rw=rw, jt=jt: e.tensor_copy(IDCi[:, 2 * jt:2 * jt + 1], rw[:, 0:1]),
                  waits=[g1] + (pg.wsnap(idx_done, ("CC",)) if jt == 0 else []))
        r2v = vop(lambda e, rw=rw, jt=jt: e.tensor_copy(IDCi[:, 2 * jt + 1:2 * jt + 2], rw[:, 1:2]))
        fpre[jt] = (r1v, r2v)

    ffn_gather(0)
    for ch in range(NCH):
        if ch + 1 < NCH:
            ffn_gather(ch + 1)
        ffn_compute(ch)
        pre_stage(2 * ch)
        pre_stage(2 * ch + 1)
    y_ev = FS['y_ev']
    if "ycmp" in dbg:
        dma("sync", "LY", dbg["ycmp"], ycmp_s, waits=pg.qbar("LY") + [("VV", y_ev)])
    ffn_snap = pg.snap()
    if stop == "ffn":
        return _finish_early()

    lfb = dma("sync", "LY", WK2[0:1, 0:1024], fbi.rearrange("(a d) -> a d", a=1),
              waits=pg.wsnap(ffn_snap, ("CC",)))
    ovf = vop(lambda e: e.memset(WK2[0:1, 1024:1152], 1.0), waits=pg.wsnap(ffn_snap, ("CC",)))
    repf_ev = 0
    for sl in range(2):
        m = mm(P[6][:, 0:512], WK2[0:1, 1024:1152], WK2[0:1, sl * 512:(sl + 1) * 512],
               True, True, waits=[lfb, ("VV", ovf), ("VV", repf_ev)])
        repf_ev = vop(lambda e, sl=sl: e.tensor_copy(REPF[:, sl * 512:(sl + 1) * 512], P[6][:, 0:512]),
                      waits=[("MM", m)])
    NP = 4
    SKEW = 2
    Af = A[:].bitcast(F32)
    Bf = Bg[:].bitcast(F32)
    ffw = [("MM", pg.cnt["MM"]), ("VV", pg.cnt["VV"]), ("SS", pg.cnt["SS"])]
    st_l = [ZT] * NP
    fmarks = {}

    def fin_gather(jt):
        b, tt = divmod(jt, NT)
        p = jt % NP
        first = [] if jt >= NP else ffw
        yb = Dg[:, p * 2048:(p + 1) * 2048]
        xb = Af[:, p * 1024:(p + 1) * 1024]
        g2 = gop(lambda e, yb=yb, jt=jt: e.indirect_dma_start(
            out=yb[:, 0:1024], out_offset=None, in_=yAG,
            in_offset=IndirectOffsetOnAxis(IDCi[:, 2 * jt:2 * jt + 1], 0)),
            waits=[("VV", fpre[jt][0]), ("CC", ag_n), st_l[p]] + first)
        g3 = gop(lambda e, yb=yb, jt=jt: e.indirect_dma_start(
            out=yb[:, 1024:2048], out_offset=None, in_=yAG,
            in_offset=IndirectOffsetOnAxis(IDCi[:, 2 * jt + 1:2 * jt + 2], 0)),
            waits=[("VV", fpre[jt][1])])
        lx2 = dma("sync", "LY", xb, xf32_s[b][tt * 128:(tt + 1) * 128, :],
                  waits=[st_l[p]] + first)
        fmarks[jt] = (g2, g3, lx2)

    def fin_compute(jt):
        b, tt = divmod(jt, NT)
        p = jt % NP
        g2, g3, lx2 = fmarks.pop(jt)
        rw = WK2[:, 1536 + 4 * jt:1540 + 4 * jt]
        yb = Dg[:, p * 2048:(p + 1) * 2048]
        xb = Af[:, p * 1024:(p + 1) * 1024]
        acc = Bf[:, p * 1024:(p + 1) * 1024]
        a1 = vop(lambda e, acc=acc, yb=yb, rw=rw: e.scalar_tensor_tensor(
            acc, yb[:, 0:1024], rw[:, 2:3], REPF[:], op0=ALU.mult, op1=ALU.add),
            waits=[g2, g3, ("VV", repf_ev)])
        a2 = vop(lambda e, acc=acc, yb=yb, rw=rw: e.scalar_tensor_tensor(
            acc, yb[:, 1024:2048], rw[:, 3:4], acc, op0=ALU.mult, op1=ALU.add))
        a3 = vop(lambda e, acc=acc, xb=xb: e.tensor_add(acc, acc, xb), waits=[lx2])
        nm, var, rstd, sqs = SMc(p * 4), SMc(p * 4 + 1), SMc(p * 4 + 2), SMc(p * 4 + 3)
        vop(lambda e, nm=nm, acc=acc: e.reduce_sum(nm, acc, axis=AX.X, negate=True),
            waits=[("VV", a3)])
        vop(lambda e, nm=nm: e.tensor_scalar(nm, nm, 1.0 / 1024.0, None, op0=ALU.mult))
        v3 = vop(lambda e, nm=nm, acc=acc: e.tensor_scalar(acc, acc, nm, None, op0=ALU.add))
        s1 = vop(lambda e, sqs=sqs, acc=acc, xb=xb: e.scalar_tensor_tensor(
            xb, acc, 1.0, acc, op0=ALU.mult, op1=ALU.mult, accum_out=sqs),
            waits=[("VV", v3)])
        v4 = vop(lambda e, var=var, sqs=sqs: e.tensor_scalar(var, sqs, 1.0 / 1024.0, 1e-5,
                                                             op0=ALU.mult, op1=ALU.add))
        s2 = sop(lambda e, rstd=rstd, var=var: e.activation(rstd, var, AF.Ln), waits=[("VV", v4)])
        s3 = sop(lambda e, rstd=rstd: e.activation(rstd, rstd, AF.Exp, scale=-0.5))
        v5 = vop(lambda e, rstd=rstd, acc=acc: e.tensor_scalar(acc, acc, rstd, None, op0=ALU.mult),
                 waits=[("SS", s3)])
        st_l[p] = dma("sync", "LY", out_c[b][tt * 128:(tt + 1) * 128, :], acc, waits=[("VV", v5)])

    NTILE = NSTEP * NT
    for jt in range(NTILE):
        fin_gather(jt)
        if jt >= SKEW:
            fin_compute(jt - SKEW)
    for jt in range(NTILE - SKEW, NTILE):
        fin_compute(jt)
    pg.emit("sync", lambda e: e.nop(), waits=list(st_l))

    pg.realize()
    return nc


_NC_CACHE = None


def kernel(_stop=None, _dbg=None, **inputs):
    global _NC_CACHE
    perm = np.concatenate([np.arange(0, 64, 2), np.arange(1, 64, 2)])
    PERM = np.concatenate([h * 64 + perm for h in range(H)])
    f32 = np.float32
    Wq = np.ascontiguousarray(inputs["L0_attn_Wq"][:, PERM], dtype=f32)
    bq = np.ascontiguousarray(np.asarray(inputs["L0_attn_bq"], dtype=f32)[PERM].reshape(8, 128).T)
    Wk = np.ascontiguousarray(inputs["L0_attn_Wk"][:, PERM], dtype=f32)
    bk = np.ascontiguousarray(np.asarray(inputs["L0_attn_bk"], dtype=f32)[PERM].reshape(8, 128).T)
    x = np.asarray(inputs["L0_attn_x"], dtype=f32)
    if _stop is None and _dbg is None:
        if _NC_CACHE is None:
            _NC_CACHE = build_program()
        nc = _NC_CACHE
    else:
        nc = build_program(stop=_stop, dbg_keys=_dbg or ())
    in_maps = []
    for c in range(NCORE):
        batches = [c + 8 * bb for bb in range(NSTEP)]
        oh = np.zeros((128, 8), f32)
        oh[:, c] = 1.0
        tglob = np.concatenate([(np.arange(S, dtype=np.int64) + bb * S) for bb in batches])
        ownpos = ((tglob % 128) * 512 + tglob // 128).astype(np.int32)
        own = np.ascontiguousarray(ownpos.reshape(64, 128).T)
        in_maps.append({
            "x4": np.ascontiguousarray(x[batches]),
            "wqp": Wq, "bqp": bq, "wkp": Wk, "bkp": bk,
            "wvi": np.ascontiguousarray(inputs["L0_attn_Wv"], dtype=f32),
            "bvi": np.ascontiguousarray(inputs["L0_attn_bv"], dtype=f32),
            "wgi": np.ascontiguousarray(inputs["L0_attn_Wg"], dtype=f32),
            "bgi": np.ascontiguousarray(inputs["L0_attn_bg"], dtype=f32),
            "woi": np.ascontiguousarray(inputs["L0_attn_Wo"], dtype=f32),
            "boi": np.ascontiguousarray(inputs["L0_attn_bo"], dtype=f32),
            "rwi": np.ascontiguousarray(inputs["L0_ffn_router_w"], dtype=f32),
            "w1c": np.ascontiguousarray(inputs["L0_ffn_w1"][c], dtype=f32),
            "w2c": np.ascontiguousarray(inputs["L0_ffn_w2"][c], dtype=f32),
            "fbi": np.ascontiguousarray(np.asarray(inputs["L0_ffn_bias"], dtype=f32).reshape(D)),
            "onehot": oh,
            "ownrows": own,
        })
    res = run_bass_kernel_spmd(nc, in_maps, core_ids=list(range(NCORE)))
    if _stop is not None or _dbg:
        return res
    out = np.empty((B, S, D), np.float32)
    for c in range(NCORE):
        oc = res.results[c]["out_c"]
        for bb in range(NSTEP):
            out[c + 8 * bb] = oc[bb]
    return out


# version 16
# speedup vs baseline: 1.0185x; 1.0061x over previous
import numpy as np
import ml_dtypes

import concourse.bass as bass
import concourse.mybir as mybir
from concourse.bass_utils import run_bass_kernel_spmd
from concourse.bass_core_rust import IndirectOffsetOnAxis

F32 = mybir.dt.float32
BF16 = mybir.dt.bfloat16
I32 = mybir.dt.int32
AF = mybir.ActivationFunctionType
ALU = mybir.AluOpType
AX = mybir.AxisListType
BF = ml_dtypes.bfloat16

B, S, D = 32, 2048, 1024
H = 16
NCORE, NSTEP = 8, 4
NT = S // 128
GT = 512
CAP = 8192
PAD = 128
PLANE = 2 * (CAP + PAD)
ZROW = NCORE * 2 * CAP
ENGS = ("tensor", "vector", "scalar", "gpsimd", "sync")
DEBUG_OUTS = ()


def _tables():
    t = {}
    ah = 1.0 / 10000.0 ** np.linspace(0.0, 1.0, 32, dtype=np.float64)
    idx = np.arange(S, dtype=np.float64)
    cos_h = np.cos(idx[:, None] * ah[None, :])
    sin_h = np.sin(idx[:, None] * ah[None, :])
    t["cos4"] = np.tile(cos_h.T.astype(np.float32), (4, 1)).astype(BF)
    sgn = np.repeat(np.array([-1.0, 1.0, -1.0, 1.0], np.float32), 32)
    t["sin4"] = (np.tile(sin_h.T.astype(np.float32), (4, 1)) * sgn[:, None]).astype(BF)
    decay = np.log(1.0 - 2.0 ** (-5.0 - np.arange(H, dtype=np.float64)))
    g_negt = np.exp(-decay[:, None] * idx[None, :])
    gk = np.empty((8, 128, S), np.float32)
    for j in range(8):
        gk[j, 0:64] = (64.0 ** -0.5) * g_negt[2 * j]
        gk[j, 64:128] = (64.0 ** -0.5) * g_negt[2 * j + 1]
    t["gkrep"] = gk.astype(BF)
    masksum = (1.0 - np.exp(decay[:, None] * (idx[None, :] + 1))) / (1.0 - np.exp(decay[:, None]))
    c_tab = (np.exp(decay[:, None] * idx[None, :]) / np.sqrt(masksum)).astype(np.float32)
    ca = np.empty((128, 256), np.float32)
    for h in range(H):
        for s0 in range(16):
            ca[:, h * 16 + s0] = c_tab[h, s0 * 128:(s0 + 1) * 128]
    t["c_all"] = ca
    tt, ss = np.meshgrid(np.arange(128), np.arange(128), indexing="ij")
    t["tri"] = (tt <= ss).astype(np.float32).astype(BF)
    t["su"] = (tt < ss).astype(np.float32).astype(BF)
    t["id128b"] = np.eye(128, dtype=np.float32).astype(BF)
    t["id8f"] = np.eye(8, dtype=np.float32)
    t["iota128r"] = np.broadcast_to(np.arange(128, dtype=np.float32), (128, 128)).copy()
    t["iota8r"] = np.broadcast_to(np.arange(8, dtype=np.float32), (128, 8)).copy()
    t["rii8r"] = (99.0 - t["iota8r"]).copy()
    t["ones1x"] = np.ones((1, 128), np.float32).astype(BF)
    t["iotacol"] = np.arange(128, dtype=np.float32).reshape(128, 1)
    t["onescol"] = np.ones((128, 1), np.float32).astype(BF)
    idc = np.empty((128, GT, 2), np.float32)
    p = np.arange(128)
    for tile in range(GT):
        idc[:, tile, 0] = (tile % 2) * 128 + p
        idc[:, tile, 1] = tile // 2
    t["idcols"] = idc.astype(BF)
    return t


QDEPTH = 4


class Prog:
    def __init__(self, nc):
        self.nc = nc
        qsems = tuple(f"{q}{i}" for q in ("LY", "LG", "LS") for i in range(QDEPTH))
        self.names = ("MM", "VV", "SS", "CC") + qsems
        self.sems = {n: nc.alloc_semaphore(n) for n in self.names}
        self.cnt = {n: 0 for n in self.names}
        self.hw = {e: {n: 0 for n in self.names} for e in ENGS}
        self.qidx = {"LY": 0, "LG": 0, "LS": 0}

    def emit(self, eng, fn, waits=(), inc=None, force_self=False):
        w = []
        for s, v in waits:
            if v <= 0 or self.hw[eng][s] >= v:
                continue
            self.hw[eng][s] = v
            w.append((s, v))
        self.ops_list(eng).append((fn, w, inc))
        if inc is not None:
            self.cnt[inc[0]] += inc[1]
        return self.cnt[inc[0]] if inc else None

    def ops_list(self, eng):
        if not hasattr(self, "_ops"):
            self._ops = {e: [] for e in ENGS}
        return self._ops[eng]

    def snap(self):
        return dict(self.cnt)

    def qbar(self, *qs):
        out = []
        for q in qs:
            for i in range(QDEPTH):
                s = f"{q}{i}"
                out.append((s, self.cnt[s]))
        return out

    def wsnap(self, sn, skip=()):
        return [(s, v) for s, v in sn.items() if s in self.names and s not in skip]

    def realize(self):
        with self.nc.Block() as blk:
            for name in ENGS:
                def mk(nm):
                    def body(e):
                        for fn, w, inc in self._ops[nm]:
                            for s, v in w:
                                e.wait_ge(self.sems[s], v)
                            inst = fn(e)
                            if inc is not None:
                                inst.then_inc(self.sems[inc[0]], inc[1])
                    return body
                getattr(blk, name)(mk(name))


def build_program(stop=None, dbg_keys=None):
    global DEBUG_OUTS
    if dbg_keys is not None:
        DEBUG_OUTS = tuple(dbg_keys)
    T = _tables()
    nc = bass.Bass("TRN2", target_bir_lowering=False)

    def din(name, shape, dt=F32):
        return nc.dram_tensor(name, shape, dt, kind="ExternalInput").ap()
    x4 = din("x4", [NSTEP, S, D])
    wqp = din("wqp", [D, D]); bqp = din("bqp", [128, 8])
    wkp = din("wkp", [D, D]); bkp = din("bkp", [128, 8])
    wvi = din("wvi", [D, 2 * D]); bvi = din("bvi", [2 * D])
    wgi = din("wgi", [D, 2 * D]); bgi = din("bgi", [2 * D])
    woi = din("woi", [2 * D, D]); boi = din("boi", [D])
    rwi = din("rwi", [D, 8])
    w1c = din("w1c", [D, 4096]); w2c = din("w2c", [4096, D])
    fbi = din("fbi", [D])
    onehot = din("onehot", [128, 8])
    ownrows = din("ownrows", [128, 64], I32)
    out_c = nc.dram_tensor("out_c", [NSTEP, S, D], F32, kind="ExternalOutput").ap()
    dbg = {}
    for k in DEBUG_OUTS:
        shp = {"xf": [NSTEP * S, D], "logits": [NSTEP, 128, 128],
               "islot": [2, GT * 128], "ycmp": [2 * CAP, D]}[k]
        dt = {"xf": F32, "logits": F32, "islot": I32, "ycmp": BF16}[k]
        dbg[k] = nc.dram_tensor("dbg_" + k, shp, dt, kind="ExternalOutput").ap()

    def cinl(key):
        return nc.inline_tensor(T[key], name="c_" + key).ap()
    cos4_d, sin4_d, gk_d = cinl("cos4"), cinl("sin4"), cinl("gkrep")
    call_d, tri_d, su_d = cinl("c_all"), cinl("tri"), cinl("su")
    id128_d, id8_d = cinl("id128b"), cinl("id8f")
    i128r_d, i8r_d, rii8_d = cinl("iota128r"), cinl("iota8r"), cinl("rii8r")
    ones1_d, onesc_d, idc_d = cinl("ones1x"), cinl("onescol"), cinl("idcols")
    iotac_d = cinl("iotacol")

    def dsc(name, shape, dt):
        return nc.dram_tensor(name, shape, dt, kind="Internal").ap()
    xbf_s = dsc("xbf_s", [2, S, D], BF16)
    v_s = dsc("v_s", [S, 2 * D], BF16)
    g_s = dsc("g_s", [S, 2 * D], BF16)
    gT_s = dsc("gT_s", [2 * D, S], BF16)
    xf32_s = dsc("xf32_s", [NSTEP, S, D], F32)
    xfb_s = dsc("xfb_s", [NSTEP * S, D], BF16)
    lg_s = dsc("lg_s", [NSTEP, 128, 128], F32)
    xfAG = dsc("xfAG", [NCORE * NSTEP * S, D], BF16)
    lgAG = dsc("lgAG", [NSTEP * NCORE * 128, 128], F32)
    cstage = dsc("cstage", [GT * 16], F32)
    bstage = dsc("bstage", [GT * 16], F32)
    ctflat = [dsc("ctflat0", [GT * 128, 1], I32), dsc("ctflat1", [GT * 128, 1], I32)]
    rowstage = dsc("rowstage", [4096], F32)
    yrw_s = dsc("yrw_s", [NCORE * NSTEP * S, 4], F32)
    comp_s = dsc("comp_s", [2 * CAP, D], BF16)
    ycmp_s = dsc("ycmp_s", [2 * CAP, D], BF16)
    yAG = dsc("yAG", [ZROW, D], BF16)

    A = nc.alloc_sbuf_tensor("BIGA", [128, 16384], BF16)
    Bg = nc.alloc_sbuf_tensor("BIGB", [128, 16384], BF16)
    C = nc.alloc_sbuf_tensor("BIGC", [128, 16384], BF16)
    Dg = nc.alloc_sbuf_tensor("BIGD", [128, 16384], BF16)
    E = nc.alloc_sbuf_tensor("BIGE", [128, 16384], BF16)
    Fg = nc.alloc_sbuf_tensor("BIGF", [128, 8192], BF16, is_transpose_dest=True)
    WK = nc.alloc_sbuf_tensor("WK", [128, 2048], F32)
    WK2 = nc.alloc_sbuf_tensor("WK2", [128, 2048], F32, is_transpose_dest=True)
    REPB = nc.alloc_sbuf_tensor("REPB", [128, 2048], BF16)
    REPF = nc.alloc_sbuf_tensor("REPF", [128, 1024], F32)
    SM = nc.alloc_sbuf_tensor("SM", [128, 512], F32)
    SMB = nc.alloc_sbuf_tensor("SMB", [128, 640], BF16)
    IDC = nc.alloc_sbuf_tensor("IDC", [128, 1024], BF16)
    ISL = nc.alloc_sbuf_tensor("ISL", [128, 140], I32)
    OWN = nc.alloc_sbuf_tensor("OWN", [128, 64], I32)
    ONESF = nc.alloc_sbuf_tensor("ONESF", [1, 128], F32)
    P = [nc.alloc_psum_tensor(f"P{i}", [128, 512], F32) for i in range(8)]
    assert nc.sbuf_bytes_remaining >= 0

    SM_call = SM[:, 0:256]
    SM_i8 = SM[:, 256:264]
    SM_rii8 = SM[:, 264:272]
    SM_id8 = SM[:, 272:280]
    SM_bq = SM[:, 280:288]
    SM_bk = SM[:, 288:296]
    SM_oh = SM[:, 296:304]
    SM_i128 = SM[:, 304:432]
    SMc = lambda i: SM[:, 432 + i:433 + i]
    SM_iotac = SM[:, 448:449]
    SM_base = [SM[:, 452 + 12 * k:456 + 12 * k] for k in range(2)]
    SM_basep1 = [SM[:, 456 + 12 * k:460 + 12 * k] for k in range(2)]
    SM_cnt = [SM[:, 460 + 12 * k:464 + 12 * k] for k in range(2)]
    SMB_tri = SMB[:, 0:128]
    SMB_su = SMB[:, 128:256]
    SMB_id = SMB[:, 256:384]
    SMB_ones1 = SMB[0:1, 384:512]
    SMB_onesc = SMB[:, 512:513]
    SMB_rw = SMB[:, 520:584].rearrange("p (dt e) -> p dt e", e=8)

    pg = Prog(nc)
    ZT = ("MM", 0)

    def dma(eng, q, out, in_, waits=(), **kw):
        s = f"{q}{pg.qidx[q] % QDEPTH}"
        pg.qidx[q] += 1
        def f(e, o=out, i=in_, k=dict(kw)):
            return e.dma_start(out=o, in_=i, **k)
        pg.emit(eng, f, waits=list(waits) + [(s, pg.cnt[s])], inc=(s, 16))
        return (s, pg.cnt[s])

    def mm(out, lhsT, rhs, start, stop, waits=()):
        def f(e, o=out, l=lhsT, r=rhs, st=start, sp=stop):
            return e.matmul(o, l, r, start=st, stop=sp)
        return pg.emit("tensor", f, waits=waits, inc=("MM", 1))

    def tr(out, in_, ident, waits=()):
        def f(e, o=out, i=in_, d=ident):
            return e.transpose(o, i, d)
        return pg.emit("tensor", f, waits=waits, inc=("MM", 1))

    def vop(fn, waits=()):
        return pg.emit("vector", fn, waits=list(waits) + [("VV", pg.cnt["VV"])],
                       inc=("VV", 1), force_self=True)

    def sop(fn, waits=()):
        return pg.emit("scalar", fn, waits=list(waits) + [("SS", pg.cnt["SS"])],
                       inc=("SS", 1), force_self=True)

    def gop(fn, waits=(), inc=None):
        if inc is not None:
            pg.emit("gpsimd", fn, waits=waits, inc=inc)
            return ("CC", pg.cnt["CC"])
        s = f"LG{pg.qidx['LG'] % QDEPTH}"
        pg.qidx["LG"] += 1
        pg.emit("gpsimd", fn, waits=list(waits) + [(s, pg.cnt[s])], inc=(s, 16))
        return (s, pg.cnt[s])

    dma("sync", "LY", SM_call, call_d)
    dma("sync", "LY", SM_i8, i8r_d)
    dma("sync", "LY", SM_rii8, rii8_d)
    dma("sync", "LY", SM_id8[0:8, :], id8_d)
    dma("sync", "LY", SM_bq, bqp)
    dma("sync", "LY", SM_bk, bkp)
    dma("sync", "LY", SM_oh, onehot)
    dma("sync", "LY", SM_i128, i128r_d)
    dma("sync", "LY", SM_iotac, iotac_d)
    dma("sync", "LY", SMB_tri, tri_d)
    dma("sync", "LY", SMB_su, su_d)
    dma("sync", "LY", SMB_id, id128_d)
    dma("sync", "LY", SMB_ones1, ones1_d)
    dma("sync", "LY", SMB_onesc, onesc_d)
    dma("sync", "LY", IDC[:].rearrange("p (t two) -> p t two", two=2), idc_d)
    dma("sync", "LY", OWN[:], ownrows)
    lrw = dma("gpsimd", "LG", SMB_rw, rwi.rearrange("(dt p) e -> p dt e", p=128))
    zv = vop(lambda e: e.memset(WK[:], 0.0))
    vop(lambda e: e.memset(ONESF[:], 1.0))
    lbo = dma("sync", "LY", WK2[0:1, 0:1024], boi.rearrange("(a d) -> a d", a=1))
    ov1 = vop(lambda e: e.memset(WK2[0:1, 1024:1152], 1.0))
    rv0 = 0
    for sl in range(2):
        m = mm(P[6][:, 0:512], WK2[0:1, 1024:1152], WK2[0:1, sl * 512:(sl + 1) * 512],
               True, True, waits=[lbo, ("VV", ov1), ("VV", rv0)])
        rv0 = vop(lambda e, sl=sl: e.tensor_copy(REPF[:, sl * 512:(sl + 1) * 512], P[6][:, 0:512]),
                  waits=[("MM", m)])
    init_snap = pg.snap()

    xT = A[:].rearrange("p (dt s) -> p dt s", dt=8)
    qrT = Bg[:].rearrange("p (dt s) -> p dt s", dt=8)
    krT = C[:].rearrange("p (dt s) -> p dt s", dt=8)
    W2b = WK2[:].bitcast(BF16)
    qtmp = W2b[:, 0:2048]
    qsw = W2b[:, 2048:4096]
    F_gk = Fg[:, 0:2048]
    F_t2 = Fg[:, 2048:4096]
    F_cos = Fg[:, 4096:6144]
    F_sin = Fg[:, 6144:8192]
    wq_sb = E[:, 0:8192].rearrange("p (dt o) -> p dt o", dt=8)
    wk_sb = E[:, 8192:16384].rearrange("p (dt o) -> p dt o", dt=8)
    wvg_sb = E[:].rearrange("p (dt o) -> p dt o", dt=8)
    E_wo = E[:].rearrange("p (g d) -> p g d", g=16)
    D_xfb = Dg[:, 8192:9216]
    D_lgf = Dg[:, 9216:9472].bitcast(F32)
    D_xfT = [Bg[:, 12288 + i * 2048: 12288 + (i + 1) * 2048] for i in range(2)]

    prev_step = init_snap
    lcos = dma("sync", "LY", F_cos, cos4_d, waits=pg.wsnap(init_snap))
    lsin = dma("sync", "LY", F_sin, sin4_d)
    pre_x, pre_w = {}, {}
    xt_prev = []
    ov = 0
    tr_last = 0
    lg_marks = []
    ag_n = 0
    _sub = stop is not None and (stop in ('a0', 'b0', 'c0', 'd0', 'ef0', 'dskip') or stop.startswith('dh') or stop.startswith('dm') or stop in ('dep', 'dld', 'dsa', 'dva'))
    nstep_run = 0 if stop == 'p0' else (1 if _sub else NSTEP)
    for b in range(nstep_run):
        if b in pre_x:
            lgc, xt_l = pre_x.pop(b)
        else:
            lgc = dma("gpsimd", "LG", xbf_s[b % 2], x4[b], waits=pg.wsnap(prev_step, ("CC",)))
            xt_l = [dma("scalar", "LS", xT[:, j, :], xbf_s[b % 2][:, j * 128:(j + 1) * 128],
                        waits=[lgc], transpose=True) for j in range(8)]
        if b in pre_w:
            lq, lk = pre_w.pop(b)
        else:
            lq = dma("gpsimd", "LG", wq_sb, wqp.rearrange("(dt p) o -> p dt o", p=128),
                     waits=pg.wsnap(prev_step, ("CC",)))
            lk = dma("gpsimd", "LG", wk_sb, wkp.rearrange("(dt p) o -> p dt o", p=128))

        if stop == 'a0':
            continue
        pa_free_s = [0, 0]
        last_rot = 0
        for which in range(2):
            wsb, wld = (wq_sb, lq) if which == 0 else (wk_sb, lk)
            bcol = SM_bq if which == 0 else SM_bk
            dstT = qrT if which == 0 else krT
            for j in range(8):
                if which == 1:
                    lgk = dma("gpsimd", "LG", F_gk, gk_d[j],
                              waits=[("VV", last_rot)] +
                                    (pg.wsnap(prev_step, ("CC",)) if j == 0 else []))
                for sl in range(4):
                    ps = P[sl % 2]
                    for dt in range(8):
                        w = ([wld, ("VV", max(pa_free_s[sl % 2], last_rot))]
                             if dt == 0 else []) + [xt_l[dt]]
                        m = mm(ps[:, 0:512], wsb[:, dt, j * 128:(j + 1) * 128],
                               xT[:, dt, sl * 512:(sl + 1) * 512], dt == 0, dt == 7, waits=w)
                    pa_free_s[sl % 2] = vop(
                        lambda e, ps=ps, sl=sl, bc=bcol, j=j: e.tensor_scalar(
                            qtmp[:, sl * 512:(sl + 1) * 512], ps[:, 0:512],
                            bc[:, j:j + 1], None, op0=ALU.add),
                        waits=[("MM", m), ("VV", last_rot)])
                sse = max(pa_free_s)
                if which == 1:
                    gv = vop(lambda e: e.tensor_mul(qtmp[:], qtmp[:], F_gk[:]),
                             waits=[("VV", sse), lgk])
                else:
                    gv = 0
                w0 = [("VV", max(sse, gv)), lsin, lcos] + \
                     (pg.wsnap(prev_step, ("CC",)) if (which == 0 and j == 0) else [])
                vop(lambda e: e.tensor_copy(qsw[0:32, :], qtmp[32:64, :]), waits=w0)
                vop(lambda e: e.tensor_copy(qsw[32:64, :], qtmp[0:32, :]))
                vop(lambda e: e.tensor_copy(qsw[64:96, :], qtmp[96:128, :]))
                c4 = vop(lambda e: e.tensor_copy(qsw[96:128, :], qtmp[64:96, :]))
                vop(lambda e, dT=dstT, j=j: e.tensor_mul(dT[:, j, :], qtmp[:], F_cos[:]))
                t2 = vop(lambda e: e.tensor_mul(F_t2[:], qsw[:], F_sin[:]), waits=[("VV", c4)])
                last_rot = vop(lambda e, dT=dstT, j=j: e.tensor_add(dT[:, j, :], dT[:, j, :], F_t2[:]),
                               waits=[("VV", t2)])
        rot_done = pg.snap()

        if stop == 'b0':
            continue
        vg_last_store = ZT
        for which in range(2):
            wten, brow_d = (wvi, bvi) if which == 0 else (wgi, bgi)
            dst_d = v_s if which == 0 else g_s
            lw = dma("gpsimd", "LG", wvg_sb, wten.rearrange("(dt p) o -> p dt o", p=128),
                     waits=(pg.wsnap(rot_done) if which == 0 else
                            [("VV", pg.cnt["VV"]), ("MM", pg.cnt["MM"])]))
            lbr = dma("gpsimd", "LG", REPB[0:1, :], brow_d.rearrange("(a d) -> a d", a=1),
                      waits=pg.wsnap(rot_done))
            repv = 0
            for sl in range(4):
                m = mm(P[2][:, 0:512], SMB_ones1,
                       REPB[0:1, sl * 512:(sl + 1) * 512], True, True,
                       waits=[lbr, ("VV", repv)])
                repv = vop(lambda e, sl=sl: e.tensor_copy(REPB[:, sl * 512:(sl + 1) * 512], P[2][:, 0:512]),
                           waits=[("MM", m)])
            pv_free = [repv, repv]
            for tt in range(NT):
                ev = 0
                for dvs in range(4):
                    ps = P[dvs % 2]
                    for dt in range(8):
                        w = ([lw, ("VV", pv_free[dvs % 2])] if dt == 0 else []) + [xt_l[dt]]
                        m = mm(ps[:, 0:512], xT[:, dt, tt * 128:(tt + 1) * 128],
                               wvg_sb[:, dt, dvs * 512:(dvs + 1) * 512], dt == 0, dt == 7, waits=w)
                    ev = vop(lambda e, ps=ps, dvs=dvs: e.tensor_add(
                        qtmp[:, dvs * 512:(dvs + 1) * 512], ps[:, 0:512],
                        REPB[:, dvs * 512:(dvs + 1) * 512]),
                        waits=[("MM", m), vg_last_store])
                    pv_free[dvs % 2] = ev
                vg_last_store = dma("sync", "LY", dst_d[tt * 128:(tt + 1) * 128, :], qtmp[:],
                                    waits=[("VV", ev)])

        if stop == 'c0':
            continue
        vg_bar = pg.qbar("LY")
        lwo = dma("gpsimd", "LG", E_wo, woi.rearrange("(g p) d -> p g d", p=128),
                  waits=[("MM", pg.cnt["MM"])])
        if b + 1 < nstep_run:
            nb = b + 1
            plgc = dma("gpsimd", "LG", xbf_s[nb % 2], x4[nb], waits=list(xt_prev))
            pxt = [dma("scalar", "LS", xT[:, j, :], xbf_s[nb % 2][:, j * 128:(j + 1) * 128],
                       waits=[plgc, ("MM", pg.cnt["MM"])], transpose=True) for j in range(8)]
            pre_x[nb] = (plgc, pxt)
        xt_prev = xt_l
        D_vre4 = [Dg[:, hr * 2112:(hr + 1) * 2112].rearrange("p (tt c) -> p tt c", c=132)
                  for hr in range(4)]
        D_gh2 = [Dg[:, 8448 + i * 2048:8448 + (i + 1) * 2048].rearrange("p (q d) -> p q d", d=512)
                 for i in range(2)]
        D_owb = Dg[:, 12544:14592].rearrange("p (s0 d) -> p s0 d", d=128)
        D_st = [Dg[:, 14592 + i * 512:14592 + (i + 1) * 512] for i in range(3)]
        D_gacc = [Fg[:, i * 2048:(i + 1) * 2048] for i in range(2)]
        SG_all = W2b[:, 2048:4096]
        OrawR = WK[:, 384:900].rearrange("p (rs c) -> p rs c", c=129)
        R4 = WK[:, 96:100]
        gat_all = W2b[:, 0:2048]
        if b == 0:
            ov = vop(lambda e: e.memset(
                Dg[:, 0:8448].rearrange("p (x c) -> p x c", c=132)[:, :, 128:129], 1.0),
                waits=pg.wsnap(prev_step, ("CC",)))
        MM1P = (0, 1, 7)
        st_mm_free = [0, 0, 0]
        pa_s = [0, 0, 0]
        pa_v = [pg.cnt["VV"]] * 3
        slot_i = 0
        po_ep = [0, 0, 0, 0]
        vre_mm4 = [0, 0, 0, 0]
        gh_ss = [0, 0]
        gacc_fl = [[], []]
        pmt4 = [0, 0, 0, 0]
        vg2_4 = [0, 0, 0, 0]
        ghp = 0
        _dH = 0 if stop == 'dskip' else H
        if stop is not None and stop.startswith("dh"):
            _dH = int(stop[2:])
        tail_q = []
        for hq in range(0, H, 4):
            if hq >= _dH:
                break
            lvh = []
            for hr in range(4):
                h = hq + hr
                lvh.append(dma("sync", "LY", D_vre4[hr][:, :, 0:128],
                               v_s.rearrange("(tt p) o -> p tt o", p=128)[:, :, h * 128:(h + 1) * 128],
                               waits=(vg_bar if hq == 0 else []) + [("VV", ov),
                                     ("MM", pg.cnt["MM"] if hq == 0 else vre_mm4[hr])]))
            for g in range(4):
                s_lo = g * 4
                pp = ghp % 2
                lgh = dma("sync", "LY", D_gh2[pp][:, :, :],
                          g_s.rearrange("(tt p) o -> p tt o", p=128)[:, 4 * g:4 * g + 4,
                                                                    hq * 128:hq * 128 + 512],
                          waits=(vg_bar if ghp < 2 else []) + [("SS", gh_ss[pp])])
                ghp += 1

                def emit_mm2(ent, s_lo=s_lo):
                    sl3, t0, hr2, sv, vv = ent
                    lastm = 0
                    for s0 in range(max(s_lo, t0), s_lo + 4):
                        rs = s0 - s_lo
                        lastm = mm(P[2 + rs][:, 0:129],
                                   D_st[sl3][:, rs * 128:(rs + 1) * 128],
                                   D_vre4[hr2][:, t0, 0:129], t0 == 0, s0 == t0,
                                   waits=[("SS", sv),
                                          ("VV", max(vv, po_ep[rs] if t0 == 0 else 0)),
                                          lvh[hr2]])
                    st_mm_free[sl3] = lastm
                    vre_mm4[hr2] = max(vre_mm4[hr2], lastm)

                for hr in range(4):
                    h = hq + hr
                    if h >= _dH:
                        continue
                    j, half = divmod(h, 2)
                    pb = half * 64
                    pend = []
                    for t0 in range(s_lo + 4):
                        sl3 = slot_i % 3
                        slot_i += 1
                        ps = P[MM1P[sl3]]
                        m = mm(ps[:, 0:512], krT[pb:pb + 64, j, t0 * 128:(t0 + 1) * 128],
                               qrT[pb:pb + 64, j, g * 512:(g + 1) * 512], True, True,
                               waits=[("SS", pa_s[sl3]), ("VV", pa_v[sl3])])
                        rel = t0 - s_lo
                        sv = vv = 0
                        if rel < 0:
                            sv = sop(lambda e, ps=ps, sl=sl3: e.activation(
                                D_st[sl][:, 0:512], ps[:, 0:512], AF.Copy),
                                waits=[("MM", max(m, st_mm_free[sl3]))])
                        else:
                            c0 = rel * 128
                            vv = vop(lambda e, ps=ps, sl=sl3, c0=c0: e.tensor_mul(
                                D_st[sl][:, c0:c0 + 128], ps[:, c0:c0 + 128], SMB_tri),
                                waits=[("MM", max(m, st_mm_free[sl3]))])
                            if c0 + 128 < 512:
                                vv = vop(lambda e, ps=ps, sl=sl3, c0=c0: e.tensor_copy(
                                    D_st[sl][:, c0 + 128:512], ps[:, c0 + 128:512]))
                        pa_s[sl3] = max(pa_s[sl3], sv)
                        pa_v[sl3] = max(pa_v[sl3], vv)
                        pend.append((sl3, t0, hr, sv, vv))
                        if tail_q:
                            tail_q.pop(0)()
                        if len(pend) > 2:
                            emit_mm2(pend.pop(0))
                    while pend:
                        emit_mm2(pend.pop(0))
                    if tail_q:
                        tail_q.pop(0)()
                    mfin = pg.cnt["MM"]
                    for rs in range(4):
                        po_ep[rs] = vop(lambda e, rs=rs: e.tensor_copy(
                            OrawR[:, rs, :], P[2 + rs][:, 0:129]),
                            waits=[("MM", mfin)] if rs == 0 else [])
                    csl4 = SM_call[:, h * 16 + s_lo: h * 16 + s_lo + 4]
                    tc, tn, wc = WK[:, 104:108], WK[:, 108:112], WK[:, 100:104]
                    vop(lambda e: e.tensor_copy(
                        R4, OrawR[:, :, 128:129].rearrange("p a c -> p (a c)")))
                    vop(lambda e, csl4=csl4: e.tensor_tensor(tc, csl4, R4, op=ALU.mult))
                    vop(lambda e: e.tensor_scalar(tn, tc, -1.0, None, op0=ALU.mult))
                    vop(lambda e: e.tensor_tensor(tc, tc, tn, op=ALU.max))
                    vop(lambda e: e.tensor_scalar(tc, tc, 1.0, None, op0=ALU.max))
                    vop(lambda e: e.reciprocal(tc, tc))
                    vop(lambda e, csl4=csl4: e.tensor_mul(wc, csl4, tc))
                    for rs in range(4):
                        idx = hr * 4 + rs
                        mus = WK[:, idx:idx + 1]
                        sqs = WK[:, 16 + idx:17 + idx]
                        vop(lambda e, rs=rs, idx=idx, mus=mus: e.tensor_scalar(
                            D_owb[:, idx, :], OrawR[:, rs, 0:128], wc[:, rs:rs + 1], 0.0,
                            op0=ALU.mult, op1=ALU.add, accum_out=mus))
                        vop(lambda e, idx=idx, sqs=sqs: e.scalar_tensor_tensor(
                            WK[:, 128:256], D_owb[:, idx, :], 1.0, D_owb[:, idx, :],
                            op0=ALU.mult, op1=ALU.mult, accum_out=sqs))
                nm16, var16, rstd16, e16 = WK[:, 32:48], WK[:, 48:64], WK[:, 64:80], WK[:, 80:96]
                vop(lambda e: e.tensor_scalar(nm16, WK[:, 0:16], -1.0 / 128.0, None, op0=ALU.mult))
                vop(lambda e: e.tensor_tensor(e16, WK[:, 0:16], WK[:, 0:16], op=ALU.mult))
                vop(lambda e: e.tensor_scalar(e16, e16, -1.0 / 16384.0, None, op0=ALU.mult))
                vop(lambda e: e.scalar_tensor_tensor(
                    var16, WK[:, 16:32], 1.0 / 128.0, e16, op0=ALU.mult, op1=ALU.add))
                vv3 = vop(lambda e: e.tensor_scalar(var16, var16, 1e-5, None, op0=ALU.add))
                s2 = sop(lambda e: e.activation(rstd16, var16, AF.Ln), waits=[("VV", vv3)])
                s3 = sop(lambda e: e.activation(rstd16, rstd16, AF.Exp, scale=-0.5))
                gh_ss[pp] = s4 = sop(lambda e, pp=pp: e.activation(
                    SG_all, D_gh2[pp][:, :, :].rearrange("p q d -> p (q d)"), AF.Silu),
                    waits=[lgh, ("VV", pg.cnt["VV"])])
                pp2 = (ghp - 1) % 2
                first_p2 = True
                ucell = {"first": True}
                for hr in range(4):
                    h = hq + hr
                    if h >= _dH:
                        continue
                    for rs in range(4):
                        idx = hr * 4 + rs
                        ctr = WK[:, 256:384]
                        vop(lambda e, idx=idx: e.tensor_scalar(
                            ctr, D_owb[:, idx, :], nm16[:, idx:idx + 1], None, op0=ALU.add),
                            waits=[("SS", s3)] if first_p2 else [])
                        v13 = vop(lambda e, idx=idx, rs=rs, hr=hr: e.scalar_tensor_tensor(
                            gat_all[:, idx * 128:(idx + 1) * 128], ctr, rstd16[:, idx:idx + 1],
                            SG_all[:, rs * 512 + hr * 128: rs * 512 + (hr + 1) * 128],
                            op0=ALU.mult, op1=ALU.mult),
                            waits=[("SS", s4)] + ([("MM", tr_last)] if first_p2 else []))
                        first_p2 = False

                        def t_go(idx=idx, hr=hr, rs=rs, pp2=pp2, v13=v13, ucell=ucell):
                            nonlocal tr_last
                            sl4 = idx % 4
                            pt = P[6][:, sl4 * 64:(sl4 + 1) * 64].bitcast(BF16)
                            pmt4[sl4] = tr_last = tr(
                                pt, gat_all[:, idx * 128:(idx + 1) * 128], SMB_id,
                                waits=[("VV", v13), ("VV", max(vg2_4))])
                            wfl = []
                            if ucell["first"]:
                                ucell["first"] = False
                                wfl = gacc_fl[pp2]
                                gacc_fl[pp2] = []
                            vg2_4[sl4] = vop(lambda e, pt=pt, hr=hr, rs=rs, pp2=pp2: e.tensor_copy(
                                D_gacc[pp2][:, hr * 512 + rs * 128: hr * 512 + (rs + 1) * 128], pt),
                                waits=[("MM", pmt4[sl4])] + wfl)
                        tail_q.append(t_go)

                    def f_go(h=h, g=g, hr=hr, pp2=pp2):
                        gacc_fl[pp2].append(dma(
                            "sync", "LY", gT_s[h * 128:(h + 1) * 128, g * 512:(g + 1) * 512],
                            D_gacc[pp2][:, hr * 512:(hr + 1) * 512],
                            waits=[("VV", vg2_4[(hr * 4 + 3) % 4])]))
                    tail_q.append(f_go)

        while tail_q:
            tail_q.pop(0)()
        d_done = pg.snap()

        if stop == 'd0':
            continue
        D_gstat = [Dg[:, i * 2048:(i + 1) * 2048].rearrange("p (g s) -> p g s", s=128)
                   for i in range(2)]
        gst_mm = [0, 0]
        pc_v = 0
        xf_l2 = ZT
        prev_l1 = ZT
        prev_l2 = ZT
        for tt in range(NT):
            pp = tt % 2
            lg2 = dma("sync", "LY", D_gstat[pp],
                      gT_s[:, tt * 128:(tt + 1) * 128].rearrange("(g p) s -> p g s", p=128),
                      waits=[("MM", gst_mm[pp])] + pg.wsnap(d_done))
            lx = dma("sync", "LY", WK[:, 1024:2048], x4[b][tt * 128:(tt + 1) * 128, :])
            mlast = 0
            for dgt in range(16):
                for h2 in range(2):
                    w = ([lg2, lwo, ("VV", pc_v)] + pg.wsnap(d_done)) if dgt == 0 else ()
                    mlast = mm(P[4 + h2][:, 0:512], D_gstat[pp][:, dgt, :],
                               E_wo[:, dgt, h2 * 512:(h2 + 1) * 512], dgt == 0, dgt == 15,
                               waits=w)
            gst_mm[pp] = mlast
            rbuf = WK[:, 0:1024]
            vop(lambda e: e.tensor_add(rbuf[:, 0:512], P[4][:, 0:512], REPF[:, 0:512]),
                waits=[("MM", mlast), prev_l1] + pg.wsnap(d_done))
            pc_v = vop(lambda e: e.tensor_add(rbuf[:, 512:1024], P[5][:, 0:512], REPF[:, 512:1024]))
            e3 = vop(lambda e: e.tensor_add(rbuf, rbuf, WK[:, 1024:2048]), waits=[lx])
            nm, var, rstd, sqs = SMc(7), SMc(8), SMc(9), SMc(10)
            vop(lambda e, nm=nm: e.reduce_sum(nm, rbuf, axis=AX.X, negate=True), waits=[("VV", e3)])
            vop(lambda e, nm=nm: e.tensor_scalar(nm, nm, 1.0 / 1024.0, None, op0=ALU.mult))
            v3 = vop(lambda e, nm=nm: e.tensor_scalar(rbuf, rbuf, nm, None, op0=ALU.add))
            s1 = vop(lambda e, sqs=sqs: e.scalar_tensor_tensor(
                WK[:, 1024:2048], rbuf, 1.0, rbuf, op0=ALU.mult, op1=ALU.mult, accum_out=sqs),
                waits=[("VV", v3)])
            v4 = vop(lambda e, var=var, sqs=sqs: e.tensor_scalar(var, sqs, 1.0 / 1024.0, 1e-5,
                                                                 op0=ALU.mult, op1=ALU.add))
            s2 = sop(lambda e, rstd=rstd, var=var: e.activation(rstd, var, AF.Ln), waits=[("VV", v4)])
            s3 = sop(lambda e, rstd=rstd: e.activation(rstd, rstd, AF.Exp, scale=-0.5))
            v5 = vop(lambda e, rstd=rstd: e.tensor_scalar(rbuf, rbuf, rstd, None, op0=ALU.mult),
                     waits=[("SS", s3)])
            v6 = vop(lambda e: e.tensor_copy(D_xfb[:], rbuf), waits=[prev_l2])
            prev_l1 = dma("sync", "LY", xf32_s[b][tt * 128:(tt + 1) * 128, :], rbuf,
                          waits=[("VV", v5)])
            prev_l2 = xf_l2 = dma("sync", "LY",
                                  xfb_s[b * S + tt * 128: b * S + (tt + 1) * 128, :], D_xfb,
                                  waits=[("VV", v6)])
        if b + 1 < nstep_run:
            pre_w[b + 1] = (
                dma("gpsimd", "LG", wq_sb, wqp.rearrange("(dt p) o -> p dt o", p=128),
                    waits=[("MM", max(gst_mm))]),
                dma("gpsimd", "LG", wk_sb, wkp.rearrange("(dt p) o -> p dt o", p=128)))
        xfb_bar = pg.qbar("LY")
        xfT_mm = [0, 0]
        mm_d = pg.cnt["MM"]
        for dt in range(8):
            pp = dt % 2
            lt = dma("scalar", "LS", D_xfT[pp], xfb_s[b * S:(b + 1) * S, dt * 128:(dt + 1) * 128],
                     waits=xfb_bar + [("MM", xfT_mm[pp] if dt >= 2 else mm_d)], transpose=True)
            for sl in range(4):
                xfT_mm[pp] = mm(P[sl][0:8, 0:512], SMB_rw[:, dt, :],
                                D_xfT[pp][:, sl * 512:(sl + 1) * 512], dt == 0, dt == 7,
                                waits=[lt, lrw])
        mrt = pg.cnt["MM"]
        lgTrow = WK[0:8, 0:2048]
        lgv = 0
        for sl in range(4):
            lgv = vop(lambda e, sl=sl: e.tensor_copy(lgTrow[:, sl * 512:(sl + 1) * 512],
                                                     P[sl][0:8, 0:512]),
                      waits=[("MM", mrt)])
        lgsb = D_lgf[:, 0:128].rearrange("p (tt e) -> p tt e", e=8)
        lgev = 0
        for tt in range(NT):
            mt = tr(P[6][:, 0:8], lgTrow[:, tt * 128:(tt + 1) * 128], SM_id8[0:8, :],
                    waits=[("VV", lgv), ("VV", lgev)])
            lgev = vop(lambda e, tt=tt: e.tensor_copy(lgsb[:, tt, :], P[6][:, 0:8]),
                       waits=[("MM", mt)])
        l3 = dma("sync", "LY", lg_s[b], lgsb.rearrange("p tt e -> p (tt e)"), waits=[("VV", lgev)])
        if stop == 'ef0':
            continue
        gop(lambda e, b=b: e.collective_compute(
            "AllGather", ALU.bypass, replica_groups=[list(range(NCORE))],
            ins=[lg_s[b].opt()],
            outs=[lgAG[b * NCORE * 128:(b + 1) * NCORE * 128, :].opt()]),
            waits=[l3], inc=("CC", 1))
        gop(lambda e, b=b: e.collective_compute(
            "AllGather", ALU.bypass, replica_groups=[list(range(NCORE))],
            ins=[xfb_s[b * S:(b + 1) * S, :].opt()],
            outs=[xfAG[b * NCORE * S:(b + 1) * NCORE * S, :].opt()]),
            waits=xfb_bar, inc=("CC", 1))
        ag_n += 2
        prev_step = pg.snap()

    ag_attn = ag_n
    attn_snap = pg.snap()

    def _finish_early():
        sl_ = dma("sync", "LY", out_c[0][0:128, :], REPF[:], waits=pg.wsnap(pg.snap()))
        pg.emit("sync", lambda e: e.nop(), waits=[sl_])
        pg.realize()
        return nc

    if stop is not None and stop != "attn" and (
            stop in ("p0", "attn0", "a0", "b0", "c0", "d0", "ef0")
            or stop.startswith("dh") or stop.startswith("dm") or stop in ("dep", "dskip", "dld", "dsa", "dva")):
        return _finish_early()
    if "xf" in dbg:
        dma("sync", "LY", dbg["xf"], xf32_s.rearrange("b s d -> (b s) d"), waits=pg.wsnap(attn_snap))
    if "logits" in dbg:
        dma("sync", "LY", dbg["logits"].rearrange("b p x -> (b p) x"), lg_s.rearrange("b p x -> (b p) x"), waits=pg.wsnap(attn_snap))
    if stop == "attn":
        return _finish_early()

    L = C[:, 0:8192].bitcast(F32).rearrange("p (t e) -> p t e", e=8)
    ind1 = C[:, 8192:12288].rearrange("p (t e) -> p t e", e=8)
    ind2 = C[:, 12288:16384].rearrange("p (t e) -> p t e", e=8)
    lL_toks = []
    for bb in range(NSTEP):
        for rr in range(NCORE):
            lL_toks.append(dma("sync", "LY", L[:, (bb * 8 + rr) * 16:(bb * 8 + rr + 1) * 16, :],
                     lgAG[(bb * 8 + rr) * 128:(bb * 8 + rr + 1) * 128, :].rearrange(
                         "p (tw e) -> p tw e", e=8),
                     waits=([("CC", max(ag_attn - 1, 0))] + pg.wsnap(attn_snap, ("CC",))) if (bb == 0 and rr == 0) else ()))
    Dx = Dg[:].bitcast(F32)
    m1, m2 = Dx[:, 0:512], Dx[:, 512:1024]
    i1f, i2f = Dx[:, 1024:1536], Dx[:, 1536:2048]
    wt1, wt2 = Dx[:, 2048:2560], Dx[:, 2560:3072]
    scr = Dx[:, 3072:7168]
    scr2v = scr.rearrange("p (t e) -> p t e", e=8)
    se = Dx[:, 7168:7680]
    bc = lambda ap: ap.rearrange("p (t a) -> p t a", a=1).broadcast_to([128, 512, 8])
    i8bc = SM_i8.rearrange("p (a e) -> p a e", a=1).broadcast_to([128, 512, 8])
    rii8bc = SM_rii8.rearrange("p (a e) -> p a e", a=1).broadcast_to([128, 512, 8])
    ohbc = SM_oh.rearrange("p (a e) -> p a e", a=1).broadcast_to([128, 512, 8])
    vop(lambda e: e.reduce_max(m1, L, axis=AX.X), waits=lL_toks + pg.wsnap(attn_snap, ("CC",)))
    vop(lambda e: e.tensor_tensor(scr2v, L, bc(m1), op=ALU.is_equal))
    vop(lambda e: e.tensor_mul(scr2v, scr2v, rii8bc))
    vop(lambda e: e.reduce_max(i1f, scr2v, axis=AX.X))
    vop(lambda e: e.tensor_tensor(scr2v, L, bc(m1), op=ALU.is_equal))
    vop(lambda e: e.tensor_scalar(scr2v, scr2v, 1e30, None, op0=ALU.mult))
    vop(lambda e: e.tensor_tensor(scr2v, L, scr2v, op=ALU.subtract))
    vop(lambda e: e.reduce_max(m2, scr2v, axis=AX.X))
    vop(lambda e: e.tensor_tensor(scr2v, scr2v, bc(m2), op=ALU.is_equal))
    vop(lambda e: e.tensor_mul(scr2v, scr2v, rii8bc))
    vop(lambda e: e.reduce_max(i2f, scr2v, axis=AX.X))
    vop(lambda e: e.tensor_scalar(i1f, i1f, -1.0, 99.0, op0=ALU.mult, op1=ALU.add))
    vop(lambda e: e.tensor_scalar(i2f, i2f, -1.0, 99.0, op0=ALU.mult, op1=ALU.add))
    v14 = vop(lambda e: e.tensor_tensor(scr2v, L, bc(m1), op=ALU.subtract))
    s1 = sop(lambda e: e.activation(scr, scr, AF.Exp), waits=[("VV", v14)])
    vop(lambda e: e.reduce_sum(se, scr2v, axis=AX.X), waits=[("SS", s1)])
    vop(lambda e: e.reciprocal(wt1, se))
    v17 = vop(lambda e: e.tensor_tensor(wt2, m2, m1, op=ALU.subtract))
    s2 = sop(lambda e: e.activation(wt2, wt2, AF.Exp), waits=[("VV", v17)])
    vop(lambda e: e.tensor_mul(wt2, wt2, wt1), waits=[("SS", s2)])
    vop(lambda e: e.tensor_tensor(ind1, bc(i1f), i8bc, op=ALU.is_equal))
    ind_done = vop(lambda e: e.tensor_tensor(ind2, bc(i2f), i8bc, op=ALU.is_equal))

    ranks = A[:].bitcast(F32).rearrange("p (t e) -> p t e", e=16)
    rk_ev = 0
    cs_ev = 0
    cs_toks = []
    for Gi in range(4):
        mmax = 0
        for trel in range(128):
            tile = Gi * 128 + trel
            bank = trel // 32
            col = (trel % 32) * 16
            for kk, indv in enumerate((ind1, ind2)):
                w = ([("VV", ind_done), ("VV", rk_ev)] + pg.wsnap(attn_snap, ("CC",))) if (trel == 0 and kk == 0) else ()
                mmax = mm(P[bank][:, col + kk * 8: col + kk * 8 + 8], SMB_su,
                          indv[:, tile, :], True, True, waits=w)
        for q4 in range(4):
            rk_ev = vop(lambda e, Gi=Gi, q4=q4: e.tensor_copy(
                ranks[:, Gi * 128 + q4 * 32:Gi * 128 + (q4 + 1) * 32, :],
                P[q4][:, 0:512].rearrange("p (t e) -> p t e", e=16)),
                waits=[("MM", mmax)])
        for h16 in range(4):
            mc = 0
            for tr2 in range(32):
                tile = Gi * 128 + h16 * 32 + tr2
                for kk, indv in enumerate((ind1, ind2)):
                    w = [("VV", cs_ev)] if (tr2 == 0 and kk == 0) else ()
                    mc = mm(P[4][0:1, tr2 * 16 + kk * 8: tr2 * 16 + kk * 8 + 8],
                            SMB_onesc, indv[:, tile, :], True, True, waits=w)
            cs_ev = vop(lambda e, h16=h16: e.tensor_copy(WK[0:1, h16 * 512:(h16 + 1) * 512],
                                                         P[4][0:1, 0:512]),
                        waits=[("MM", mc), (cs_toks[-1] if cs_toks else ("MM", 0))])
        cs_toks.append(dma("sync", "LY", cstage[Gi * 2048:(Gi + 1) * 2048].rearrange("(a b) -> a b", a=1),
                           WK[0:1, 0:2048], waits=[("VV", cs_ev)]))
    cs2 = REPB[:, 0:64]
    lc2 = dma("gpsimd", "LG", cs2.rearrange("p (t1 ke) -> p t1 ke", ke=16),
              cstage.rearrange("(t1 p ke) -> p t1 ke", p=128, ke=16),
              waits=cs_toks + pg.wsnap(attn_snap, ("CC",)))
    mb = mm(P[5][:, 0:64], SMB_su, cs2, True, False, waits=[lc2])
    mtb = mm(P[4][0:1, 0:64], SMB_onesc, cs2, True, True)
    tb, bp = WK2[0:1, 64:128], WK2[0:1, 128:192]
    vtb = vop(lambda e: e.tensor_copy(tb, P[4][0:1, 0:64]), waits=[("MM", mtb)])
    vop(lambda e: e.memset(bp[:, 0:16], 0.0))
    vop(lambda e: e.tensor_copy(bp[:, 16:32], tb[:, 0:16]), waits=[("VV", vtb)])
    vop(lambda e: e.tensor_add(bp[:, 32:48], bp[:, 16:32], tb[:, 16:32]))
    vb3 = vop(lambda e: e.tensor_add(bp[:, 48:64], bp[:, 32:48], tb[:, 32:48]))
    mb2 = mm(P[5][:, 0:64], ONESF[:], bp, False, True, waits=[("VV", vb3)])
    vbase = vop(lambda e: e.tensor_copy(WK2[:, 192:256], P[5][:, 0:64]), waits=[("MM", mb2)])
    lbs = dma("sync", "LY", bstage.rearrange("(t1 p ke) -> p t1 ke", p=128, ke=16),
              WK2[:, 192:256].rearrange("p (t1 ke) -> p t1 ke", ke=16), waits=[("VV", vbase)])
    radd_ev = 0
    prev_lbr2 = ("MM", 0)
    for Gi in range(4):
        lbr2 = dma("sync", "LY", WK2[0:1, 0:2048],
                   bstage[Gi * 2048:(Gi + 1) * 2048].rearrange("(a b) -> a b", a=1),
                   waits=[lbs, prev_lbr2, ("MM", pg.cnt["MM"])])
        prev_lbr2 = lbr2
        for q4 in range(4):
            mq = mm(P[q4][:, 0:512], ONESF[:], WK2[0:1, q4 * 512:(q4 + 1) * 512],
                    True, True, waits=[lbr2, ("VV", rk_ev), ("VV", radd_ev)])
            radd_ev = vop(lambda e, Gi=Gi, q4=q4: e.tensor_add(
                A[:].bitcast(F32)[:, Gi * 2048 + q4 * 512:Gi * 2048 + (q4 + 1) * 512],
                A[:].bitcast(F32)[:, Gi * 2048 + q4 * 512:Gi * 2048 + (q4 + 1) * 512],
                P[q4][:, 0:512]), waits=[("MM", mq)])
    ranks_done = radd_ev

    kept_k, vv_, pp_, jj_ = WK2[:, 0:512], WK2[:, 512:1024], WK2[:, 1024:1536], WK2[:, 1536:2048]
    yr, wfin = se, m2
    stash = WK[:, 0:2048].rearrange("p (t f) -> p t f", f=4)
    yrw_ev = 0
    for kk, (indv, iv, wv_) in enumerate(((ind1, i1f, wt1), (ind2, i2f, wt2))):
        rk = Dx[:, 7680:8192]
        vop(lambda e, indv=indv, kk=kk: e.tensor_tensor(
            scr2v, ranks[:, :, kk * 8:(kk + 1) * 8], indv, op=ALU.mult),
            waits=[("VV", ranks_done), ("VV", yrw_ev)])
        vop(lambda e, rk=rk: e.reduce_sum(rk, scr2v, axis=AX.X))
        vop(lambda e, rk=rk: e.tensor_scalar(kept_k, rk, 8192.0, None, op0=ALU.is_lt))
        vop(lambda e, rk=rk, kk=kk: e.tensor_scalar(vv_, rk, float(kk * CAP), None, op0=ALU.add))
        vi32 = scr[:, 0:512].bitcast(I32)
        pi32 = scr[:, 512:1024].bitcast(I32)
        vop(lambda e, vi32=vi32: e.tensor_copy(vi32, vv_))
        vop(lambda e, vi32=vi32, pi32=pi32: e.tensor_scalar(pi32, vi32, 2047, None, op0=ALU.bitwise_and))
        vop(lambda e, pi32=pi32: e.tensor_copy(pp_, pi32))
        vop(lambda e, vi32=vi32, pi32=pi32: e.tensor_scalar(pi32, vi32, 11, None, op0=ALU.arith_shift_right))
        vop(lambda e, pi32=pi32: e.tensor_copy(jj_, pi32))
        vop(lambda e: e.tensor_scalar(yr, jj_, 16384.0, None, op0=ALU.mult))
        vop(lambda e, iv=iv: e.scalar_tensor_tensor(yr, iv, 2048.0, yr, op0=ALU.mult, op1=ALU.add))
        vop(lambda e: e.tensor_add(yr, yr, pp_))
        msk, t1, r2 = scr[:, 1024:1536], scr[:, 1536:2048], scr[:, 2048:2560]
        t1i = t1.bitcast(I32)
        t2i = scr[:, 2560:3072].bitcast(I32)
        vop(lambda e, msk=msk: e.tensor_scalar(msk, jj_, 7.0, None, op0=ALU.is_ge))
        vop(lambda e, t1i=t1i, vi32=vi32: e.tensor_scalar(t1i, vi32, 9, None, op0=ALU.arith_shift_right))
        vop(lambda e, t1i=t1i: e.tensor_scalar(t1i, t1i, 3, None, op0=ALU.bitwise_and))
        vop(lambda e, r2=r2, t1i=t1i: e.tensor_copy(r2, t1i))
        vop(lambda e, r2=r2: e.tensor_scalar(r2, r2, 4096.0, 114688.0, op0=ALU.mult, op1=ALU.add))
        vop(lambda e, r2=r2, iv=iv: e.scalar_tensor_tensor(r2, iv, 512.0, r2, op0=ALU.mult, op1=ALU.add))
        vop(lambda e, t2i=t2i, vi32=vi32: e.tensor_scalar(t2i, vi32, 511, None, op0=ALU.bitwise_and))
        vop(lambda e, t1=t1, t2i=t2i: e.tensor_copy(t1, t2i))
        vop(lambda e, r2=r2, t1=t1: e.tensor_add(r2, r2, t1))
        vop(lambda e, r2=r2: e.tensor_sub(r2, r2, yr))
        vop(lambda e, r2=r2, msk=msk: e.tensor_mul(r2, r2, msk))
        vop(lambda e, r2=r2: e.tensor_add(yr, yr, r2))
        vop(lambda e: e.tensor_mul(yr, yr, kept_k))
        vop(lambda e, wv_=wv_: e.tensor_mul(wfin, wv_, kept_k))
        vop(lambda e, kk=kk: e.tensor_copy(stash[:, 0:512, kk:kk + 1],
                                           yr.rearrange("p (t a) -> p t a", a=1)))
        yrw_ev = vop(lambda e, kk=kk: e.tensor_copy(stash[:, 0:512, 2 + kk:3 + kk],
                                                    wfin.rearrange("p (t a) -> p t a", a=1)))
    lyrw = dma("sync", "LY", yrw_s.rearrange("(p t) f -> p t f", p=128), stash,
               waits=[("VV", yrw_ev)])

    i128bc4 = SM_i128.rearrange("p (a r) -> p a r", a=1).broadcast_to([128, 4, 128])
    PTb = E[:, 0:512].rearrange("p (t r) -> p t r", r=128)
    CLO, CHI = E[:, 1024:1536], E[:, 1536:2048]
    CT = WK2[:].bitcast(BF16)
    prev_kk = ("MM", 0)
    sent_v = vop(lambda e: e.memset(SMc(15)[0:1, :], 999999.0), waits=[("VV", yrw_ev)])
    col_toks = []
    for kk, indv in enumerate((ind1, ind2)):
        roe, sel = Dx[:, 7680:8192], Dx[:, 0:512]
        va = vop(lambda e, kk=kk: e.tensor_tensor(
            scr2v, ranks[:, :, kk * 8:(kk + 1) * 8], ohbc, op=ALU.mult),
            waits=[("VV", yrw_ev), lyrw, prev_kk])
        vop(lambda e, roe=roe: e.reduce_sum(roe, scr2v, axis=AX.X))
        vop(lambda e, indv=indv: e.tensor_tensor(scr2v, indv, ohbc, op=ALU.mult))
        selv = vop(lambda e, sel=sel: e.reduce_sum(sel, scr2v, axis=AX.X))
        lrow = dma("sync", "LY", rowstage[kk * 1024:kk * 1024 + 512].rearrange("(a b) -> a b", a=1),
                   roe[0:1, :], waits=[("VV", selv)])
        lsen = dma("sync", "LY", rowstage[kk * 1024 + 512:kk * 1024 + 513].rearrange("(a b) -> a b", a=1),
                   SMc(15)[0:1, :], waits=[("VV", sent_v)])
        for j in range(4):
            col_toks.append(dma("sync", "LY", SM_base[kk][:, j:j + 1],
                                rowstage[kk * 1024 + j * 128:kk * 1024 + (j + 1) * 128].rearrange("(p a) -> p a", a=1),
                                waits=[lrow]))
            col_toks.append(dma("sync", "LY", SM_basep1[kk][:, j:j + 1],
                                rowstage[kk * 1024 + j * 128 + 1:kk * 1024 + (j + 1) * 128 + 1].rearrange("(p a) -> p a", a=1),
                                waits=[lrow, lsen]))
        vop(lambda e, kk=kk: e.tensor_sub(SM_cnt[kk], SM_basep1[kk], SM_base[kk]),
            waits=col_toks[-8:])
        rof = Dx[:, 512:1024]
        mK = mm(P[6][:, 0:512], ONESF[:], roe[0:1, 0:512], True, True, waits=[("VV", selv)])
        vf = vop(lambda e, roe=roe, rof=rof: e.tensor_tensor(rof, roe, P[6][:, 0:512], op=ALU.subtract),
                 waits=[("MM", mK)])
        rofb, selb = C[:, 0:512], C[:, 512:1024]
        vop(lambda e, rof=rof: e.tensor_copy(rofb, rof), waits=[("VV", vf)])
        vh2 = vop(lambda e, sel=sel: e.tensor_copy(selb, sel))
        pt_ev = 0
        clo_ev = 0
        for g4 in range(128):
            rofs = rofb.rearrange("p (t a) -> p t a", a=1)[:, g4 * 4:(g4 + 1) * 4, :].broadcast_to([128, 4, 128])
            sels = selb.rearrange("p (t a) -> p t a", a=1)[:, g4 * 4:(g4 + 1) * 4, :].broadcast_to([128, 4, 128])
            vop(lambda e, rofs=rofs: e.tensor_tensor(PTb, i128bc4, rofs, op=ALU.is_equal),
                waits=[("VV", vh2), ("MM", pg.cnt["MM"])])
            pt_ev = vop(lambda e, sels=sels: e.tensor_tensor(PTb, PTb, sels, op=ALU.mult))
            mc2 = 0
            for t4 in range(4):
                tile = g4 * 4 + t4
                col = (tile % 32) * 4
                mc2 = mm(P[7][:, col:col + 2], PTb[:, t4, :], IDC[:, tile * 2:tile * 2 + 2],
                         True, True, waits=[("VV", pt_ev), ("VV", clo_ev if (tile % 32) == 0 else 0)])
            if g4 % 8 == 7:
                f0 = (g4 // 8) * 32
                pv = P[7][:, 0:128].rearrange("p (t f) -> p t f", f=4)
                clo_ev = vop(lambda e, f0=f0, pv=pv: e.tensor_copy(
                    CLO[:, f0:f0 + 32].rearrange("p (t a) -> p t a", a=1),
                    pv[:, 0:32, 0:1]), waits=[("MM", mc2)])
                clo_ev = vop(lambda e, f0=f0, pv=pv: e.tensor_copy(
                    CHI[:, f0:f0 + 32].rearrange("p (t a) -> p t a", a=1),
                    pv[:, 0:32, 1:2]))
        ct_prev = 0
        for g2 in range(4):
            for plane, srcb in enumerate((CLO, CHI)):
                mt = tr(P[6][:, 0:64].bitcast(BF16), srcb[:, g2 * 128:(g2 + 1) * 128], SMB_id,
                        waits=[("VV", clo_ev), ("VV", ct_prev)])
                ct_prev = vop(lambda e, g2=g2, plane=plane: e.tensor_copy(
                    CT[:, (plane * 4 + g2) * 128:(plane * 4 + g2 + 1) * 128],
                    P[6][:, 0:64].bitcast(BF16)), waits=[("MM", mt)])
        ctc = WK[:, 0:512].rearrange("p (g r) -> p g r", r=128)
        ctlo = CT[:, 0:512].rearrange("p (g r) -> p g r", r=128)
        cthi = CT[:, 512:1024].rearrange("p (g r) -> p g r", r=128)
        vcm = vop(lambda e: e.scalar_tensor_tensor(ctc, cthi, 256.0, ctlo, op0=ALU.mult, op1=ALU.add),
                  waits=[("VV", ct_prev), lyrw])
        cti = WK[:, 512:1024].bitcast(I32).rearrange("p (g r) -> p g r", r=128)
        vci = vop(lambda e: e.tensor_copy(cti, ctc), waits=[("VV", vcm)])
        prev_kk = dma("sync", "LY", ctflat[kk].rearrange("(g p r) a -> p g (r a)", p=128, r=128),
                      cti, waits=[("VV", vci)])
        col_toks.append(prev_kk)
    idx_done = pg.snap()
    if stop == "index":
        return _finish_early()

    w1lo = Bg[:].rearrange("p (dt f) -> p dt f", dt=4)
    w1hi = E[:].rearrange("p (dt f) -> p dt f", dt=4)
    w2sb = C[:].rearrange("p (ft d) -> p ft d", ft=16)
    w2sb2 = A[:].rearrange("p (ft d) -> p ft d", ft=16)
    lw1a = dma("gpsimd", "LG", w1lo, w1c[0:512, :].rearrange("(dt p) f -> p dt f", p=128),
               waits=pg.wsnap(idx_done, ("CC",)))
    lw1b = dma("gpsimd", "LG", w1hi, w1c[512:1024, :].rearrange("(dt p) f -> p dt f", p=128),
               waits=pg.wsnap(idx_done, ("CC",)))
    lw2a = dma("gpsimd", "LG", w2sb, w2c[0:2048, :].rearrange("(ft p) d -> p ft d", p=128),
               waits=pg.wsnap(idx_done, ("CC",)))
    lw2b = dma("gpsimd", "LG", w2sb2, w2c[2048:4096, :].rearrange("(ft p) d -> p ft d", p=128),
               waits=pg.wsnap(idx_done, ("CC",)))

    def w1slice(dt, f0, f1):
        return (w1lo if dt < 4 else w1hi)[:, dt % 4, f0:f1]

    def w2slice(ft, d0, d1):
        return (w2sb if ft < 16 else w2sb2)[:, ft % 16, d0:d1]

    hT_sb = Dg[:].rearrange("p (ft s) -> p ft s", ft=32)
    xeT = [Fg[:, i * 4096:(i + 1) * 4096].rearrange("p (dt s) -> p dt s", dt=8) for i in range(2)]
    gb = W2b[:, 0:2048]
    NCH = 32
    gat_l = [ZT, ZT, ZT, ZT]
    yag_toks = []
    xe_mm = [0, 0]
    xet_marks = {}
    FS = dict(ctm=0, cti=0, h_ev=0, y_ev=0, hT_mm=0)

    def ffn_gather(ch):
        kk = ch // 16
        j0 = (ch % 16) * 4
        pp = ch % 2
        cst_toks = []
        for q in range(4):
            jslot = j0 + q
            icol, icol2 = q * 2, q * 2 + 1
            tj = WK[:, 0:128]
            vtj = vop(lambda e, jslot=jslot, tj=tj: e.tensor_scalar(
                tj, SM_i128, float(jslot * 128), None, op0=ALU.add),
                waits=(col_toks if (ch == 0 and q == 0) else [("MM", FS['ctm'])]))
            tjb = tj.rearrange("p (a s) -> p a s", a=1).broadcast_to([128, 4, 128])
            bbc = SM_base[kk].rearrange("p (t a) -> p t a", a=1).broadcast_to([128, 4, 128])
            bp1c = SM_basep1[kk].rearrange("p (t a) -> p t a", a=1).broadcast_to([128, 4, 128])
            cntc = SM_cnt[kk].rearrange("p (t a) -> p t a", a=1).broadcast_to([128, 4, 128])
            B4 = W2b[:, 2048:2560].rearrange("p (t s) -> p t s", s=128)
            C4 = W2b[:, 2560:3072].rearrange("p (t s) -> p t s", s=128)
            vb4 = vop(lambda e, tjb=tjb, bbc=bbc, B4=B4: e.tensor_tensor(B4, tjb, bbc, op=ALU.is_ge))
            vc4 = vop(lambda e, tjb=tjb, bp1c=bp1c, C4=C4: e.tensor_tensor(C4, tjb, bp1c, op=ALU.is_ge))
            vc5 = vop(lambda e, C4=C4, cntc=cntc: e.tensor_tensor(C4, C4, cntc, op=ALU.mult))
            for t1 in range(4):
                mB = mm(P[4][:, 0:1], B4[:, t1, :], SMB_onesc, t1 == 0, t1 == 3,
                        waits=([("VV", vc5), ("VV", FS['cti'])] if t1 == 0 else ()))
                FS['ctm'] = mm(P[5][:, 0:1], C4[:, t1, :], SMB_onesc, t1 == 0, t1 == 3)
            cidx = WK[:, 128:129]
            vop(lambda e, cidx=cidx: e.tensor_scalar(cidx, P[4][:, 0:1], 128.0, -128.0,
                                                     op0=ALU.mult, op1=ALU.add),
                waits=[("MM", FS['ctm'])])
            vop(lambda e, cidx=cidx: e.scalar_tensor_tensor(
                cidx, SM_iotac, 1.0, cidx, op0=ALU.mult, op1=ALU.add))
            vop(lambda e, cidx=cidx, jslot=jslot: e.tensor_scalar(
                cidx, cidx, float(jslot * 128), None, op0=ALU.add))
            vop(lambda e, cidx=cidx: e.tensor_tensor(cidx, cidx, P[5][:, 0:1], op=ALU.subtract))
            vop(lambda e, cidx=cidx: e.tensor_scalar(cidx, cidx, 0.0, 65535.0,
                                                     op0=ALU.max, op1=ALU.min))
            FS['cti'] = vop(lambda e, cidx=cidx, icol=icol: e.tensor_copy(ISL[:, icol:icol + 1], cidx),
                            waits=[gat_l[2 + q % 2]])
            gq1 = gop(lambda e, icol=icol, icol2=icol2, kk=kk: e.indirect_dma_start(
                out=ISL[:, icol2:icol2 + 1], out_offset=None, in_=ctflat[kk],
                in_offset=IndirectOffsetOnAxis(ISL[:, icol:icol + 1], 0)),
                waits=[("VV", FS['cti'])])
            gq = gop(lambda e, icol2=icol2, q=q: e.indirect_dma_start(
                out=gb[:, (q % 2) * 1024:(q % 2) * 1024 + 1024],
                out_offset=None, in_=xfAG,
                in_offset=IndirectOffsetOnAxis(ISL[:, icol2:icol2 + 1], 0)),
                waits=[gq1, ("CC", ag_attn), gat_l[q % 2]])
            gat_l[2 + q % 2] = gq
            gat_l[q % 2] = dma("sync", "LY", comp_s[ch * 512 + q * 128:ch * 512 + (q + 1) * 128, :],
                               gb[:, (q % 2) * 1024:(q % 2) * 1024 + 1024], waits=[gq])
            cst_toks.append(gat_l[q % 2])
        xm = []
        for dt in range(8):
            xm.append(dma("scalar", "LS", xeT[pp][:, dt, :],
                          comp_s[ch * 512:(ch + 1) * 512, dt * 128:(dt + 1) * 128],
                          waits=cst_toks + [("MM", xe_mm[pp])],
                          transpose=True))
        xet_marks[ch] = xm

    def ffn_compute(ch):
        nonlocal ag_n
        pp = ch % 2
        for fs in range(32):
            psp = P[fs % 2]
            for dt in range(8):
                w = (xet_marks[ch] + [lw1a, lw1b, ("SS", FS['h_ev'])]) if dt == 0 else ()
                m = mm(psp[:, 0:512], w1slice(dt, fs * 128, (fs + 1) * 128),
                       xeT[pp][:, dt, :], dt == 0, dt == 7, waits=w)
            xe_mm[pp] = m
            FS['h_ev'] = sop(lambda e, fs=fs, psp=psp: e.activation(
                hT_sb[:, fs, :], psp[:, 0:512], AF.Gelu_apprx_tanh),
                waits=[("MM", m), ("MM", FS['hT_mm'] if fs < 2 else 0)])
        yst_l = [ZT, ZT]
        for st in range(4):
            yp = st % 2
            for dh in range(2):
                psy = P[2 + (st * 2 + dh) % 2]
                for fs in range(32):
                    w = ([("SS", FS['h_ev']), lw2a, lw2b, ("VV", FS['y_ev']),
                          yst_l[yp]] if fs == 0 else ())
                    m = mm(psy[:, 0:512], hT_sb[:, fs, st * 128:(st + 1) * 128],
                           w2slice(fs, dh * 512, (dh + 1) * 512), fs == 0, fs == 31, waits=w)
                FS['hT_mm'] = m
                FS['y_ev'] = vop(lambda e, dh=dh, psy=psy, yp=yp: e.tensor_copy(
                    REPB[:, yp * 1024 + dh * 512: yp * 1024 + (dh + 1) * 512],
                    psy[:, 0:512]), waits=[("MM", m)])
            yst_l[yp] = dma("sync", "LY",
                            ycmp_s[ch * 512 + st * 128:ch * 512 + (st + 1) * 128, :],
                            REPB[:, yp * 1024:(yp + 1) * 1024], waits=[("VV", FS['y_ev'])])
            yag_toks.append(yst_l[yp])
        if ch >= 28:
            gop(lambda e, ch=ch: e.collective_compute(
                "AllGather", ALU.bypass, replica_groups=[list(range(NCORE))],
                ins=[ycmp_s[ch * 512:(ch + 1) * 512, :].opt()],
                outs=[yAG[114688 + (ch - 28) * 4096:114688 + (ch - 27) * 4096, :].opt()]),
                waits=list(yag_toks), inc=("CC", 1))
            ag_n += 1
            yag_toks.clear()
        elif ch % 4 == 3:
            jag = ch // 4
            gop(lambda e, jag=jag: e.collective_compute(
                "AllGather", ALU.bypass, replica_groups=[list(range(NCORE))],
                ins=[ycmp_s[jag * 2048:(jag + 1) * 2048, :].opt()],
                outs=[yAG[jag * NCORE * 2048:(jag + 1) * NCORE * 2048, :].opt()]),
                waits=list(yag_toks), inc=("CC", 1))
            ag_n += 1
            yag_toks.clear()

    IDCi = IDC[:].bitcast(I32)
    fpre = {}

    def pre_stage(jt):
        rw = WK2[:, 1536 + 4 * jt:1540 + 4 * jt]
        g1 = gop(lambda e, jt=jt, rw=rw: e.indirect_dma_start(
            out=rw, out_offset=None, in_=yrw_s,
            in_offset=IndirectOffsetOnAxis(OWN[:, jt:jt + 1], 0)),
            waits=[lyrw] + (pg.wsnap(idx_done, ("CC",)) if jt == 0 else []))
        r1v = vop(lambda e, rw=rw, jt=jt: e.tensor_copy(IDCi[:, 2 * jt:2 * jt + 1], rw[:, 0:1]),
                  waits=[g1] + (pg.wsnap(idx_done, ("CC",)) if jt == 0 else []))
        r2v = vop(lambda e, rw=rw, jt=jt: e.tensor_copy(IDCi[:, 2 * jt + 1:2 * jt + 2], rw[:, 1:2]))
        fpre[jt] = (r1v, r2v)

    ffn_gather(0)
    for ch in range(NCH):
        if ch + 1 < NCH:
            ffn_gather(ch + 1)
        ffn_compute(ch)
        pre_stage(2 * ch)
        pre_stage(2 * ch + 1)
    y_ev = FS['y_ev']
    if "ycmp" in dbg:
        dma("sync", "LY", dbg["ycmp"], ycmp_s, waits=pg.qbar("LY") + [("VV", y_ev)])
    ffn_snap = pg.snap()
    if stop == "ffn":
        return _finish_early()

    lfb = dma("sync", "LY", WK2[0:1, 0:1024], fbi.rearrange("(a d) -> a d", a=1),
              waits=pg.wsnap(ffn_snap, ("CC",)))
    ovf = vop(lambda e: e.memset(WK2[0:1, 1024:1152], 1.0), waits=pg.wsnap(ffn_snap, ("CC",)))
    repf_ev = 0
    for sl in range(2):
        m = mm(P[6][:, 0:512], WK2[0:1, 1024:1152], WK2[0:1, sl * 512:(sl + 1) * 512],
               True, True, waits=[lfb, ("VV", ovf), ("VV", repf_ev)])
        repf_ev = vop(lambda e, sl=sl: e.tensor_copy(REPF[:, sl * 512:(sl + 1) * 512], P[6][:, 0:512]),
                      waits=[("MM", m)])
    NP = 4
    SKEW = 2
    Af = A[:].bitcast(F32)
    Bf = Bg[:].bitcast(F32)
    ffw = [("MM", pg.cnt["MM"]), ("VV", pg.cnt["VV"]), ("SS", pg.cnt["SS"])]
    st_l = [ZT] * NP
    fmarks = {}

    def fin_gather(jt):
        b, tt = divmod(jt, NT)
        p = jt % NP
        first = [] if jt >= NP else ffw
        yb = Dg[:, p * 2048:(p + 1) * 2048]
        xb = Af[:, p * 1024:(p + 1) * 1024]
        g2 = gop(lambda e, yb=yb, jt=jt: e.indirect_dma_start(
            out=yb[:, 0:1024], out_offset=None, in_=yAG,
            in_offset=IndirectOffsetOnAxis(IDCi[:, 2 * jt:2 * jt + 1], 0)),
            waits=[("VV", fpre[jt][0]), ("CC", ag_n), st_l[p]] + first)
        g3 = gop(lambda e, yb=yb, jt=jt: e.indirect_dma_start(
            out=yb[:, 1024:2048], out_offset=None, in_=yAG,
            in_offset=IndirectOffsetOnAxis(IDCi[:, 2 * jt + 1:2 * jt + 2], 0)),
            waits=[("VV", fpre[jt][1])])
        lx2 = dma("sync", "LY", xb, xf32_s[b][tt * 128:(tt + 1) * 128, :],
                  waits=[st_l[p]] + first)
        fmarks[jt] = (g2, g3, lx2)

    def fin_compute(jt):
        b, tt = divmod(jt, NT)
        p = jt % NP
        g2, g3, lx2 = fmarks.pop(jt)
        rw = WK2[:, 1536 + 4 * jt:1540 + 4 * jt]
        yb = Dg[:, p * 2048:(p + 1) * 2048]
        xb = Af[:, p * 1024:(p + 1) * 1024]
        acc = Bf[:, p * 1024:(p + 1) * 1024]
        a1 = vop(lambda e, acc=acc, yb=yb, rw=rw: e.scalar_tensor_tensor(
            acc, yb[:, 0:1024], rw[:, 2:3], REPF[:], op0=ALU.mult, op1=ALU.add),
            waits=[g2, g3, ("VV", repf_ev)])
        a2 = vop(lambda e, acc=acc, yb=yb, rw=rw: e.scalar_tensor_tensor(
            acc, yb[:, 1024:2048], rw[:, 3:4], acc, op0=ALU.mult, op1=ALU.add))
        a3 = vop(lambda e, acc=acc, xb=xb: e.tensor_add(acc, acc, xb), waits=[lx2])
        nm, var, rstd, sqs = SMc(p * 4), SMc(p * 4 + 1), SMc(p * 4 + 2), SMc(p * 4 + 3)
        vop(lambda e, nm=nm, acc=acc: e.reduce_sum(nm, acc, axis=AX.X, negate=True),
            waits=[("VV", a3)])
        vop(lambda e, nm=nm: e.tensor_scalar(nm, nm, 1.0 / 1024.0, None, op0=ALU.mult))
        v3 = vop(lambda e, nm=nm, acc=acc: e.tensor_scalar(acc, acc, nm, None, op0=ALU.add))
        s1 = vop(lambda e, sqs=sqs, acc=acc, xb=xb: e.scalar_tensor_tensor(
            xb, acc, 1.0, acc, op0=ALU.mult, op1=ALU.mult, accum_out=sqs),
            waits=[("VV", v3)])
        v4 = vop(lambda e, var=var, sqs=sqs: e.tensor_scalar(var, sqs, 1.0 / 1024.0, 1e-5,
                                                             op0=ALU.mult, op1=ALU.add))
        s2 = sop(lambda e, rstd=rstd, var=var: e.activation(rstd, var, AF.Ln), waits=[("VV", v4)])
        s3 = sop(lambda e, rstd=rstd: e.activation(rstd, rstd, AF.Exp, scale=-0.5))
        v5 = vop(lambda e, rstd=rstd, acc=acc: e.tensor_scalar(acc, acc, rstd, None, op0=ALU.mult),
                 waits=[("SS", s3)])
        st_l[p] = dma("sync", "LY", out_c[b][tt * 128:(tt + 1) * 128, :], acc, waits=[("VV", v5)])

    NTILE = NSTEP * NT
    for jt in range(NTILE):
        fin_gather(jt)
        if jt >= SKEW:
            fin_compute(jt - SKEW)
    for jt in range(NTILE - SKEW, NTILE):
        fin_compute(jt)
    pg.emit("sync", lambda e: e.nop(), waits=list(st_l))

    pg.realize()
    return nc


_NC_CACHE = None


def kernel(_stop=None, _dbg=None, **inputs):
    global _NC_CACHE
    perm = np.concatenate([np.arange(0, 64, 2), np.arange(1, 64, 2)])
    PERM = np.concatenate([h * 64 + perm for h in range(H)])
    f32 = np.float32
    Wq = np.ascontiguousarray(inputs["L0_attn_Wq"][:, PERM], dtype=f32)
    bq = np.ascontiguousarray(np.asarray(inputs["L0_attn_bq"], dtype=f32)[PERM].reshape(8, 128).T)
    Wk = np.ascontiguousarray(inputs["L0_attn_Wk"][:, PERM], dtype=f32)
    bk = np.ascontiguousarray(np.asarray(inputs["L0_attn_bk"], dtype=f32)[PERM].reshape(8, 128).T)
    x = np.asarray(inputs["L0_attn_x"], dtype=f32)
    if _stop is None and _dbg is None:
        if _NC_CACHE is None:
            _NC_CACHE = build_program()
        nc = _NC_CACHE
    else:
        nc = build_program(stop=_stop, dbg_keys=_dbg or ())
    in_maps = []
    for c in range(NCORE):
        batches = [c + 8 * bb for bb in range(NSTEP)]
        oh = np.zeros((128, 8), f32)
        oh[:, c] = 1.0
        tglob = np.concatenate([(np.arange(S, dtype=np.int64) + bb * S) for bb in batches])
        ownpos = ((tglob % 128) * 512 + tglob // 128).astype(np.int32)
        own = np.ascontiguousarray(ownpos.reshape(64, 128).T)
        in_maps.append({
            "x4": np.ascontiguousarray(x[batches]),
            "wqp": Wq, "bqp": bq, "wkp": Wk, "bkp": bk,
            "wvi": np.ascontiguousarray(inputs["L0_attn_Wv"], dtype=f32),
            "bvi": np.ascontiguousarray(inputs["L0_attn_bv"], dtype=f32),
            "wgi": np.ascontiguousarray(inputs["L0_attn_Wg"], dtype=f32),
            "bgi": np.ascontiguousarray(inputs["L0_attn_bg"], dtype=f32),
            "woi": np.ascontiguousarray(inputs["L0_attn_Wo"], dtype=f32),
            "boi": np.ascontiguousarray(inputs["L0_attn_bo"], dtype=f32),
            "rwi": np.ascontiguousarray(inputs["L0_ffn_router_w"], dtype=f32),
            "w1c": np.ascontiguousarray(inputs["L0_ffn_w1"][c], dtype=f32),
            "w2c": np.ascontiguousarray(inputs["L0_ffn_w2"][c], dtype=f32),
            "fbi": np.ascontiguousarray(np.asarray(inputs["L0_ffn_bias"], dtype=f32).reshape(D)),
            "onehot": oh,
            "ownrows": own,
        })
    res = run_bass_kernel_spmd(nc, in_maps, core_ids=list(range(NCORE)))
    if _stop is not None or _dbg:
        return res
    out = np.empty((B, S, D), np.float32)
    for c in range(NCORE):
        oc = res.results[c]["out_c"]
        for bb in range(NSTEP):
            out[c + 8 * bb] = oc[bb]
    return out


# version 18
# speedup vs baseline: 1.0928x; 1.0730x over previous
import numpy as np
import ml_dtypes

import concourse.bass as bass
import concourse.mybir as mybir
from concourse.bass_utils import run_bass_kernel_spmd
from concourse.bass_core_rust import IndirectOffsetOnAxis

F32 = mybir.dt.float32
BF16 = mybir.dt.bfloat16
I32 = mybir.dt.int32
AF = mybir.ActivationFunctionType
ALU = mybir.AluOpType
AX = mybir.AxisListType
BF = ml_dtypes.bfloat16

B, S, D = 32, 2048, 1024
H = 16
NCORE, NSTEP = 8, 4
NT = S // 128
GT = 512
CAP = 8192
PAD = 128
PLANE = 2 * (CAP + PAD)
ZROW = NCORE * 2 * CAP
ENGS = ("tensor", "vector", "scalar", "gpsimd", "sync")
DEBUG_OUTS = ()


def _tables():
    t = {}
    ah = 1.0 / 10000.0 ** np.linspace(0.0, 1.0, 32, dtype=np.float64)
    idx = np.arange(S, dtype=np.float64)
    cos_h = np.cos(idx[:, None] * ah[None, :])
    sin_h = np.sin(idx[:, None] * ah[None, :])
    t["cos4"] = np.tile(cos_h.T.astype(np.float32), (4, 1)).astype(BF)
    sgn = np.repeat(np.array([-1.0, 1.0, -1.0, 1.0], np.float32), 32)
    t["sin4"] = (np.tile(sin_h.T.astype(np.float32), (4, 1)) * sgn[:, None]).astype(BF)
    decay = np.log(1.0 - 2.0 ** (-5.0 - np.arange(H, dtype=np.float64)))
    g_negt = np.exp(-decay[:, None] * idx[None, :])
    gk = np.empty((8, 128, S), np.float32)
    for j in range(8):
        gk[j, 0:64] = (64.0 ** -0.5) * g_negt[2 * j]
        gk[j, 64:128] = (64.0 ** -0.5) * g_negt[2 * j + 1]
    t["gkrep"] = gk.astype(BF)
    masksum = (1.0 - np.exp(decay[:, None] * (idx[None, :] + 1))) / (1.0 - np.exp(decay[:, None]))
    c_tab = (np.exp(decay[:, None] * idx[None, :]) / np.sqrt(masksum)).astype(np.float32)
    ca = np.empty((128, 256), np.float32)
    for h in range(H):
        for s0 in range(16):
            ca[:, h * 16 + s0] = c_tab[h, s0 * 128:(s0 + 1) * 128]
    t["c_all"] = ca
    tt, ss = np.meshgrid(np.arange(128), np.arange(128), indexing="ij")
    t["tri"] = (tt <= ss).astype(np.float32).astype(BF)
    t["su"] = (tt < ss).astype(np.float32).astype(BF)
    t["id128b"] = np.eye(128, dtype=np.float32).astype(BF)
    t["id8f"] = np.eye(8, dtype=np.float32)
    t["iota128r"] = np.broadcast_to(np.arange(128, dtype=np.float32), (128, 128)).copy()
    t["iota8r"] = np.broadcast_to(np.arange(8, dtype=np.float32), (128, 8)).copy()
    t["rii8r"] = (99.0 - t["iota8r"]).copy()
    t["ones1x"] = np.ones((1, 128), np.float32).astype(BF)
    t["iotacol"] = np.arange(128, dtype=np.float32).reshape(128, 1)
    t["onescol"] = np.ones((128, 1), np.float32).astype(BF)
    idc = np.empty((128, GT, 2), np.float32)
    p = np.arange(128)
    for tile in range(GT):
        idc[:, tile, 0] = (tile % 2) * 128 + p
        idc[:, tile, 1] = tile // 2
    t["idcols"] = idc.astype(BF)
    return t


QDEPTH = 4


class Prog:
    def __init__(self, nc):
        self.nc = nc
        qsems = tuple(f"{q}{i}" for q in ("LY", "LG", "LS") for i in range(QDEPTH))
        self.names = ("MM", "VV", "SS", "CC") + qsems
        self.sems = {n: nc.alloc_semaphore(n) for n in self.names}
        self.cnt = {n: 0 for n in self.names}
        self.hw = {e: {n: 0 for n in self.names} for e in ENGS}
        self.qidx = {"LY": 0, "LG": 0, "LS": 0}

    def emit(self, eng, fn, waits=(), inc=None, force_self=False):
        w = []
        for s, v in waits:
            if v <= 0 or self.hw[eng][s] >= v:
                continue
            self.hw[eng][s] = v
            w.append((s, v))
        self.ops_list(eng).append((fn, w, inc))
        if inc is not None:
            self.cnt[inc[0]] += inc[1]
        return self.cnt[inc[0]] if inc else None

    def ops_list(self, eng):
        if not hasattr(self, "_ops"):
            self._ops = {e: [] for e in ENGS}
        return self._ops[eng]

    def snap(self):
        return dict(self.cnt)

    def qbar(self, *qs):
        out = []
        for q in qs:
            for i in range(QDEPTH):
                s = f"{q}{i}"
                out.append((s, self.cnt[s]))
        return out

    def wsnap(self, sn, skip=()):
        return [(s, v) for s, v in sn.items() if s in self.names and s not in skip]

    def realize(self):
        with self.nc.Block() as blk:
            for name in ENGS:
                def mk(nm):
                    def body(e):
                        for fn, w, inc in self._ops[nm]:
                            for s, v in w:
                                e.wait_ge(self.sems[s], v)
                            inst = fn(e)
                            if inc is not None:
                                inst.then_inc(self.sems[inc[0]], inc[1])
                    return body
                getattr(blk, name)(mk(name))


def build_program(stop=None, dbg_keys=None):
    global DEBUG_OUTS
    if dbg_keys is not None:
        DEBUG_OUTS = tuple(dbg_keys)
    T = _tables()
    nc = bass.Bass("TRN2", target_bir_lowering=False)

    def din(name, shape, dt=F32):
        return nc.dram_tensor(name, shape, dt, kind="ExternalInput").ap()
    x4 = din("x4", [NSTEP, S, D])
    wqp = din("wqp", [D, D]); bqp = din("bqp", [128, 8])
    wkp = din("wkp", [D, D]); bkp = din("bkp", [128, 8])
    wvi = din("wvi", [D, 2 * D]); bvi = din("bvi", [2 * D])
    wgi = din("wgi", [D, 2 * D]); bgi = din("bgi", [2 * D])
    woi = din("woi", [2 * D, D]); boi = din("boi", [D])
    rwi = din("rwi", [D, 8])
    w1c = din("w1c", [D, 4096]); w2c = din("w2c", [4096, D])
    fbi = din("fbi", [D])
    onehot = din("onehot", [128, 8])
    ownrows = din("ownrows", [128, 64], I32)
    out_c = nc.dram_tensor("out_c", [NSTEP, S, D], F32, kind="ExternalOutput").ap()
    dbg = {}
    for k in DEBUG_OUTS:
        shp = {"xf": [NSTEP * S, D], "logits": [NSTEP, 128, 128],
               "islot": [2, GT * 128], "ycmp": [2 * CAP, D]}[k]
        dt = {"xf": F32, "logits": F32, "islot": I32, "ycmp": BF16}[k]
        dbg[k] = nc.dram_tensor("dbg_" + k, shp, dt, kind="ExternalOutput").ap()

    def cinl(key):
        return nc.inline_tensor(T[key], name="c_" + key).ap()
    cos4_d, sin4_d, gk_d = cinl("cos4"), cinl("sin4"), cinl("gkrep")
    call_d, tri_d, su_d = cinl("c_all"), cinl("tri"), cinl("su")
    id128_d, id8_d = cinl("id128b"), cinl("id8f")
    i128r_d, i8r_d, rii8_d = cinl("iota128r"), cinl("iota8r"), cinl("rii8r")
    ones1_d, onesc_d, idc_d = cinl("ones1x"), cinl("onescol"), cinl("idcols")
    iotac_d = cinl("iotacol")

    def dsc(name, shape, dt):
        return nc.dram_tensor(name, shape, dt, kind="Internal").ap()
    xbf_s = dsc("xbf_s", [2, S, D], BF16)
    v_s = dsc("v_s", [S, 2 * D], BF16)
    g_s = dsc("g_s", [S, 2 * D], BF16)
    gT_s = dsc("gT_s", [2 * D, S], BF16)
    xf32_s = dsc("xf32_s", [NSTEP, S, D], F32)
    xfb_s = dsc("xfb_s", [NSTEP * S, D], BF16)
    lg_s = dsc("lg_s", [NSTEP, 128, 128], F32)
    xfAG = dsc("xfAG", [NCORE * NSTEP * S, D], BF16)
    lgAG = dsc("lgAG", [NSTEP * NCORE * 128, 128], F32)
    cstage = dsc("cstage", [GT * 16], F32)
    bstage = dsc("bstage", [GT * 16], F32)
    ctflat = [dsc("ctflat0", [GT * 128, 1], I32), dsc("ctflat1", [GT * 128, 1], I32)]
    rowstage = dsc("rowstage", [4096], F32)
    yrw_s = dsc("yrw_s", [NCORE * NSTEP * S, 4], F32)
    comp_s = dsc("comp_s", [2 * CAP, D], BF16)
    ycmp_s = dsc("ycmp_s", [2 * CAP, D], BF16)
    yAG = dsc("yAG", [ZROW, D], BF16)

    A = nc.alloc_sbuf_tensor("BIGA", [128, 16384], BF16)
    Bg = nc.alloc_sbuf_tensor("BIGB", [128, 16384], BF16)
    C = nc.alloc_sbuf_tensor("BIGC", [128, 16384], BF16)
    Dg = nc.alloc_sbuf_tensor("BIGD", [128, 16384], BF16)
    E = nc.alloc_sbuf_tensor("BIGE", [128, 16384], BF16)
    Fg = nc.alloc_sbuf_tensor("BIGF", [128, 8192], BF16, is_transpose_dest=True)
    WK = nc.alloc_sbuf_tensor("WK", [128, 2048], F32)
    WK2 = nc.alloc_sbuf_tensor("WK2", [128, 2048], F32, is_transpose_dest=True)
    REPB = nc.alloc_sbuf_tensor("REPB", [128, 2048], BF16)
    REPF = nc.alloc_sbuf_tensor("REPF", [128, 1024], F32)
    SM = nc.alloc_sbuf_tensor("SM", [128, 512], F32)
    SMB = nc.alloc_sbuf_tensor("SMB", [128, 640], BF16)
    IDC = nc.alloc_sbuf_tensor("IDC", [128, 1024], BF16)
    ISL = nc.alloc_sbuf_tensor("ISL", [128, 140], I32)
    OWN = nc.alloc_sbuf_tensor("OWN", [128, 64], I32)
    ONESF = nc.alloc_sbuf_tensor("ONESF", [1, 128], F32)
    P = [nc.alloc_psum_tensor(f"P{i}", [128, 512], F32) for i in range(8)]
    assert nc.sbuf_bytes_remaining >= 0

    SM_call = SM[:, 0:256]
    SM_i8 = SM[:, 256:264]
    SM_rii8 = SM[:, 264:272]
    SM_id8 = SM[:, 272:280]
    SM_bq = SM[:, 280:288]
    SM_bk = SM[:, 288:296]
    SM_oh = SM[:, 296:304]
    SM_i128 = SM[:, 304:432]
    SMc = lambda i: SM[:, 432 + i:433 + i]
    SM_iotac = SM[:, 448:449]
    SM_base = [SM[:, 452 + 12 * k:456 + 12 * k] for k in range(2)]
    SM_basep1 = [SM[:, 456 + 12 * k:460 + 12 * k] for k in range(2)]
    SM_cnt = [SM[:, 460 + 12 * k:464 + 12 * k] for k in range(2)]
    SMB_tri = SMB[:, 0:128]
    SMB_su = SMB[:, 128:256]
    SMB_id = SMB[:, 256:384]
    SMB_ones1 = SMB[0:1, 384:512]
    SMB_onesc = SMB[:, 512:513]
    SMB_rw = SMB[:, 520:584].rearrange("p (dt e) -> p dt e", e=8)

    pg = Prog(nc)
    ZT = ("MM", 0)

    def dma(eng, q, out, in_, waits=(), **kw):
        s = f"{q}{pg.qidx[q] % QDEPTH}"
        pg.qidx[q] += 1
        def f(e, o=out, i=in_, k=dict(kw)):
            return e.dma_start(out=o, in_=i, **k)
        pg.emit(eng, f, waits=list(waits) + [(s, pg.cnt[s])], inc=(s, 16))
        return (s, pg.cnt[s])

    def mm(out, lhsT, rhs, start, stop, waits=()):
        def f(e, o=out, l=lhsT, r=rhs, st=start, sp=stop):
            return e.matmul(o, l, r, start=st, stop=sp)
        return pg.emit("tensor", f, waits=waits, inc=("MM", 1))

    def tr(out, in_, ident, waits=()):
        def f(e, o=out, i=in_, d=ident):
            return e.transpose(o, i, d)
        return pg.emit("tensor", f, waits=waits, inc=("MM", 1))

    def vop(fn, waits=()):
        return pg.emit("vector", fn, waits=list(waits) + [("VV", pg.cnt["VV"])],
                       inc=("VV", 1), force_self=True)

    def sop(fn, waits=()):
        return pg.emit("scalar", fn, waits=list(waits) + [("SS", pg.cnt["SS"])],
                       inc=("SS", 1), force_self=True)

    def gop(fn, waits=(), inc=None):
        if inc is not None:
            pg.emit("gpsimd", fn, waits=waits, inc=inc)
            return ("CC", pg.cnt["CC"])
        s = f"LG{pg.qidx['LG'] % QDEPTH}"
        pg.qidx["LG"] += 1
        pg.emit("gpsimd", fn, waits=list(waits) + [(s, pg.cnt[s])], inc=(s, 16))
        return (s, pg.cnt[s])

    dma("sync", "LY", SM_call, call_d)
    dma("sync", "LY", SM_i8, i8r_d)
    dma("sync", "LY", SM_rii8, rii8_d)
    dma("sync", "LY", SM_id8[0:8, :], id8_d)
    dma("sync", "LY", SM_bq, bqp)
    dma("sync", "LY", SM_bk, bkp)
    dma("sync", "LY", SM_oh, onehot)
    dma("sync", "LY", SM_i128, i128r_d)
    dma("sync", "LY", SM_iotac, iotac_d)
    dma("sync", "LY", SMB_tri, tri_d)
    dma("sync", "LY", SMB_su, su_d)
    dma("sync", "LY", SMB_id, id128_d)
    dma("sync", "LY", SMB_ones1, ones1_d)
    dma("sync", "LY", SMB_onesc, onesc_d)
    dma("sync", "LY", IDC[:].rearrange("p (t two) -> p t two", two=2), idc_d)
    dma("sync", "LY", OWN[:], ownrows)
    lrw = dma("gpsimd", "LG", SMB_rw, rwi.rearrange("(dt p) e -> p dt e", p=128))
    zv = vop(lambda e: e.memset(WK[:], 0.0))
    vop(lambda e: e.memset(ONESF[:], 1.0))
    lbo = dma("sync", "LY", WK2[0:1, 0:1024], boi.rearrange("(a d) -> a d", a=1))
    ov1 = vop(lambda e: e.memset(WK2[0:1, 1024:1152], 1.0))
    rv0 = 0
    for sl in range(2):
        m = mm(P[6][:, 0:512], WK2[0:1, 1024:1152], WK2[0:1, sl * 512:(sl + 1) * 512],
               True, True, waits=[lbo, ("VV", ov1), ("VV", rv0)])
        rv0 = vop(lambda e, sl=sl: e.tensor_copy(REPF[:, sl * 512:(sl + 1) * 512], P[6][:, 0:512]),
                  waits=[("MM", m)])
    init_snap = pg.snap()

    xT = A[:].rearrange("p (dt s) -> p dt s", dt=8)
    qrT = Bg[:].rearrange("p (dt s) -> p dt s", dt=8)
    krT = C[:].rearrange("p (dt s) -> p dt s", dt=8)
    W2b = WK2[:].bitcast(BF16)
    qtmp = W2b[:, 0:2048]
    qsw = W2b[:, 2048:4096]
    F_gk = Fg[:, 0:2048]
    F_t2 = Fg[:, 2048:4096]
    F_cos = Fg[:, 4096:6144]
    F_sin = Fg[:, 6144:8192]
    wq_sb = E[:, 0:8192].rearrange("p (dt o) -> p dt o", dt=8)
    wk_sb = E[:, 8192:16384].rearrange("p (dt o) -> p dt o", dt=8)
    wvg_sb = E[:].rearrange("p (dt o) -> p dt o", dt=8)
    E_wo = E[:].rearrange("p (g d) -> p g d", g=16)
    D_xfb = Dg[:, 8192:9216]
    D_lgf = Dg[:, 9216:9472].bitcast(F32)
    D_xfT = [Bg[:, 12288 + i * 2048: 12288 + (i + 1) * 2048] for i in range(2)]

    prev_step = init_snap
    lcos = dma("sync", "LY", F_cos, cos4_d, waits=pg.wsnap(init_snap))
    lsin = dma("sync", "LY", F_sin, sin4_d)
    pre_x, pre_w = {}, {}
    xt_prev = []
    ov = 0
    tr_last = 0
    lg_marks = []
    ag_n = 0
    _sub = stop is not None and (stop in ('a0', 'b0', 'c0', 'd0', 'ef0', 'dskip') or stop.startswith('dh') or stop.startswith('dm') or stop in ('dep', 'dld', 'dsa', 'dva'))
    nstep_run = 0 if stop == 'p0' else (1 if _sub else NSTEP)
    for b in range(nstep_run):
        if b in pre_x:
            lgc, xt_l = pre_x.pop(b)
        else:
            lgc = dma("gpsimd", "LG", xbf_s[b % 2], x4[b], waits=pg.wsnap(prev_step, ("CC",)))
            xt_l = [dma("scalar", "LS", xT[:, j, :], xbf_s[b % 2][:, j * 128:(j + 1) * 128],
                        waits=[lgc], transpose=True) for j in range(8)]
        if b in pre_w:
            lq, lk = pre_w.pop(b)
        else:
            lq = dma("gpsimd", "LG", wq_sb, wqp.rearrange("(dt p) o -> p dt o", p=128),
                     waits=pg.wsnap(prev_step, ("CC",)))
            lk = dma("gpsimd", "LG", wk_sb, wkp.rearrange("(dt p) o -> p dt o", p=128))

        if stop == 'a0':
            continue
        pa_free_s = [0, 0]
        last_rot = 0
        for which in range(2):
            wsb, wld = (wq_sb, lq) if which == 0 else (wk_sb, lk)
            bcol = SM_bq if which == 0 else SM_bk
            dstT = qrT if which == 0 else krT
            for j in range(8):
                if which == 1:
                    lgk = dma("gpsimd", "LG", F_gk, gk_d[j],
                              waits=[("VV", last_rot)] +
                                    (pg.wsnap(prev_step, ("CC",)) if j == 0 else []))
                for sl in range(4):
                    ps = P[sl % 2]
                    for dt in range(8):
                        w = ([wld, ("VV", max(pa_free_s[sl % 2], last_rot))]
                             if dt == 0 else []) + [xt_l[dt]]
                        m = mm(ps[:, 0:512], wsb[:, dt, j * 128:(j + 1) * 128],
                               xT[:, dt, sl * 512:(sl + 1) * 512], dt == 0, dt == 7, waits=w)
                    pa_free_s[sl % 2] = vop(
                        lambda e, ps=ps, sl=sl, bc=bcol, j=j: e.tensor_scalar(
                            qtmp[:, sl * 512:(sl + 1) * 512], ps[:, 0:512],
                            bc[:, j:j + 1], None, op0=ALU.add),
                        waits=[("MM", m), ("VV", last_rot)])
                sse = max(pa_free_s)
                if which == 1:
                    gv = vop(lambda e: e.tensor_mul(qtmp[:], qtmp[:], F_gk[:]),
                             waits=[("VV", sse), lgk])
                else:
                    gv = 0
                w0 = [("VV", max(sse, gv)), lsin, lcos] + \
                     (pg.wsnap(prev_step, ("CC",)) if (which == 0 and j == 0) else [])
                vop(lambda e: e.tensor_copy(qsw[0:32, :], qtmp[32:64, :]), waits=w0)
                vop(lambda e: e.tensor_copy(qsw[32:64, :], qtmp[0:32, :]))
                vop(lambda e: e.tensor_copy(qsw[64:96, :], qtmp[96:128, :]))
                c4 = vop(lambda e: e.tensor_copy(qsw[96:128, :], qtmp[64:96, :]))
                vop(lambda e, dT=dstT, j=j: e.tensor_mul(dT[:, j, :], qtmp[:], F_cos[:]))
                t2 = vop(lambda e: e.tensor_mul(F_t2[:], qsw[:], F_sin[:]), waits=[("VV", c4)])
                last_rot = vop(lambda e, dT=dstT, j=j: e.tensor_add(dT[:, j, :], dT[:, j, :], F_t2[:]),
                               waits=[("VV", t2)])
        rot_done = pg.snap()

        if stop == 'b0':
            continue
        vg_last_store = ZT
        for which in range(2):
            wten, brow_d = (wvi, bvi) if which == 0 else (wgi, bgi)
            dst_d = v_s if which == 0 else g_s
            lw = dma("gpsimd", "LG", wvg_sb, wten.rearrange("(dt p) o -> p dt o", p=128),
                     waits=(pg.wsnap(rot_done) if which == 0 else
                            [("VV", pg.cnt["VV"]), ("MM", pg.cnt["MM"])]))
            lbr = dma("gpsimd", "LG", REPB[0:1, :], brow_d.rearrange("(a d) -> a d", a=1),
                      waits=pg.wsnap(rot_done))
            repv = 0
            for sl in range(4):
                m = mm(P[2][:, 0:512], SMB_ones1,
                       REPB[0:1, sl * 512:(sl + 1) * 512], True, True,
                       waits=[lbr, ("VV", repv)])
                repv = vop(lambda e, sl=sl: e.tensor_copy(REPB[:, sl * 512:(sl + 1) * 512], P[2][:, 0:512]),
                           waits=[("MM", m)])
            pv_free = [repv, repv]
            vg_st = [vg_last_store, vg_last_store]
            for tt in range(NT):
                buf = qtmp if tt % 2 == 0 else qsw
                ev = 0
                for dvs in range(4):
                    ps = P[dvs % 2]
                    for dt in range(8):
                        w = ([lw, ("VV", pv_free[dvs % 2])] if dt == 0 else []) + [xt_l[dt]]
                        m = mm(ps[:, 0:512], xT[:, dt, tt * 128:(tt + 1) * 128],
                               wvg_sb[:, dt, dvs * 512:(dvs + 1) * 512], dt == 0, dt == 7, waits=w)
                    ev = vop(lambda e, ps=ps, dvs=dvs, buf=buf: e.tensor_add(
                        buf[:, dvs * 512:(dvs + 1) * 512], ps[:, 0:512],
                        REPB[:, dvs * 512:(dvs + 1) * 512]),
                        waits=[("MM", m), vg_st[tt % 2]])
                    pv_free[dvs % 2] = ev
                vg_st[tt % 2] = dma("sync", "LY", dst_d[tt * 128:(tt + 1) * 128, :], buf[:],
                                    waits=[("VV", ev)])
            vg_last_store = vg_st[(NT - 1) % 2]

        if stop == 'c0':
            continue
        vg_bar = pg.qbar("LY")
        lwo = dma("gpsimd", "LG", E_wo, woi.rearrange("(g p) d -> p g d", p=128),
                  waits=[("MM", pg.cnt["MM"])])
        if b + 1 < nstep_run:
            nb = b + 1
            plgc = dma("gpsimd", "LG", xbf_s[nb % 2], x4[nb], waits=list(xt_prev))
            pxt = [dma("scalar", "LS", xT[:, j, :], xbf_s[nb % 2][:, j * 128:(j + 1) * 128],
                       waits=[plgc, ("MM", pg.cnt["MM"])], transpose=True) for j in range(8)]
            pre_x[nb] = (plgc, pxt)
        xt_prev = xt_l
        D_vre4 = [Dg[:, hr * 2112:(hr + 1) * 2112].rearrange("p (tt c) -> p tt c", c=132)
                  for hr in range(4)]
        D_gh2 = [Dg[:, 8448 + i * 2048:8448 + (i + 1) * 2048].rearrange("p (q d) -> p q d", d=512)
                 for i in range(2)]
        D_owb = Dg[:, 12544:14592].rearrange("p (s0 d) -> p s0 d", d=128)
        D_st = [Dg[:, 14592 + i * 512:14592 + (i + 1) * 512] for i in range(3)]
        D_gacc = [Fg[:, i * 2048:(i + 1) * 2048] for i in range(2)]
        SG_all = W2b[:, 2048:4096]
        OrawR = WK[:, 384:900].rearrange("p (rs c) -> p rs c", c=129)
        R4 = WK[:, 96:100]
        gat_all = W2b[:, 0:2048]
        if b == 0:
            ov = vop(lambda e: e.memset(
                Dg[:, 0:8448].rearrange("p (x c) -> p x c", c=132)[:, :, 128:129], 1.0),
                waits=pg.wsnap(prev_step, ("CC",)))
        MM1P = (0, 1, 7)
        st_mm_free = [0, 0, 0]
        pa_s = [0, 0, 0]
        pa_v = [pg.cnt["VV"]] * 3
        slot_i = 0
        po_ep = [0, 0, 0, 0]
        vre_mm4 = [0, 0, 0, 0]
        gh_ss = [0, 0]
        gacc_fl = [[], []]
        pmt4 = [0, 0, 0, 0]
        vg2_4 = [0, 0, 0, 0]
        ghp = 0
        _dH = 0 if stop == 'dskip' else H
        if stop is not None and stop.startswith("dh"):
            _dH = int(stop[2:])
        tail_q = []
        for hq in range(0, H, 4):
            if hq >= _dH:
                break
            lvh = []
            for hr in range(4):
                h = hq + hr
                lvh.append(dma("sync", "LY", D_vre4[hr][:, :, 0:128],
                               v_s.rearrange("(tt p) o -> p tt o", p=128)[:, :, h * 128:(h + 1) * 128],
                               waits=(vg_bar if hq == 0 else []) + [("VV", ov),
                                     ("MM", pg.cnt["MM"] if hq == 0 else vre_mm4[hr])]))
            for g in range(4):
                s_lo = g * 4
                pp = ghp % 2
                lgh = dma("sync", "LY", D_gh2[pp][:, :, :],
                          g_s.rearrange("(tt p) o -> p tt o", p=128)[:, 4 * g:4 * g + 4,
                                                                    hq * 128:hq * 128 + 512],
                          waits=(vg_bar if ghp < 2 else []) + [("SS", gh_ss[pp])])
                ghp += 1

                def emit_mm2(ent, s_lo=s_lo):
                    sl3, t0, hr2, sv, vv = ent
                    lastm = 0
                    for s0 in range(max(s_lo, t0), s_lo + 4):
                        rs = s0 - s_lo
                        lastm = mm(P[2 + rs][:, 0:129],
                                   D_st[sl3][:, rs * 128:(rs + 1) * 128],
                                   D_vre4[hr2][:, t0, 0:129], t0 == 0, s0 == t0,
                                   waits=[("SS", sv),
                                          ("VV", max(vv, po_ep[rs] if t0 == 0 else 0)),
                                          lvh[hr2]])
                    st_mm_free[sl3] = lastm
                    vre_mm4[hr2] = max(vre_mm4[hr2], lastm)

                for hr in range(4):
                    h = hq + hr
                    if h >= _dH:
                        continue
                    j, half = divmod(h, 2)
                    pb = half * 64
                    pend = []
                    for t0 in range(s_lo + 4):
                        sl3 = slot_i % 3
                        slot_i += 1
                        ps = P[MM1P[sl3]]
                        m = mm(ps[:, 0:512], krT[pb:pb + 64, j, t0 * 128:(t0 + 1) * 128],
                               qrT[pb:pb + 64, j, g * 512:(g + 1) * 512], True, True,
                               waits=[("SS", pa_s[sl3]), ("VV", pa_v[sl3])])
                        rel = t0 - s_lo
                        sv = vv = 0
                        if rel < 0:
                            sv = sop(lambda e, ps=ps, sl=sl3: e.activation(
                                D_st[sl][:, 0:512], ps[:, 0:512], AF.Copy),
                                waits=[("MM", max(m, st_mm_free[sl3]))])
                        else:
                            c0 = rel * 128
                            vv = vop(lambda e, ps=ps, sl=sl3, c0=c0: e.tensor_mul(
                                D_st[sl][:, c0:c0 + 128], ps[:, c0:c0 + 128], SMB_tri),
                                waits=[("MM", max(m, st_mm_free[sl3]))])
                            if c0 + 128 < 512:
                                vv = vop(lambda e, ps=ps, sl=sl3, c0=c0: e.tensor_copy(
                                    D_st[sl][:, c0 + 128:512], ps[:, c0 + 128:512]))
                        pa_s[sl3] = max(pa_s[sl3], sv)
                        pa_v[sl3] = max(pa_v[sl3], vv)
                        pend.append((sl3, t0, hr, sv, vv))
                        if tail_q:
                            tail_q.pop(0)()
                        if len(pend) > 2:
                            emit_mm2(pend.pop(0))
                    while pend:
                        emit_mm2(pend.pop(0))
                    if tail_q:
                        tail_q.pop(0)()
                    mfin = pg.cnt["MM"]
                    for rs in range(4):
                        po_ep[rs] = vop(lambda e, rs=rs: e.tensor_copy(
                            OrawR[:, rs, :], P[2 + rs][:, 0:129]),
                            waits=[("MM", mfin)] if rs == 0 else [])
                    csl4 = SM_call[:, h * 16 + s_lo: h * 16 + s_lo + 4]
                    tc, tn, wc = WK[:, 104:108], WK[:, 108:112], WK[:, 100:104]
                    vop(lambda e: e.tensor_copy(
                        R4, OrawR[:, :, 128:129].rearrange("p a c -> p (a c)")))
                    vop(lambda e, csl4=csl4: e.tensor_tensor(tc, csl4, R4, op=ALU.mult))
                    vop(lambda e: e.tensor_scalar(tn, tc, -1.0, None, op0=ALU.mult))
                    vop(lambda e: e.tensor_tensor(tc, tc, tn, op=ALU.max))
                    vop(lambda e: e.tensor_scalar(tc, tc, 1.0, None, op0=ALU.max))
                    vop(lambda e: e.reciprocal(tc, tc))
                    vop(lambda e, csl4=csl4: e.tensor_mul(wc, csl4, tc))
                    for rs in range(4):
                        idx = hr * 4 + rs
                        mus = WK[:, idx:idx + 1]
                        sqs = WK[:, 16 + idx:17 + idx]
                        vop(lambda e, rs=rs, idx=idx, mus=mus: e.tensor_scalar(
                            D_owb[:, idx, :], OrawR[:, rs, 0:128], wc[:, rs:rs + 1], 0.0,
                            op0=ALU.mult, op1=ALU.add, accum_out=mus))
                        vop(lambda e, idx=idx, sqs=sqs: e.scalar_tensor_tensor(
                            WK[:, 128:256], D_owb[:, idx, :], 1.0, D_owb[:, idx, :],
                            op0=ALU.mult, op1=ALU.mult, accum_out=sqs))
                nm16, var16, rstd16, e16 = WK[:, 32:48], WK[:, 48:64], WK[:, 64:80], WK[:, 80:96]
                vop(lambda e: e.tensor_scalar(nm16, WK[:, 0:16], -1.0 / 128.0, None, op0=ALU.mult))
                vop(lambda e: e.tensor_tensor(e16, WK[:, 0:16], WK[:, 0:16], op=ALU.mult))
                vop(lambda e: e.tensor_scalar(e16, e16, -1.0 / 16384.0, None, op0=ALU.mult))
                vop(lambda e: e.scalar_tensor_tensor(
                    var16, WK[:, 16:32], 1.0 / 128.0, e16, op0=ALU.mult, op1=ALU.add))
                vv3 = vop(lambda e: e.tensor_scalar(var16, var16, 1e-5, None, op0=ALU.add))
                s2 = sop(lambda e: e.activation(rstd16, var16, AF.Ln), waits=[("VV", vv3)])
                s3 = sop(lambda e: e.activation(rstd16, rstd16, AF.Exp, scale=-0.5))
                gh_ss[pp] = s4 = sop(lambda e, pp=pp: e.activation(
                    SG_all, D_gh2[pp][:, :, :].rearrange("p q d -> p (q d)"), AF.Silu),
                    waits=[lgh, ("VV", pg.cnt["VV"])])
                pp2 = (ghp - 1) % 2
                first_p2 = True
                ucell = {"first": True}
                for hr in range(4):
                    h = hq + hr
                    if h >= _dH:
                        continue
                    for rs in range(4):
                        idx = hr * 4 + rs
                        ctr = WK[:, 256:384]
                        vop(lambda e, idx=idx: e.tensor_scalar(
                            ctr, D_owb[:, idx, :], nm16[:, idx:idx + 1], None, op0=ALU.add),
                            waits=[("SS", s3)] if first_p2 else [])
                        v13 = vop(lambda e, idx=idx, rs=rs, hr=hr: e.scalar_tensor_tensor(
                            gat_all[:, idx * 128:(idx + 1) * 128], ctr, rstd16[:, idx:idx + 1],
                            SG_all[:, rs * 512 + hr * 128: rs * 512 + (hr + 1) * 128],
                            op0=ALU.mult, op1=ALU.mult),
                            waits=[("SS", s4)] + ([("MM", tr_last)] if first_p2 else []))
                        first_p2 = False

                        def t_go(idx=idx, hr=hr, rs=rs, pp2=pp2, v13=v13, ucell=ucell):
                            nonlocal tr_last
                            sl4 = idx % 4
                            pt = P[6][:, sl4 * 64:(sl4 + 1) * 64].bitcast(BF16)
                            pmt4[sl4] = tr_last = tr(
                                pt, gat_all[:, idx * 128:(idx + 1) * 128], SMB_id,
                                waits=[("VV", v13), ("VV", max(vg2_4))])
                            wfl = []
                            if ucell["first"]:
                                ucell["first"] = False
                                wfl = gacc_fl[pp2]
                                gacc_fl[pp2] = []
                            vg2_4[sl4] = vop(lambda e, pt=pt, hr=hr, rs=rs, pp2=pp2: e.tensor_copy(
                                D_gacc[pp2][:, hr * 512 + rs * 128: hr * 512 + (rs + 1) * 128], pt),
                                waits=[("MM", pmt4[sl4])] + wfl)
                        tail_q.append(t_go)

                    def f_go(h=h, g=g, hr=hr, pp2=pp2):
                        gacc_fl[pp2].append(dma(
                            "sync", "LY", gT_s[h * 128:(h + 1) * 128, g * 512:(g + 1) * 512],
                            D_gacc[pp2][:, hr * 512:(hr + 1) * 512],
                            waits=[("VV", vg2_4[(hr * 4 + 3) % 4])]))
                    tail_q.append(f_go)

        while tail_q:
            tail_q.pop(0)()
        d_done = pg.snap()

        if stop == 'd0':
            continue
        D_gstat = [Dg[:, i * 2048:(i + 1) * 2048].rearrange("p (g s) -> p g s", s=128)
                   for i in range(2)]
        gst_mm = [0, 0]
        pc_v = 0
        xf_l2 = ZT
        prev_l1 = ZT
        prev_l2 = ZT
        for tt in range(NT):
            pp = tt % 2
            lg2 = dma("sync", "LY", D_gstat[pp],
                      gT_s[:, tt * 128:(tt + 1) * 128].rearrange("(g p) s -> p g s", p=128),
                      waits=[("MM", gst_mm[pp])] + pg.wsnap(d_done))
            lx = dma("sync", "LY", WK[:, 1024:2048], x4[b][tt * 128:(tt + 1) * 128, :])
            mlast = 0
            for dgt in range(16):
                for h2 in range(2):
                    w = ([lg2, lwo, ("VV", pc_v)] + pg.wsnap(d_done)) if dgt == 0 else ()
                    mlast = mm(P[4 + h2][:, 0:512], D_gstat[pp][:, dgt, :],
                               E_wo[:, dgt, h2 * 512:(h2 + 1) * 512], dgt == 0, dgt == 15,
                               waits=w)
            gst_mm[pp] = mlast
            rbuf = WK[:, 0:1024]
            vop(lambda e: e.tensor_add(rbuf[:, 0:512], P[4][:, 0:512], REPF[:, 0:512]),
                waits=[("MM", mlast), prev_l1] + pg.wsnap(d_done))
            pc_v = vop(lambda e: e.tensor_add(rbuf[:, 512:1024], P[5][:, 0:512], REPF[:, 512:1024]))
            e3 = vop(lambda e: e.tensor_add(rbuf, rbuf, WK[:, 1024:2048]), waits=[lx])
            nm, var, rstd, sqs = SMc(7), SMc(8), SMc(9), SMc(10)
            vop(lambda e, nm=nm: e.reduce_sum(nm, rbuf, axis=AX.X, negate=True), waits=[("VV", e3)])
            vop(lambda e, nm=nm: e.tensor_scalar(nm, nm, 1.0 / 1024.0, None, op0=ALU.mult))
            v3 = vop(lambda e, nm=nm: e.tensor_scalar(rbuf, rbuf, nm, None, op0=ALU.add))
            s1 = vop(lambda e, sqs=sqs: e.scalar_tensor_tensor(
                WK[:, 1024:2048], rbuf, 1.0, rbuf, op0=ALU.mult, op1=ALU.mult, accum_out=sqs),
                waits=[("VV", v3)])
            v4 = vop(lambda e, var=var, sqs=sqs: e.tensor_scalar(var, sqs, 1.0 / 1024.0, 1e-5,
                                                                 op0=ALU.mult, op1=ALU.add))
            s2 = sop(lambda e, rstd=rstd, var=var: e.activation(rstd, var, AF.Ln), waits=[("VV", v4)])
            s3 = sop(lambda e, rstd=rstd: e.activation(rstd, rstd, AF.Exp, scale=-0.5))
            v5 = vop(lambda e, rstd=rstd: e.tensor_scalar(rbuf, rbuf, rstd, None, op0=ALU.mult),
                     waits=[("SS", s3)])
            v6 = vop(lambda e: e.tensor_copy(D_xfb[:], rbuf), waits=[prev_l2])
            prev_l1 = dma("sync", "LY", xf32_s[b][tt * 128:(tt + 1) * 128, :], rbuf,
                          waits=[("VV", v5)])
            prev_l2 = xf_l2 = dma("sync", "LY",
                                  xfb_s[b * S + tt * 128: b * S + (tt + 1) * 128, :], D_xfb,
                                  waits=[("VV", v6)])
        if b + 1 < nstep_run:
            pre_w[b + 1] = (
                dma("gpsimd", "LG", wq_sb, wqp.rearrange("(dt p) o -> p dt o", p=128),
                    waits=[("MM", max(gst_mm))]),
                dma("gpsimd", "LG", wk_sb, wkp.rearrange("(dt p) o -> p dt o", p=128)))
        xfb_bar = pg.qbar("LY")
        xfT_mm = [0, 0]
        mm_d = pg.cnt["MM"]
        for dt in range(8):
            pp = dt % 2
            lt = dma("scalar", "LS", D_xfT[pp], xfb_s[b * S:(b + 1) * S, dt * 128:(dt + 1) * 128],
                     waits=xfb_bar + [("MM", xfT_mm[pp] if dt >= 2 else mm_d)], transpose=True)
            for sl in range(4):
                xfT_mm[pp] = mm(P[sl][0:8, 0:512], SMB_rw[:, dt, :],
                                D_xfT[pp][:, sl * 512:(sl + 1) * 512], dt == 0, dt == 7,
                                waits=[lt, lrw])
        mrt = pg.cnt["MM"]
        lgTrow = WK[0:8, 0:2048]
        lgv = 0
        for sl in range(4):
            lgv = vop(lambda e, sl=sl: e.tensor_copy(lgTrow[:, sl * 512:(sl + 1) * 512],
                                                     P[sl][0:8, 0:512]),
                      waits=[("MM", mrt)])
        lgsb = D_lgf[:, 0:128].rearrange("p (tt e) -> p tt e", e=8)
        lgev = 0
        for tt in range(NT):
            mt = tr(P[6][:, 0:8], lgTrow[:, tt * 128:(tt + 1) * 128], SM_id8[0:8, :],
                    waits=[("VV", lgv), ("VV", lgev)])
            lgev = vop(lambda e, tt=tt: e.tensor_copy(lgsb[:, tt, :], P[6][:, 0:8]),
                       waits=[("MM", mt)])
        l3 = dma("sync", "LY", lg_s[b], lgsb.rearrange("p tt e -> p (tt e)"), waits=[("VV", lgev)])
        if stop == 'ef0':
            continue
        gop(lambda e, b=b: e.collective_compute(
            "AllGather", ALU.bypass, replica_groups=[list(range(NCORE))],
            ins=[lg_s[b].opt()],
            outs=[lgAG[b * NCORE * 128:(b + 1) * NCORE * 128, :].opt()]),
            waits=[l3], inc=("CC", 1))
        gop(lambda e, b=b: e.collective_compute(
            "AllGather", ALU.bypass, replica_groups=[list(range(NCORE))],
            ins=[xfb_s[b * S:(b + 1) * S, :].opt()],
            outs=[xfAG[b * NCORE * S:(b + 1) * NCORE * S, :].opt()]),
            waits=xfb_bar, inc=("CC", 1))
        ag_n += 2
        prev_step = pg.snap()

    ag_attn = ag_n
    attn_snap = pg.snap()

    def _finish_early():
        sl_ = dma("sync", "LY", out_c[0][0:128, :], REPF[:], waits=pg.wsnap(pg.snap()))
        pg.emit("sync", lambda e: e.nop(), waits=[sl_])
        pg.realize()
        return nc

    if stop is not None and stop != "attn" and (
            stop in ("p0", "attn0", "a0", "b0", "c0", "d0", "ef0")
            or stop.startswith("dh") or stop.startswith("dm") or stop in ("dep", "dskip", "dld", "dsa", "dva")):
        return _finish_early()
    if "xf" in dbg:
        dma("sync", "LY", dbg["xf"], xf32_s.rearrange("b s d -> (b s) d"), waits=pg.wsnap(attn_snap))
    if "logits" in dbg:
        dma("sync", "LY", dbg["logits"].rearrange("b p x -> (b p) x"), lg_s.rearrange("b p x -> (b p) x"), waits=pg.wsnap(attn_snap))
    if stop == "attn":
        return _finish_early()

    L = C[:, 0:8192].bitcast(F32).rearrange("p (t e) -> p t e", e=8)
    ind1 = C[:, 8192:12288].rearrange("p (t e) -> p t e", e=8)
    ind2 = C[:, 12288:16384].rearrange("p (t e) -> p t e", e=8)
    lL_toks = []
    for bb in range(NSTEP):
        for rr in range(NCORE):
            lL_toks.append(dma("sync", "LY", L[:, (bb * 8 + rr) * 16:(bb * 8 + rr + 1) * 16, :],
                     lgAG[(bb * 8 + rr) * 128:(bb * 8 + rr + 1) * 128, :].rearrange(
                         "p (tw e) -> p tw e", e=8),
                     waits=([("CC", max(ag_attn - 1, 0))] + pg.wsnap(attn_snap, ("CC",))) if (bb == 0 and rr == 0) else ()))
    Dx = Dg[:].bitcast(F32)
    m1, m2 = Dx[:, 0:512], Dx[:, 512:1024]
    i1f, i2f = Dx[:, 1024:1536], Dx[:, 1536:2048]
    wt1, wt2 = Dx[:, 2048:2560], Dx[:, 2560:3072]
    scr = Dx[:, 3072:7168]
    scr2v = scr.rearrange("p (t e) -> p t e", e=8)
    se = Dx[:, 7168:7680]
    bc = lambda ap: ap.rearrange("p (t a) -> p t a", a=1).broadcast_to([128, 512, 8])
    i8bc = SM_i8.rearrange("p (a e) -> p a e", a=1).broadcast_to([128, 512, 8])
    rii8bc = SM_rii8.rearrange("p (a e) -> p a e", a=1).broadcast_to([128, 512, 8])
    ohbc = SM_oh.rearrange("p (a e) -> p a e", a=1).broadcast_to([128, 512, 8])
    vop(lambda e: e.reduce_max(m1, L, axis=AX.X), waits=lL_toks + pg.wsnap(attn_snap, ("CC",)))
    vop(lambda e: e.tensor_tensor(scr2v, L, bc(m1), op=ALU.is_equal))
    vop(lambda e: e.tensor_mul(scr2v, scr2v, rii8bc))
    vop(lambda e: e.reduce_max(i1f, scr2v, axis=AX.X))
    vop(lambda e: e.tensor_tensor(scr2v, L, bc(m1), op=ALU.is_equal))
    vop(lambda e: e.tensor_scalar(scr2v, scr2v, 1e30, None, op0=ALU.mult))
    vop(lambda e: e.tensor_tensor(scr2v, L, scr2v, op=ALU.subtract))
    vop(lambda e: e.reduce_max(m2, scr2v, axis=AX.X))
    vop(lambda e: e.tensor_tensor(scr2v, scr2v, bc(m2), op=ALU.is_equal))
    vop(lambda e: e.tensor_mul(scr2v, scr2v, rii8bc))
    vop(lambda e: e.reduce_max(i2f, scr2v, axis=AX.X))
    vop(lambda e: e.tensor_scalar(i1f, i1f, -1.0, 99.0, op0=ALU.mult, op1=ALU.add))
    vop(lambda e: e.tensor_scalar(i2f, i2f, -1.0, 99.0, op0=ALU.mult, op1=ALU.add))
    v14 = vop(lambda e: e.tensor_tensor(scr2v, L, bc(m1), op=ALU.subtract))
    s1 = sop(lambda e: e.activation(scr, scr, AF.Exp), waits=[("VV", v14)])
    vop(lambda e: e.reduce_sum(se, scr2v, axis=AX.X), waits=[("SS", s1)])
    vop(lambda e: e.reciprocal(wt1, se))
    v17 = vop(lambda e: e.tensor_tensor(wt2, m2, m1, op=ALU.subtract))
    s2 = sop(lambda e: e.activation(wt2, wt2, AF.Exp), waits=[("VV", v17)])
    vop(lambda e: e.tensor_mul(wt2, wt2, wt1), waits=[("SS", s2)])
    vop(lambda e: e.tensor_tensor(ind1, bc(i1f), i8bc, op=ALU.is_equal))
    ind_done = vop(lambda e: e.tensor_tensor(ind2, bc(i2f), i8bc, op=ALU.is_equal))

    ranks = A[:].bitcast(F32).rearrange("p (t e) -> p t e", e=16)
    rk_ev = 0
    cs_ev = 0
    cs_toks = []
    for Gi in range(4):
        mmax = 0
        for trel in range(128):
            tile = Gi * 128 + trel
            bank = trel // 32
            col = (trel % 32) * 16
            for kk, indv in enumerate((ind1, ind2)):
                w = ([("VV", ind_done), ("VV", rk_ev)] + pg.wsnap(attn_snap, ("CC",))) if (trel == 0 and kk == 0) else ()
                mmax = mm(P[bank][:, col + kk * 8: col + kk * 8 + 8], SMB_su,
                          indv[:, tile, :], True, True, waits=w)
        for q4 in range(4):
            rk_ev = vop(lambda e, Gi=Gi, q4=q4: e.tensor_copy(
                ranks[:, Gi * 128 + q4 * 32:Gi * 128 + (q4 + 1) * 32, :],
                P[q4][:, 0:512].rearrange("p (t e) -> p t e", e=16)),
                waits=[("MM", mmax)])
        for h16 in range(4):
            mc = 0
            for tr2 in range(32):
                tile = Gi * 128 + h16 * 32 + tr2
                for kk, indv in enumerate((ind1, ind2)):
                    w = [("VV", cs_ev)] if (tr2 == 0 and kk == 0) else ()
                    mc = mm(P[4][0:1, tr2 * 16 + kk * 8: tr2 * 16 + kk * 8 + 8],
                            SMB_onesc, indv[:, tile, :], True, True, waits=w)
            cs_ev = vop(lambda e, h16=h16: e.tensor_copy(WK[0:1, h16 * 512:(h16 + 1) * 512],
                                                         P[4][0:1, 0:512]),
                        waits=[("MM", mc), (cs_toks[-1] if cs_toks else ("MM", 0))])
        cs_toks.append(dma("sync", "LY", cstage[Gi * 2048:(Gi + 1) * 2048].rearrange("(a b) -> a b", a=1),
                           WK[0:1, 0:2048], waits=[("VV", cs_ev)]))
    cs2 = REPB[:, 0:64]
    lc2 = dma("gpsimd", "LG", cs2.rearrange("p (t1 ke) -> p t1 ke", ke=16),
              cstage.rearrange("(t1 p ke) -> p t1 ke", p=128, ke=16),
              waits=cs_toks + pg.wsnap(attn_snap, ("CC",)))
    mb = mm(P[5][:, 0:64], SMB_su, cs2, True, False, waits=[lc2])
    mtb = mm(P[4][0:1, 0:64], SMB_onesc, cs2, True, True)
    tb, bp = WK2[0:1, 64:128], WK2[0:1, 128:192]
    vtb = vop(lambda e: e.tensor_copy(tb, P[4][0:1, 0:64]), waits=[("MM", mtb)])
    vop(lambda e: e.memset(bp[:, 0:16], 0.0))
    vop(lambda e: e.tensor_copy(bp[:, 16:32], tb[:, 0:16]), waits=[("VV", vtb)])
    vop(lambda e: e.tensor_add(bp[:, 32:48], bp[:, 16:32], tb[:, 16:32]))
    vb3 = vop(lambda e: e.tensor_add(bp[:, 48:64], bp[:, 32:48], tb[:, 32:48]))
    mb2 = mm(P[5][:, 0:64], ONESF[:], bp, False, True, waits=[("VV", vb3)])
    vbase = vop(lambda e: e.tensor_copy(WK2[:, 192:256], P[5][:, 0:64]), waits=[("MM", mb2)])
    lbs = dma("sync", "LY", bstage.rearrange("(t1 p ke) -> p t1 ke", p=128, ke=16),
              WK2[:, 192:256].rearrange("p (t1 ke) -> p t1 ke", ke=16), waits=[("VV", vbase)])
    radd_ev = 0
    prev_lbr2 = ("MM", 0)
    for Gi in range(4):
        lbr2 = dma("sync", "LY", WK2[0:1, 0:2048],
                   bstage[Gi * 2048:(Gi + 1) * 2048].rearrange("(a b) -> a b", a=1),
                   waits=[lbs, prev_lbr2, ("MM", pg.cnt["MM"])])
        prev_lbr2 = lbr2
        for q4 in range(4):
            mq = mm(P[q4][:, 0:512], ONESF[:], WK2[0:1, q4 * 512:(q4 + 1) * 512],
                    True, True, waits=[lbr2, ("VV", rk_ev), ("VV", radd_ev)])
            radd_ev = vop(lambda e, Gi=Gi, q4=q4: e.tensor_add(
                A[:].bitcast(F32)[:, Gi * 2048 + q4 * 512:Gi * 2048 + (q4 + 1) * 512],
                A[:].bitcast(F32)[:, Gi * 2048 + q4 * 512:Gi * 2048 + (q4 + 1) * 512],
                P[q4][:, 0:512]), waits=[("MM", mq)])
    ranks_done = radd_ev

    kept_k, vv_, pp_, jj_ = WK2[:, 0:512], WK2[:, 512:1024], WK2[:, 1024:1536], WK2[:, 1536:2048]
    yr, wfin = se, m2
    stash = WK[:, 0:2048].rearrange("p (t f) -> p t f", f=4)
    yrw_ev = 0
    for kk, (indv, iv, wv_) in enumerate(((ind1, i1f, wt1), (ind2, i2f, wt2))):
        rk = Dx[:, 7680:8192]
        vop(lambda e, indv=indv, kk=kk: e.tensor_tensor(
            scr2v, ranks[:, :, kk * 8:(kk + 1) * 8], indv, op=ALU.mult),
            waits=[("VV", ranks_done), ("VV", yrw_ev)])
        vop(lambda e, rk=rk: e.reduce_sum(rk, scr2v, axis=AX.X))
        vop(lambda e, rk=rk: e.tensor_scalar(kept_k, rk, 8192.0, None, op0=ALU.is_lt))
        vop(lambda e, rk=rk, kk=kk: e.tensor_scalar(vv_, rk, float(kk * CAP), None, op0=ALU.add))
        vi32 = scr[:, 0:512].bitcast(I32)
        pi32 = scr[:, 512:1024].bitcast(I32)
        vop(lambda e, vi32=vi32: e.tensor_copy(vi32, vv_))
        vop(lambda e, vi32=vi32, pi32=pi32: e.tensor_scalar(pi32, vi32, 2047, None, op0=ALU.bitwise_and))
        vop(lambda e, pi32=pi32: e.tensor_copy(pp_, pi32))
        vop(lambda e, vi32=vi32, pi32=pi32: e.tensor_scalar(pi32, vi32, 11, None, op0=ALU.arith_shift_right))
        vop(lambda e, pi32=pi32: e.tensor_copy(jj_, pi32))
        vop(lambda e: e.tensor_scalar(yr, jj_, 16384.0, None, op0=ALU.mult))
        vop(lambda e, iv=iv: e.scalar_tensor_tensor(yr, iv, 2048.0, yr, op0=ALU.mult, op1=ALU.add))
        vop(lambda e: e.tensor_add(yr, yr, pp_))
        vop(lambda e: e.tensor_mul(yr, yr, kept_k))
        vop(lambda e, wv_=wv_: e.tensor_mul(wfin, wv_, kept_k))
        vop(lambda e, kk=kk: e.tensor_copy(stash[:, 0:512, kk:kk + 1],
                                           yr.rearrange("p (t a) -> p t a", a=1)))
        yrw_ev = vop(lambda e, kk=kk: e.tensor_copy(stash[:, 0:512, 2 + kk:3 + kk],
                                                    wfin.rearrange("p (t a) -> p t a", a=1)))
    lyrw = dma("sync", "LY", yrw_s.rearrange("(p t) f -> p t f", p=128), stash,
               waits=[("VV", yrw_ev)])

    i128bc4 = SM_i128.rearrange("p (a r) -> p a r", a=1).broadcast_to([128, 4, 128])
    PTb = E[:, 0:512].rearrange("p (t r) -> p t r", r=128)
    CLO, CHI = E[:, 1024:1536], E[:, 1536:2048]
    CT = WK2[:].bitcast(BF16)
    prev_kk = ("MM", 0)
    sent_v = vop(lambda e: e.memset(SMc(15)[0:1, :], 999999.0), waits=[("VV", yrw_ev)])
    col_toks = []
    for kk, indv in enumerate((ind1, ind2)):
        roe, sel = Dx[:, 7680:8192], Dx[:, 0:512]
        va = vop(lambda e, kk=kk: e.tensor_tensor(
            scr2v, ranks[:, :, kk * 8:(kk + 1) * 8], ohbc, op=ALU.mult),
            waits=[("VV", yrw_ev), lyrw, prev_kk])
        vop(lambda e, roe=roe: e.reduce_sum(roe, scr2v, axis=AX.X))
        vop(lambda e, indv=indv: e.tensor_tensor(scr2v, indv, ohbc, op=ALU.mult))
        selv = vop(lambda e, sel=sel: e.reduce_sum(sel, scr2v, axis=AX.X))
        lrow = dma("sync", "LY", rowstage[kk * 1024:kk * 1024 + 512].rearrange("(a b) -> a b", a=1),
                   roe[0:1, :], waits=[("VV", selv)])
        lsen = dma("sync", "LY", rowstage[kk * 1024 + 512:kk * 1024 + 513].rearrange("(a b) -> a b", a=1),
                   SMc(15)[0:1, :], waits=[("VV", sent_v)])
        for j in range(4):
            col_toks.append(dma("sync", "LY", SM_base[kk][:, j:j + 1],
                                rowstage[kk * 1024 + j * 128:kk * 1024 + (j + 1) * 128].rearrange("(p a) -> p a", a=1),
                                waits=[lrow]))
            col_toks.append(dma("sync", "LY", SM_basep1[kk][:, j:j + 1],
                                rowstage[kk * 1024 + j * 128 + 1:kk * 1024 + (j + 1) * 128 + 1].rearrange("(p a) -> p a", a=1),
                                waits=[lrow, lsen]))
        vop(lambda e, kk=kk: e.tensor_sub(SM_cnt[kk], SM_basep1[kk], SM_base[kk]),
            waits=col_toks[-8:])
        rof = Dx[:, 512:1024]
        mK = mm(P[6][:, 0:512], ONESF[:], roe[0:1, 0:512], True, True, waits=[("VV", selv)])
        vf = vop(lambda e, roe=roe, rof=rof: e.tensor_tensor(rof, roe, P[6][:, 0:512], op=ALU.subtract),
                 waits=[("MM", mK)])
        rofb, selb = C[:, 0:512], C[:, 512:1024]
        vop(lambda e, rof=rof: e.tensor_copy(rofb, rof), waits=[("VV", vf)])
        vh2 = vop(lambda e, sel=sel: e.tensor_copy(selb, sel))
        pt_ev = 0
        clo_ev = 0
        for g4 in range(128):
            rofs = rofb.rearrange("p (t a) -> p t a", a=1)[:, g4 * 4:(g4 + 1) * 4, :].broadcast_to([128, 4, 128])
            sels = selb.rearrange("p (t a) -> p t a", a=1)[:, g4 * 4:(g4 + 1) * 4, :].broadcast_to([128, 4, 128])
            vop(lambda e, rofs=rofs: e.tensor_tensor(PTb, i128bc4, rofs, op=ALU.is_equal),
                waits=[("VV", vh2), ("MM", pg.cnt["MM"])])
            pt_ev = vop(lambda e, sels=sels: e.tensor_tensor(PTb, PTb, sels, op=ALU.mult))
            mc2 = 0
            for t4 in range(4):
                tile = g4 * 4 + t4
                col = (tile % 32) * 4
                mc2 = mm(P[7][:, col:col + 2], PTb[:, t4, :], IDC[:, tile * 2:tile * 2 + 2],
                         True, True, waits=[("VV", pt_ev), ("VV", clo_ev if (tile % 32) == 0 else 0)])
            if g4 % 8 == 7:
                f0 = (g4 // 8) * 32
                pv = P[7][:, 0:128].rearrange("p (t f) -> p t f", f=4)
                clo_ev = vop(lambda e, f0=f0, pv=pv: e.tensor_copy(
                    CLO[:, f0:f0 + 32].rearrange("p (t a) -> p t a", a=1),
                    pv[:, 0:32, 0:1]), waits=[("MM", mc2)])
                clo_ev = vop(lambda e, f0=f0, pv=pv: e.tensor_copy(
                    CHI[:, f0:f0 + 32].rearrange("p (t a) -> p t a", a=1),
                    pv[:, 0:32, 1:2]))
        ct_prev = 0
        for g2 in range(4):
            for plane, srcb in enumerate((CLO, CHI)):
                mt = tr(P[6][:, 0:64].bitcast(BF16), srcb[:, g2 * 128:(g2 + 1) * 128], SMB_id,
                        waits=[("VV", clo_ev), ("VV", ct_prev)])
                ct_prev = vop(lambda e, g2=g2, plane=plane: e.tensor_copy(
                    CT[:, (plane * 4 + g2) * 128:(plane * 4 + g2 + 1) * 128],
                    P[6][:, 0:64].bitcast(BF16)), waits=[("MM", mt)])
        ctc = WK[:, 0:512].rearrange("p (g r) -> p g r", r=128)
        ctlo = CT[:, 0:512].rearrange("p (g r) -> p g r", r=128)
        cthi = CT[:, 512:1024].rearrange("p (g r) -> p g r", r=128)
        vcm = vop(lambda e: e.scalar_tensor_tensor(ctc, cthi, 256.0, ctlo, op0=ALU.mult, op1=ALU.add),
                  waits=[("VV", ct_prev), lyrw])
        cti = WK[:, 512:1024].bitcast(I32).rearrange("p (g r) -> p g r", r=128)
        vci = vop(lambda e: e.tensor_copy(cti, ctc), waits=[("VV", vcm)])
        prev_kk = dma("sync", "LY", ctflat[kk].rearrange("(g p r) a -> p g (r a)", p=128, r=128),
                      cti, waits=[("VV", vci)])
        col_toks.append(prev_kk)
    idx_done = pg.snap()
    if stop == "index":
        return _finish_early()

    w1lo = Bg[:].rearrange("p (dt f) -> p dt f", dt=4)
    w1hi = E[:].rearrange("p (dt f) -> p dt f", dt=4)
    w2sb = C[:].rearrange("p (ft d) -> p ft d", ft=16)
    w2sb2 = A[:].rearrange("p (ft d) -> p ft d", ft=16)
    lw1a = dma("gpsimd", "LG", w1lo, w1c[0:512, :].rearrange("(dt p) f -> p dt f", p=128),
               waits=pg.wsnap(idx_done, ("CC",)))
    lw1b = dma("gpsimd", "LG", w1hi, w1c[512:1024, :].rearrange("(dt p) f -> p dt f", p=128),
               waits=pg.wsnap(idx_done, ("CC",)))
    lw2a = dma("gpsimd", "LG", w2sb, w2c[0:2048, :].rearrange("(ft p) d -> p ft d", p=128),
               waits=pg.wsnap(idx_done, ("CC",)))
    lw2b = dma("gpsimd", "LG", w2sb2, w2c[2048:4096, :].rearrange("(ft p) d -> p ft d", p=128),
               waits=pg.wsnap(idx_done, ("CC",)))

    def w1slice(dt, f0, f1):
        return (w1lo if dt < 4 else w1hi)[:, dt % 4, f0:f1]

    def w2slice(ft, d0, d1):
        return (w2sb if ft < 16 else w2sb2)[:, ft % 16, d0:d1]

    hT_sb = Dg[:].rearrange("p (ft s) -> p ft s", ft=32)
    xeT = [Fg[:, i * 4096:(i + 1) * 4096].rearrange("p (dt s) -> p dt s", dt=8) for i in range(2)]
    gb = W2b[:, 0:2048]
    NCH = 32
    gat_l = [ZT, ZT, ZT, ZT]
    yag_toks = []
    xe_mm = [0, 0]
    xet_marks = {}
    FS = dict(ctm=0, cti=0, h_ev=0, y_ev=0, hT_mm=0,
              h_evs=[0, 0, 0], y_evs=[0, 0, 0])

    def ffn_gather(ch):
        kk = ch // 16
        j0 = (ch % 16) * 4
        pp = ch % 2
        cst_toks = []
        for q in range(4):
            jslot = j0 + q
            icol, icol2 = q * 2, q * 2 + 1
            tj = WK[:, 0:128]
            vtj = vop(lambda e, jslot=jslot, tj=tj: e.tensor_scalar(
                tj, SM_i128, float(jslot * 128), None, op0=ALU.add),
                waits=(col_toks if (ch == 0 and q == 0) else [("MM", FS['ctm'])]))
            tjb = tj.rearrange("p (a s) -> p a s", a=1).broadcast_to([128, 4, 128])
            bbc = SM_base[kk].rearrange("p (t a) -> p t a", a=1).broadcast_to([128, 4, 128])
            bp1c = SM_basep1[kk].rearrange("p (t a) -> p t a", a=1).broadcast_to([128, 4, 128])
            cntc = SM_cnt[kk].rearrange("p (t a) -> p t a", a=1).broadcast_to([128, 4, 128])
            B4 = W2b[:, 2048:2560].rearrange("p (t s) -> p t s", s=128)
            C4 = W2b[:, 2560:3072].rearrange("p (t s) -> p t s", s=128)
            vb4 = vop(lambda e, tjb=tjb, bbc=bbc, B4=B4: e.tensor_tensor(B4, tjb, bbc, op=ALU.is_ge))
            vc4 = vop(lambda e, tjb=tjb, bp1c=bp1c, C4=C4: e.tensor_tensor(C4, tjb, bp1c, op=ALU.is_ge))
            vc5 = vop(lambda e, C4=C4, cntc=cntc: e.tensor_tensor(C4, C4, cntc, op=ALU.mult))
            for t1 in range(4):
                mB = mm(P[4][:, 0:1], B4[:, t1, :], SMB_onesc, t1 == 0, t1 == 3,
                        waits=([("VV", vc5), ("VV", FS['cti'])] if t1 == 0 else ()))
                FS['ctm'] = mm(P[5][:, 0:1], C4[:, t1, :], SMB_onesc, t1 == 0, t1 == 3)
            cidx = WK[:, 128:129]
            vop(lambda e, cidx=cidx: e.tensor_scalar(cidx, P[4][:, 0:1], 128.0, -128.0,
                                                     op0=ALU.mult, op1=ALU.add),
                waits=[("MM", FS['ctm'])])
            vop(lambda e, cidx=cidx: e.scalar_tensor_tensor(
                cidx, SM_iotac, 1.0, cidx, op0=ALU.mult, op1=ALU.add))
            vop(lambda e, cidx=cidx, jslot=jslot: e.tensor_scalar(
                cidx, cidx, float(jslot * 128), None, op0=ALU.add))
            vop(lambda e, cidx=cidx: e.tensor_tensor(cidx, cidx, P[5][:, 0:1], op=ALU.subtract))
            vop(lambda e, cidx=cidx: e.tensor_scalar(cidx, cidx, 0.0, 65535.0,
                                                     op0=ALU.max, op1=ALU.min))
            FS['cti'] = vop(lambda e, cidx=cidx, icol=icol: e.tensor_copy(ISL[:, icol:icol + 1], cidx),
                            waits=[gat_l[2 + q % 2]])
            gq1 = gop(lambda e, icol=icol, icol2=icol2, kk=kk: e.indirect_dma_start(
                out=ISL[:, icol2:icol2 + 1], out_offset=None, in_=ctflat[kk],
                in_offset=IndirectOffsetOnAxis(ISL[:, icol:icol + 1], 0)),
                waits=[("VV", FS['cti'])])
            gq = gop(lambda e, icol2=icol2, q=q: e.indirect_dma_start(
                out=gb[:, (q % 2) * 1024:(q % 2) * 1024 + 1024],
                out_offset=None, in_=xfAG,
                in_offset=IndirectOffsetOnAxis(ISL[:, icol2:icol2 + 1], 0)),
                waits=[gq1, ("CC", ag_attn), gat_l[q % 2]])
            gat_l[2 + q % 2] = gq
            gat_l[q % 2] = dma("sync", "LY", comp_s[ch * 512 + q * 128:ch * 512 + (q + 1) * 128, :],
                               gb[:, (q % 2) * 1024:(q % 2) * 1024 + 1024], waits=[gq])
            cst_toks.append(gat_l[q % 2])
        xm = []
        for dt in range(8):
            xm.append(dma("scalar", "LS", xeT[pp][:, dt, :],
                          comp_s[ch * 512:(ch + 1) * 512, dt * 128:(dt + 1) * 128],
                          waits=cst_toks + [("MM", xe_mm[pp])],
                          transpose=True))
        xet_marks[ch] = xm

    ST1B = (0, 1, 6)
    ST2B = (2, 3, 7)

    def ffn_compute(ch):
        nonlocal ag_n
        pp = ch % 2
        for fs in range(32):
            bk = fs % 3
            psp = P[ST1B[bk]]
            for dt in range(8):
                w = (xet_marks[ch] + [lw1a, lw1b, ("SS", FS['h_evs'][bk])]) if dt == 0 else ()
                m = mm(psp[:, 0:512], w1slice(dt, fs * 128, (fs + 1) * 128),
                       xeT[pp][:, dt, :], dt == 0, dt == 7, waits=w)
            xe_mm[pp] = m
            FS['h_evs'][bk] = FS['h_ev'] = sop(lambda e, fs=fs, psp=psp: e.activation(
                hT_sb[:, fs, :], psp[:, 0:512], AF.Gelu_apprx_tanh),
                waits=[("MM", m), ("MM", FS['hT_mm'] if fs < 2 else 0)])
        yst_l = [ZT, ZT]
        for st in range(4):
            yp = st % 2
            for dh in range(2):
                bk = (st * 2 + dh) % 3
                psy = P[ST2B[bk]]
                for fs in range(32):
                    w = ([("SS", FS['h_ev']), lw2a, lw2b, ("VV", FS['y_evs'][bk]),
                          yst_l[yp]] if fs == 0 else ())
                    m = mm(psy[:, 0:512], hT_sb[:, fs, st * 128:(st + 1) * 128],
                           w2slice(fs, dh * 512, (dh + 1) * 512), fs == 0, fs == 31, waits=w)
                FS['hT_mm'] = m
                FS['y_evs'][bk] = FS['y_ev'] = vop(lambda e, dh=dh, psy=psy, yp=yp: e.tensor_copy(
                    REPB[:, yp * 1024 + dh * 512: yp * 1024 + (dh + 1) * 512],
                    psy[:, 0:512]), waits=[("MM", m)])
            yst_l[yp] = dma("sync", "LY",
                            ycmp_s[ch * 512 + st * 128:ch * 512 + (st + 1) * 128, :],
                            REPB[:, yp * 1024:(yp + 1) * 1024], waits=[("VV", FS['y_ev'])])
            yag_toks.append(yst_l[yp])
        if ch % 4 == 3:
            jag = ch // 4
            gop(lambda e, jag=jag: e.collective_compute(
                "AllGather", ALU.bypass, replica_groups=[list(range(NCORE))],
                ins=[ycmp_s[jag * 2048:(jag + 1) * 2048, :].opt()],
                outs=[yAG[jag * NCORE * 2048:(jag + 1) * NCORE * 2048, :].opt()]),
                waits=list(yag_toks), inc=("CC", 1))
            ag_n += 1
            yag_toks.clear()

    IDCi = IDC[:].bitcast(I32)
    fpre = {}

    def pre_stage(jt):
        rw = WK2[:, 1536 + 4 * jt:1540 + 4 * jt]
        g1 = gop(lambda e, jt=jt, rw=rw: e.indirect_dma_start(
            out=rw, out_offset=None, in_=yrw_s,
            in_offset=IndirectOffsetOnAxis(OWN[:, jt:jt + 1], 0)),
            waits=[lyrw] + (pg.wsnap(idx_done, ("CC",)) if jt == 0 else []))
        r1v = vop(lambda e, rw=rw, jt=jt: e.tensor_copy(IDCi[:, 2 * jt:2 * jt + 1], rw[:, 0:1]),
                  waits=[g1] + (pg.wsnap(idx_done, ("CC",)) if jt == 0 else []))
        r2v = vop(lambda e, rw=rw, jt=jt: e.tensor_copy(IDCi[:, 2 * jt + 1:2 * jt + 2], rw[:, 1:2]))
        fpre[jt] = (r1v, r2v)

    ffn_gather(0)
    for ch in range(NCH):
        if ch + 1 < NCH:
            ffn_gather(ch + 1)
        ffn_compute(ch)
        pre_stage(2 * ch)
        pre_stage(2 * ch + 1)
    y_ev = FS['y_ev']
    if "ycmp" in dbg:
        dma("sync", "LY", dbg["ycmp"], ycmp_s, waits=pg.qbar("LY") + [("VV", y_ev)])
    ffn_snap = pg.snap()
    if stop == "ffn":
        return _finish_early()

    lfb = dma("sync", "LY", WK2[0:1, 0:1024], fbi.rearrange("(a d) -> a d", a=1),
              waits=pg.wsnap(ffn_snap, ("CC",)))
    ovf = vop(lambda e: e.memset(WK2[0:1, 1024:1152], 1.0), waits=pg.wsnap(ffn_snap, ("CC",)))
    repf_ev = 0
    for sl in range(2):
        m = mm(P[6][:, 0:512], WK2[0:1, 1024:1152], WK2[0:1, sl * 512:(sl + 1) * 512],
               True, True, waits=[lfb, ("VV", ovf), ("VV", repf_ev)])
        repf_ev = vop(lambda e, sl=sl: e.tensor_copy(REPF[:, sl * 512:(sl + 1) * 512], P[6][:, 0:512]),
                      waits=[("MM", m)])
    NP = 4
    SKEW = 2
    Af = A[:].bitcast(F32)
    Bf = Bg[:].bitcast(F32)
    ffw = [("MM", pg.cnt["MM"]), ("VV", pg.cnt["VV"]), ("SS", pg.cnt["SS"])]
    st_l = [ZT] * NP
    fmarks = {}

    def fin_gather(jt):
        b, tt = divmod(jt, NT)
        p = jt % NP
        first = [] if jt >= NP else ffw
        yb = Dg[:, p * 2048:(p + 1) * 2048]
        xb = Af[:, p * 1024:(p + 1) * 1024]
        g2 = gop(lambda e, yb=yb, jt=jt: e.indirect_dma_start(
            out=yb[:, 0:1024], out_offset=None, in_=yAG,
            in_offset=IndirectOffsetOnAxis(IDCi[:, 2 * jt:2 * jt + 1], 0)),
            waits=[("VV", fpre[jt][0]), ("CC", ag_n), st_l[p]] + first)
        g3 = gop(lambda e, yb=yb, jt=jt: e.indirect_dma_start(
            out=yb[:, 1024:2048], out_offset=None, in_=yAG,
            in_offset=IndirectOffsetOnAxis(IDCi[:, 2 * jt + 1:2 * jt + 2], 0)),
            waits=[("VV", fpre[jt][1])])
        lx2 = dma("sync", "LY", xb, xf32_s[b][tt * 128:(tt + 1) * 128, :],
                  waits=[st_l[p]] + first)
        fmarks[jt] = (g2, g3, lx2)

    def fin_compute(jt):
        b, tt = divmod(jt, NT)
        p = jt % NP
        g2, g3, lx2 = fmarks.pop(jt)
        rw = WK2[:, 1536 + 4 * jt:1540 + 4 * jt]
        yb = Dg[:, p * 2048:(p + 1) * 2048]
        xb = Af[:, p * 1024:(p + 1) * 1024]
        acc = Bf[:, p * 1024:(p + 1) * 1024]
        a1 = vop(lambda e, acc=acc, yb=yb, rw=rw: e.scalar_tensor_tensor(
            acc, yb[:, 0:1024], rw[:, 2:3], REPF[:], op0=ALU.mult, op1=ALU.add),
            waits=[g2, g3, ("VV", repf_ev)])
        a2 = vop(lambda e, acc=acc, yb=yb, rw=rw: e.scalar_tensor_tensor(
            acc, yb[:, 1024:2048], rw[:, 3:4], acc, op0=ALU.mult, op1=ALU.add))
        a3 = vop(lambda e, acc=acc, xb=xb: e.tensor_add(acc, acc, xb), waits=[lx2])
        nm, var, rstd, sqs = SMc(p * 4), SMc(p * 4 + 1), SMc(p * 4 + 2), SMc(p * 4 + 3)
        vop(lambda e, nm=nm, acc=acc: e.reduce_sum(nm, acc, axis=AX.X, negate=True),
            waits=[("VV", a3)])
        vop(lambda e, nm=nm: e.tensor_scalar(nm, nm, 1.0 / 1024.0, None, op0=ALU.mult))
        v3 = vop(lambda e, nm=nm, acc=acc: e.tensor_scalar(acc, acc, nm, None, op0=ALU.add))
        s1 = vop(lambda e, sqs=sqs, acc=acc, xb=xb: e.scalar_tensor_tensor(
            xb, acc, 1.0, acc, op0=ALU.mult, op1=ALU.mult, accum_out=sqs),
            waits=[("VV", v3)])
        v4 = vop(lambda e, var=var, sqs=sqs: e.tensor_scalar(var, sqs, 1.0 / 1024.0, 1e-5,
                                                             op0=ALU.mult, op1=ALU.add))
        s2 = sop(lambda e, rstd=rstd, var=var: e.activation(rstd, var, AF.Ln), waits=[("VV", v4)])
        s3 = sop(lambda e, rstd=rstd: e.activation(rstd, rstd, AF.Exp, scale=-0.5))
        v5 = vop(lambda e, rstd=rstd, acc=acc: e.tensor_scalar(acc, acc, rstd, None, op0=ALU.mult),
                 waits=[("SS", s3)])
        st_l[p] = dma("sync", "LY", out_c[b][tt * 128:(tt + 1) * 128, :], acc, waits=[("VV", v5)])

    NTILE = NSTEP * NT
    for jt in range(NTILE):
        fin_gather(jt)
        if jt >= SKEW:
            fin_compute(jt - SKEW)
    for jt in range(NTILE - SKEW, NTILE):
        fin_compute(jt)
    pg.emit("sync", lambda e: e.nop(), waits=list(st_l))

    pg.realize()
    return nc


_NC_CACHE = None


def kernel(_stop=None, _dbg=None, **inputs):
    global _NC_CACHE
    perm = np.concatenate([np.arange(0, 64, 2), np.arange(1, 64, 2)])
    PERM = np.concatenate([h * 64 + perm for h in range(H)])
    f32 = np.float32
    Wq = np.ascontiguousarray(inputs["L0_attn_Wq"][:, PERM], dtype=f32)
    bq = np.ascontiguousarray(np.asarray(inputs["L0_attn_bq"], dtype=f32)[PERM].reshape(8, 128).T)
    Wk = np.ascontiguousarray(inputs["L0_attn_Wk"][:, PERM], dtype=f32)
    bk = np.ascontiguousarray(np.asarray(inputs["L0_attn_bk"], dtype=f32)[PERM].reshape(8, 128).T)
    x = np.asarray(inputs["L0_attn_x"], dtype=f32)
    if _stop is None and _dbg is None:
        if _NC_CACHE is None:
            _NC_CACHE = build_program()
        nc = _NC_CACHE
    else:
        nc = build_program(stop=_stop, dbg_keys=_dbg or ())
    in_maps = []
    for c in range(NCORE):
        batches = [c + 8 * bb for bb in range(NSTEP)]
        oh = np.zeros((128, 8), f32)
        oh[:, c] = 1.0
        tglob = np.concatenate([(np.arange(S, dtype=np.int64) + bb * S) for bb in batches])
        ownpos = ((tglob % 128) * 512 + tglob // 128).astype(np.int32)
        own = np.ascontiguousarray(ownpos.reshape(64, 128).T)
        in_maps.append({
            "x4": np.ascontiguousarray(x[batches]),
            "wqp": Wq, "bqp": bq, "wkp": Wk, "bkp": bk,
            "wvi": np.ascontiguousarray(inputs["L0_attn_Wv"], dtype=f32),
            "bvi": np.ascontiguousarray(inputs["L0_attn_bv"], dtype=f32),
            "wgi": np.ascontiguousarray(inputs["L0_attn_Wg"], dtype=f32),
            "bgi": np.ascontiguousarray(inputs["L0_attn_bg"], dtype=f32),
            "woi": np.ascontiguousarray(inputs["L0_attn_Wo"], dtype=f32),
            "boi": np.ascontiguousarray(inputs["L0_attn_bo"], dtype=f32),
            "rwi": np.ascontiguousarray(inputs["L0_ffn_router_w"], dtype=f32),
            "w1c": np.ascontiguousarray(inputs["L0_ffn_w1"][c], dtype=f32),
            "w2c": np.ascontiguousarray(inputs["L0_ffn_w2"][c], dtype=f32),
            "fbi": np.ascontiguousarray(np.asarray(inputs["L0_ffn_bias"], dtype=f32).reshape(D)),
            "onehot": oh,
            "ownrows": own,
        })
    res = run_bass_kernel_spmd(nc, in_maps, core_ids=list(range(NCORE)))
    if _stop is not None or _dbg:
        return res
    out = np.empty((B, S, D), np.float32)
    for c in range(NCORE):
        oc = res.results[c]["out_c"]
        for bb in range(NSTEP):
            out[c + 8 * bb] = oc[bb]
    return out


# version 19
# speedup vs baseline: 1.1050x; 1.0111x over previous
import numpy as np
import ml_dtypes

import concourse.bass as bass
import concourse.mybir as mybir
from concourse.bass_utils import run_bass_kernel_spmd
from concourse.bass_core_rust import IndirectOffsetOnAxis

F32 = mybir.dt.float32
BF16 = mybir.dt.bfloat16
I32 = mybir.dt.int32
AF = mybir.ActivationFunctionType
ALU = mybir.AluOpType
AX = mybir.AxisListType
BF = ml_dtypes.bfloat16

B, S, D = 32, 2048, 1024
H = 16
NCORE, NSTEP = 8, 4
NT = S // 128
GT = 512
CAP = 8192
PAD = 128
PLANE = 2 * (CAP + PAD)
ZROW = NCORE * 2 * CAP
ENGS = ("tensor", "vector", "scalar", "gpsimd", "sync")
DEBUG_OUTS = ()


def _tables():
    t = {}
    ah = 1.0 / 10000.0 ** np.linspace(0.0, 1.0, 32, dtype=np.float64)
    idx = np.arange(S, dtype=np.float64)
    cos_h = np.cos(idx[:, None] * ah[None, :])
    sin_h = np.sin(idx[:, None] * ah[None, :])
    t["cos4"] = np.tile(cos_h.T.astype(np.float32), (4, 1)).astype(BF)
    sgn = np.repeat(np.array([-1.0, 1.0, -1.0, 1.0], np.float32), 32)
    t["sin4"] = (np.tile(sin_h.T.astype(np.float32), (4, 1)) * sgn[:, None]).astype(BF)
    decay = np.log(1.0 - 2.0 ** (-5.0 - np.arange(H, dtype=np.float64)))
    g_negt = np.exp(-decay[:, None] * idx[None, :])
    gk = np.empty((8, 128, S), np.float32)
    for j in range(8):
        gk[j, 0:64] = (64.0 ** -0.5) * g_negt[2 * j]
        gk[j, 64:128] = (64.0 ** -0.5) * g_negt[2 * j + 1]
    t["gkrep"] = gk.astype(BF)
    masksum = (1.0 - np.exp(decay[:, None] * (idx[None, :] + 1))) / (1.0 - np.exp(decay[:, None]))
    c_tab = (np.exp(decay[:, None] * idx[None, :]) / np.sqrt(masksum)).astype(np.float32)
    ca = np.empty((128, 256), np.float32)
    for h in range(H):
        for s0 in range(16):
            ca[:, h * 16 + s0] = c_tab[h, s0 * 128:(s0 + 1) * 128]
    t["c_all"] = ca
    tt, ss = np.meshgrid(np.arange(128), np.arange(128), indexing="ij")
    t["tri"] = (tt <= ss).astype(np.float32).astype(BF)
    t["su"] = (tt < ss).astype(np.float32).astype(BF)
    t["id128b"] = np.eye(128, dtype=np.float32).astype(BF)
    t["id8f"] = np.eye(8, dtype=np.float32)
    t["iota128r"] = np.broadcast_to(np.arange(128, dtype=np.float32), (128, 128)).copy()
    t["iota8r"] = np.broadcast_to(np.arange(8, dtype=np.float32), (128, 8)).copy()
    t["rii8r"] = (99.0 - t["iota8r"]).copy()
    t["ones1x"] = np.ones((1, 128), np.float32).astype(BF)
    t["iotacol"] = np.arange(128, dtype=np.float32).reshape(128, 1)
    t["onescol"] = np.ones((128, 1), np.float32).astype(BF)
    idc = np.empty((128, GT, 2), np.float32)
    p = np.arange(128)
    for tile in range(GT):
        idc[:, tile, 0] = (tile % 2) * 128 + p
        idc[:, tile, 1] = tile // 2
    t["idcols"] = idc.astype(BF)
    return t


QDEPTH = 4


class Prog:
    def __init__(self, nc):
        self.nc = nc
        qsems = tuple(f"{q}{i}" for q in ("LY", "LG", "LS") for i in range(QDEPTH))
        self.names = ("MM", "VV", "SS", "CC") + qsems
        self.sems = {n: nc.alloc_semaphore(n) for n in self.names}
        self.cnt = {n: 0 for n in self.names}
        self.hw = {e: {n: 0 for n in self.names} for e in ENGS}
        self.qidx = {"LY": 0, "LG": 0, "LS": 0}

    def emit(self, eng, fn, waits=(), inc=None, force_self=False):
        w = []
        for s, v in waits:
            if v <= 0 or self.hw[eng][s] >= v:
                continue
            self.hw[eng][s] = v
            w.append((s, v))
        self.ops_list(eng).append((fn, w, inc))
        if inc is not None:
            self.cnt[inc[0]] += inc[1]
        return self.cnt[inc[0]] if inc else None

    def ops_list(self, eng):
        if not hasattr(self, "_ops"):
            self._ops = {e: [] for e in ENGS}
        return self._ops[eng]

    def snap(self):
        return dict(self.cnt)

    def qbar(self, *qs):
        out = []
        for q in qs:
            for i in range(QDEPTH):
                s = f"{q}{i}"
                out.append((s, self.cnt[s]))
        return out

    def wsnap(self, sn, skip=()):
        return [(s, v) for s, v in sn.items() if s in self.names and s not in skip]

    def realize(self):
        with self.nc.Block() as blk:
            for name in ENGS:
                def mk(nm):
                    def body(e):
                        for fn, w, inc in self._ops[nm]:
                            for s, v in w:
                                e.wait_ge(self.sems[s], v)
                            inst = fn(e)
                            if inc is not None:
                                inst.then_inc(self.sems[inc[0]], inc[1])
                    return body
                getattr(blk, name)(mk(name))


def build_program(stop=None, dbg_keys=None):
    global DEBUG_OUTS
    if dbg_keys is not None:
        DEBUG_OUTS = tuple(dbg_keys)
    T = _tables()
    nc = bass.Bass("TRN2", target_bir_lowering=False)

    def din(name, shape, dt=F32):
        return nc.dram_tensor(name, shape, dt, kind="ExternalInput").ap()
    x4 = din("x4", [NSTEP, S, D])
    wqp = din("wqp", [D, D]); bqp = din("bqp", [128, 8])
    wkp = din("wkp", [D, D]); bkp = din("bkp", [128, 8])
    wvi = din("wvi", [D, 2 * D]); bvi = din("bvi", [2 * D])
    wgi = din("wgi", [D, 2 * D]); bgi = din("bgi", [2 * D])
    woi = din("woi", [2 * D, D]); boi = din("boi", [D])
    rwi = din("rwi", [D, 8])
    w1c = din("w1c", [D, 4096]); w2c = din("w2c", [4096, D])
    fbi = din("fbi", [D])
    onehot = din("onehot", [128, 8])
    ownrows = din("ownrows", [128, 64], I32)
    out_c = nc.dram_tensor("out_c", [NSTEP, S, D], F32, kind="ExternalOutput").ap()
    dbg = {}
    for k in DEBUG_OUTS:
        shp = {"xf": [NSTEP * S, D], "logits": [NSTEP, 128, 128],
               "islot": [2, GT * 128], "ycmp": [2 * CAP, D]}[k]
        dt = {"xf": F32, "logits": F32, "islot": I32, "ycmp": BF16}[k]
        dbg[k] = nc.dram_tensor("dbg_" + k, shp, dt, kind="ExternalOutput").ap()

    def cinl(key):
        return nc.inline_tensor(T[key], name="c_" + key).ap()
    cos4_d, sin4_d, gk_d = cinl("cos4"), cinl("sin4"), cinl("gkrep")
    call_d, tri_d, su_d = cinl("c_all"), cinl("tri"), cinl("su")
    id128_d, id8_d = cinl("id128b"), cinl("id8f")
    i128r_d, i8r_d, rii8_d = cinl("iota128r"), cinl("iota8r"), cinl("rii8r")
    ones1_d, onesc_d, idc_d = cinl("ones1x"), cinl("onescol"), cinl("idcols")
    iotac_d = cinl("iotacol")

    def dsc(name, shape, dt):
        return nc.dram_tensor(name, shape, dt, kind="Internal").ap()
    xbf_s = dsc("xbf_s", [2, S, D], BF16)
    v_s = dsc("v_s", [S, 2 * D], BF16)
    g_s = dsc("g_s", [S, 2 * D], BF16)
    gT_s = dsc("gT_s", [2 * D, S], BF16)
    xf32_s = dsc("xf32_s", [NSTEP, S, D], F32)
    xfb_s = dsc("xfb_s", [NSTEP * S, D], BF16)
    lg_s = dsc("lg_s", [NSTEP, 128, 128], F32)
    xfAG = dsc("xfAG", [NCORE * NSTEP * S, D], BF16)
    lgAG = dsc("lgAG", [NSTEP * NCORE * 128, 128], F32)
    cstage = dsc("cstage", [GT * 16], F32)
    bstage = dsc("bstage", [GT * 16], F32)
    ctflat = [dsc("ctflat0", [GT * 128, 1], I32), dsc("ctflat1", [GT * 128, 1], I32)]
    rowstage = dsc("rowstage", [4096], F32)
    yrw_s = dsc("yrw_s", [NCORE * NSTEP * S, 4], F32)
    comp_s = dsc("comp_s", [2 * CAP, D], BF16)
    ycmp_s = dsc("ycmp_s", [2 * CAP, D], BF16)
    yAG = dsc("yAG", [ZROW, D], BF16)

    A = nc.alloc_sbuf_tensor("BIGA", [128, 16384], BF16)
    Bg = nc.alloc_sbuf_tensor("BIGB", [128, 16384], BF16)
    C = nc.alloc_sbuf_tensor("BIGC", [128, 16384], BF16)
    Dg = nc.alloc_sbuf_tensor("BIGD", [128, 16384], BF16)
    E = nc.alloc_sbuf_tensor("BIGE", [128, 16384], BF16)
    Fg = nc.alloc_sbuf_tensor("BIGF", [128, 8192], BF16, is_transpose_dest=True)
    WK = nc.alloc_sbuf_tensor("WK", [128, 2048], F32)
    WK2 = nc.alloc_sbuf_tensor("WK2", [128, 2048], F32, is_transpose_dest=True)
    REPB = nc.alloc_sbuf_tensor("REPB", [128, 2048], BF16)
    REPF = nc.alloc_sbuf_tensor("REPF", [128, 1024], F32)
    SM = nc.alloc_sbuf_tensor("SM", [128, 512], F32)
    SMB = nc.alloc_sbuf_tensor("SMB", [128, 640], BF16)
    IDC = nc.alloc_sbuf_tensor("IDC", [128, 1024], BF16)
    ISL = nc.alloc_sbuf_tensor("ISL", [128, 140], I32)
    OWN = nc.alloc_sbuf_tensor("OWN", [128, 64], I32)
    ONESF = nc.alloc_sbuf_tensor("ONESF", [1, 128], F32)
    P = [nc.alloc_psum_tensor(f"P{i}", [128, 512], F32) for i in range(8)]
    assert nc.sbuf_bytes_remaining >= 0

    SM_call = SM[:, 0:256]
    SM_i8 = SM[:, 256:264]
    SM_rii8 = SM[:, 264:272]
    SM_id8 = SM[:, 272:280]
    SM_bq = SM[:, 280:288]
    SM_bk = SM[:, 288:296]
    SM_oh = SM[:, 296:304]
    SM_i128 = SM[:, 304:432]
    SMc = lambda i: SM[:, 432 + i:433 + i]
    SM_iotac = SM[:, 448:449]
    SM_base = [SM[:, 452 + 12 * k:456 + 12 * k] for k in range(2)]
    SM_basep1 = [SM[:, 456 + 12 * k:460 + 12 * k] for k in range(2)]
    SM_cnt = [SM[:, 460 + 12 * k:464 + 12 * k] for k in range(2)]
    SMB_tri = SMB[:, 0:128]
    SMB_su = SMB[:, 128:256]
    SMB_id = SMB[:, 256:384]
    SMB_ones1 = SMB[0:1, 384:512]
    SMB_onesc = SMB[:, 512:513]
    SMB_rw = SMB[:, 520:584].rearrange("p (dt e) -> p dt e", e=8)

    pg = Prog(nc)
    ZT = ("MM", 0)

    def dma(eng, q, out, in_, waits=(), **kw):
        s = f"{q}{pg.qidx[q] % QDEPTH}"
        pg.qidx[q] += 1
        def f(e, o=out, i=in_, k=dict(kw)):
            return e.dma_start(out=o, in_=i, **k)
        pg.emit(eng, f, waits=list(waits) + [(s, pg.cnt[s])], inc=(s, 16))
        return (s, pg.cnt[s])

    def mm(out, lhsT, rhs, start, stop, waits=()):
        def f(e, o=out, l=lhsT, r=rhs, st=start, sp=stop):
            return e.matmul(o, l, r, start=st, stop=sp)
        return pg.emit("tensor", f, waits=waits, inc=("MM", 1))

    def tr(out, in_, ident, waits=()):
        def f(e, o=out, i=in_, d=ident):
            return e.transpose(o, i, d)
        return pg.emit("tensor", f, waits=waits, inc=("MM", 1))

    def vop(fn, waits=()):
        return pg.emit("vector", fn, waits=list(waits) + [("VV", pg.cnt["VV"])],
                       inc=("VV", 1), force_self=True)

    def sop(fn, waits=()):
        return pg.emit("scalar", fn, waits=list(waits) + [("SS", pg.cnt["SS"])],
                       inc=("SS", 1), force_self=True)

    def gop(fn, waits=(), inc=None):
        if inc is not None:
            pg.emit("gpsimd", fn, waits=waits, inc=inc)
            return ("CC", pg.cnt["CC"])
        s = f"LG{pg.qidx['LG'] % QDEPTH}"
        pg.qidx["LG"] += 1
        pg.emit("gpsimd", fn, waits=list(waits) + [(s, pg.cnt[s])], inc=(s, 16))
        return (s, pg.cnt[s])

    dma("sync", "LY", SM_call, call_d)
    dma("sync", "LY", SM_i8, i8r_d)
    dma("sync", "LY", SM_rii8, rii8_d)
    dma("sync", "LY", SM_id8[0:8, :], id8_d)
    dma("sync", "LY", SM_bq, bqp)
    dma("sync", "LY", SM_bk, bkp)
    dma("sync", "LY", SM_oh, onehot)
    dma("sync", "LY", SM_i128, i128r_d)
    dma("sync", "LY", SM_iotac, iotac_d)
    dma("sync", "LY", SMB_tri, tri_d)
    dma("sync", "LY", SMB_su, su_d)
    dma("sync", "LY", SMB_id, id128_d)
    dma("sync", "LY", SMB_ones1, ones1_d)
    dma("sync", "LY", SMB_onesc, onesc_d)
    dma("sync", "LY", IDC[:].rearrange("p (t two) -> p t two", two=2), idc_d)
    dma("sync", "LY", OWN[:], ownrows)
    lrw = dma("gpsimd", "LG", SMB_rw, rwi.rearrange("(dt p) e -> p dt e", p=128))
    zv = vop(lambda e: e.memset(WK[:], 0.0))
    vop(lambda e: e.memset(ONESF[:], 1.0))
    lbo = dma("sync", "LY", WK2[0:1, 0:1024], boi.rearrange("(a d) -> a d", a=1))
    ov1 = vop(lambda e: e.memset(WK2[0:1, 1024:1152], 1.0))
    rv0 = 0
    for sl in range(2):
        m = mm(P[6][:, 0:512], WK2[0:1, 1024:1152], WK2[0:1, sl * 512:(sl + 1) * 512],
               True, True, waits=[lbo, ("VV", ov1), ("VV", rv0)])
        rv0 = vop(lambda e, sl=sl: e.tensor_copy(REPF[:, sl * 512:(sl + 1) * 512], P[6][:, 0:512]),
                  waits=[("MM", m)])
    init_snap = pg.snap()

    xT = A[:].rearrange("p (dt s) -> p dt s", dt=8)
    qrT = Bg[:].rearrange("p (dt s) -> p dt s", dt=8)
    krT = C[:].rearrange("p (dt s) -> p dt s", dt=8)
    W2b = WK2[:].bitcast(BF16)
    qtmp = W2b[:, 0:2048]
    qsw = W2b[:, 2048:4096]
    F_gk = Fg[:, 0:2048]
    F_t2 = Fg[:, 2048:4096]
    F_cos = Fg[:, 4096:6144]
    F_sin = Fg[:, 6144:8192]
    wq_sb = E[:, 0:8192].rearrange("p (dt o) -> p dt o", dt=8)
    wk_sb = E[:, 8192:16384].rearrange("p (dt o) -> p dt o", dt=8)
    wvg_sb = E[:].rearrange("p (dt o) -> p dt o", dt=8)
    E_wo = E[:].rearrange("p (g d) -> p g d", g=16)
    D_xfb = Dg[:, 8192:9216]
    D_lgf = Dg[:, 9216:9472].bitcast(F32)
    D_xfT = [Bg[:, 12288 + i * 2048: 12288 + (i + 1) * 2048] for i in range(2)]

    prev_step = init_snap
    lcos = dma("sync", "LY", F_cos, cos4_d, waits=pg.wsnap(init_snap))
    lsin = dma("sync", "LY", F_sin, sin4_d)
    pre_x, pre_w = {}, {}
    xt_prev = []
    ov = 0
    tr_last = 0
    lg_marks = []
    ag_n = 0
    _sub = stop is not None and (stop in ('a0', 'b0', 'c0', 'd0', 'ef0', 'dskip') or stop.startswith('dh') or stop.startswith('dm') or stop in ('dep', 'dld', 'dsa', 'dva'))
    nstep_run = 0 if stop == 'p0' else (1 if _sub else NSTEP)
    for b in range(nstep_run):
        if b in pre_x:
            lgc, xt_l = pre_x.pop(b)
        else:
            lgc = dma("gpsimd", "LG", xbf_s[b % 2], x4[b], waits=pg.wsnap(prev_step, ("CC",)))
            xt_l = [dma("scalar", "LS", xT[:, j, :], xbf_s[b % 2][:, j * 128:(j + 1) * 128],
                        waits=[lgc], transpose=True) for j in range(8)]
        if b in pre_w:
            lq, lk = pre_w.pop(b)
        else:
            lq = dma("gpsimd", "LG", wq_sb, wqp.rearrange("(dt p) o -> p dt o", p=128),
                     waits=pg.wsnap(prev_step, ("CC",)))
            lk = dma("gpsimd", "LG", wk_sb, wkp.rearrange("(dt p) o -> p dt o", p=128))

        if stop == 'a0':
            continue
        pa_free_s = [0, 0, 0, 0]
        last_rot = 0
        for which in range(2):
            wsb, wld = (wq_sb, lq) if which == 0 else (wk_sb, lk)
            bcol = SM_bq if which == 0 else SM_bk
            dstT = qrT if which == 0 else krT
            for j in range(8):
                if which == 1:
                    lgk = dma("gpsimd", "LG", F_gk, gk_d[j],
                              waits=[("VV", last_rot)] +
                                    (pg.wsnap(prev_step, ("CC",)) if j == 0 else []))
                for sl in range(4):
                    ps = P[sl]
                    for dt in range(8):
                        w = ([wld, ("VV", pa_free_s[sl])]
                             if dt == 0 else []) + [xt_l[dt]]
                        m = mm(ps[:, 0:512], wsb[:, dt, j * 128:(j + 1) * 128],
                               xT[:, dt, sl * 512:(sl + 1) * 512], dt == 0, dt == 7, waits=w)
                    pa_free_s[sl] = vop(
                        lambda e, ps=ps, sl=sl, bc=bcol, j=j: e.tensor_scalar(
                            qtmp[:, sl * 512:(sl + 1) * 512], ps[:, 0:512],
                            bc[:, j:j + 1], None, op0=ALU.add),
                        waits=[("MM", m), ("VV", last_rot)])
                sse = max(pa_free_s)
                if which == 1:
                    gv = vop(lambda e: e.tensor_mul(qtmp[:], qtmp[:], F_gk[:]),
                             waits=[("VV", sse), lgk])
                else:
                    gv = 0
                w0 = [("VV", max(sse, gv)), lsin, lcos] + \
                     (pg.wsnap(prev_step, ("CC",)) if (which == 0 and j == 0) else [])
                vop(lambda e: e.tensor_copy(qsw[0:32, :], qtmp[32:64, :]), waits=w0)
                vop(lambda e: e.tensor_copy(qsw[32:64, :], qtmp[0:32, :]))
                vop(lambda e: e.tensor_copy(qsw[64:96, :], qtmp[96:128, :]))
                c4 = vop(lambda e: e.tensor_copy(qsw[96:128, :], qtmp[64:96, :]))
                vop(lambda e, dT=dstT, j=j: e.tensor_mul(dT[:, j, :], qtmp[:], F_cos[:]))
                t2 = vop(lambda e: e.tensor_mul(F_t2[:], qsw[:], F_sin[:]), waits=[("VV", c4)])
                last_rot = vop(lambda e, dT=dstT, j=j: e.tensor_add(dT[:, j, :], dT[:, j, :], F_t2[:]),
                               waits=[("VV", t2)])
        rot_done = pg.snap()

        if stop == 'b0':
            continue
        vg_last_store = ZT
        for which in range(2):
            wten, brow_d = (wvi, bvi) if which == 0 else (wgi, bgi)
            dst_d = v_s if which == 0 else g_s
            lw = dma("gpsimd", "LG", wvg_sb, wten.rearrange("(dt p) o -> p dt o", p=128),
                     waits=(pg.wsnap(rot_done) if which == 0 else
                            [("VV", pg.cnt["VV"]), ("MM", pg.cnt["MM"])]))
            lbr = dma("gpsimd", "LG", REPB[0:1, :], brow_d.rearrange("(a d) -> a d", a=1),
                      waits=pg.wsnap(rot_done))
            repv = 0
            for sl in range(4):
                m = mm(P[2][:, 0:512], SMB_ones1,
                       REPB[0:1, sl * 512:(sl + 1) * 512], True, True,
                       waits=[lbr, ("VV", repv)])
                repv = vop(lambda e, sl=sl: e.tensor_copy(REPB[:, sl * 512:(sl + 1) * 512], P[2][:, 0:512]),
                           waits=[("MM", m)])
            pv_free = [repv, repv, repv, repv]
            vg_st = [vg_last_store, vg_last_store]
            for tt in range(NT):
                buf = qtmp if tt % 2 == 0 else qsw
                ev = 0
                for dvs in range(4):
                    ps = P[dvs]
                    for dt in range(8):
                        w = ([lw, ("VV", pv_free[dvs])] if dt == 0 else []) + [xt_l[dt]]
                        m = mm(ps[:, 0:512], xT[:, dt, tt * 128:(tt + 1) * 128],
                               wvg_sb[:, dt, dvs * 512:(dvs + 1) * 512], dt == 0, dt == 7, waits=w)
                    ev = vop(lambda e, ps=ps, dvs=dvs, buf=buf: e.tensor_add(
                        buf[:, dvs * 512:(dvs + 1) * 512], ps[:, 0:512],
                        REPB[:, dvs * 512:(dvs + 1) * 512]),
                        waits=[("MM", m), vg_st[tt % 2]])
                    pv_free[dvs] = ev
                vg_st[tt % 2] = dma("sync", "LY", dst_d[tt * 128:(tt + 1) * 128, :], buf[:],
                                    waits=[("VV", ev)])
            vg_last_store = vg_st[(NT - 1) % 2]

        if stop == 'c0':
            continue
        vg_bar = pg.qbar("LY")
        lwo = dma("gpsimd", "LG", E_wo, woi.rearrange("(g p) d -> p g d", p=128),
                  waits=[("MM", pg.cnt["MM"])])
        if b + 1 < nstep_run:
            nb = b + 1
            plgc = dma("gpsimd", "LG", xbf_s[nb % 2], x4[nb], waits=list(xt_prev))
            pxt = [dma("scalar", "LS", xT[:, j, :], xbf_s[nb % 2][:, j * 128:(j + 1) * 128],
                       waits=[plgc, ("MM", pg.cnt["MM"])], transpose=True) for j in range(8)]
            pre_x[nb] = (plgc, pxt)
        xt_prev = xt_l
        D_vre4 = [Dg[:, hr * 2112:(hr + 1) * 2112].rearrange("p (tt c) -> p tt c", c=132)
                  for hr in range(4)]
        D_gh2 = [Dg[:, 8448 + i * 2048:8448 + (i + 1) * 2048].rearrange("p (q d) -> p q d", d=512)
                 for i in range(2)]
        D_owb = Dg[:, 12544:14592].rearrange("p (s0 d) -> p s0 d", d=128)
        D_st = [Dg[:, 14592 + i * 512:14592 + (i + 1) * 512] for i in range(3)]
        D_gacc = [Fg[:, i * 2048:(i + 1) * 2048] for i in range(2)]
        SG_all = W2b[:, 2048:4096]
        OrawR = WK[:, 384:900].rearrange("p (rs c) -> p rs c", c=129)
        R4 = WK[:, 96:100]
        gat_all = W2b[:, 0:2048]
        if b == 0:
            ov = vop(lambda e: e.memset(
                Dg[:, 0:8448].rearrange("p (x c) -> p x c", c=132)[:, :, 128:129], 1.0),
                waits=pg.wsnap(prev_step, ("CC",)))
        MM1P = (0, 1, 7)
        st_mm_free = [0, 0, 0]
        pa_s = [0, 0, 0]
        pa_v = [pg.cnt["VV"]] * 3
        slot_i = 0
        po_ep = [0, 0, 0, 0]
        vre_mm4 = [0, 0, 0, 0]
        gh_ss = [0, 0]
        gacc_fl = [[], []]
        pmt4 = [0, 0, 0, 0]
        vg2_4 = [0, 0, 0, 0]
        ghp = 0
        _dH = 0 if stop == 'dskip' else H
        if stop is not None and stop.startswith("dh"):
            _dH = int(stop[2:])
        tail_q = []
        for hq in range(0, H, 4):
            if hq >= _dH:
                break
            lvh = []
            for hr in range(4):
                h = hq + hr
                lvh.append(dma("sync", "LY", D_vre4[hr][:, :, 0:128],
                               v_s.rearrange("(tt p) o -> p tt o", p=128)[:, :, h * 128:(h + 1) * 128],
                               waits=(vg_bar if hq == 0 else []) + [("VV", ov),
                                     ("MM", pg.cnt["MM"] if hq == 0 else vre_mm4[hr])]))
            for g in range(4):
                s_lo = g * 4
                pp = ghp % 2
                lgh = dma("sync", "LY", D_gh2[pp][:, :, :],
                          g_s.rearrange("(tt p) o -> p tt o", p=128)[:, 4 * g:4 * g + 4,
                                                                    hq * 128:hq * 128 + 512],
                          waits=(vg_bar if ghp < 2 else []) + [("SS", gh_ss[pp])])
                ghp += 1

                def emit_mm2(ent, s_lo=s_lo):
                    sl3, t0, hr2, sv, vv = ent
                    lastm = 0
                    for s0 in range(max(s_lo, t0), s_lo + 4):
                        rs = s0 - s_lo
                        lastm = mm(P[2 + rs][:, 0:129],
                                   D_st[sl3][:, rs * 128:(rs + 1) * 128],
                                   D_vre4[hr2][:, t0, 0:129], t0 == 0, s0 == t0,
                                   waits=[("SS", sv),
                                          ("VV", max(vv, po_ep[rs] if t0 == 0 else 0)),
                                          lvh[hr2]])
                    st_mm_free[sl3] = lastm
                    vre_mm4[hr2] = max(vre_mm4[hr2], lastm)

                for hr in range(4):
                    h = hq + hr
                    if h >= _dH:
                        continue
                    j, half = divmod(h, 2)
                    pb = half * 64
                    pend = []
                    for t0 in range(s_lo + 4):
                        sl3 = slot_i % 3
                        slot_i += 1
                        ps = P[MM1P[sl3]]
                        m = mm(ps[:, 0:512], krT[pb:pb + 64, j, t0 * 128:(t0 + 1) * 128],
                               qrT[pb:pb + 64, j, g * 512:(g + 1) * 512], True, True,
                               waits=[("SS", pa_s[sl3]), ("VV", pa_v[sl3])])
                        rel = t0 - s_lo
                        sv = vv = 0
                        if rel < 0:
                            sv = sop(lambda e, ps=ps, sl=sl3: e.activation(
                                D_st[sl][:, 0:512], ps[:, 0:512], AF.Copy),
                                waits=[("MM", max(m, st_mm_free[sl3]))])
                        else:
                            c0 = rel * 128
                            vv = vop(lambda e, ps=ps, sl=sl3, c0=c0: e.tensor_mul(
                                D_st[sl][:, c0:c0 + 128], ps[:, c0:c0 + 128], SMB_tri),
                                waits=[("MM", max(m, st_mm_free[sl3]))])
                            if c0 + 128 < 512:
                                vv = vop(lambda e, ps=ps, sl=sl3, c0=c0: e.tensor_copy(
                                    D_st[sl][:, c0 + 128:512], ps[:, c0 + 128:512]))
                        pa_s[sl3] = max(pa_s[sl3], sv)
                        pa_v[sl3] = max(pa_v[sl3], vv)
                        pend.append((sl3, t0, hr, sv, vv))
                        if tail_q:
                            tail_q.pop(0)()
                        if len(pend) > 2:
                            emit_mm2(pend.pop(0))
                    while pend:
                        emit_mm2(pend.pop(0))
                    if tail_q:
                        tail_q.pop(0)()
                    mfin = pg.cnt["MM"]
                    for rs in range(4):
                        po_ep[rs] = vop(lambda e, rs=rs: e.tensor_copy(
                            OrawR[:, rs, :], P[2 + rs][:, 0:129]),
                            waits=[("MM", mfin)] if rs == 0 else [])
                    csl4 = SM_call[:, h * 16 + s_lo: h * 16 + s_lo + 4]
                    tc, tn, wc = WK[:, 104:108], WK[:, 108:112], WK[:, 100:104]
                    vop(lambda e: e.tensor_copy(
                        R4, OrawR[:, :, 128:129].rearrange("p a c -> p (a c)")))
                    vop(lambda e, csl4=csl4: e.tensor_tensor(tc, csl4, R4, op=ALU.mult))
                    vop(lambda e: e.tensor_scalar(tn, tc, -1.0, None, op0=ALU.mult))
                    vop(lambda e: e.tensor_tensor(tc, tc, tn, op=ALU.max))
                    vop(lambda e: e.tensor_scalar(tc, tc, 1.0, None, op0=ALU.max))
                    vop(lambda e: e.reciprocal(tc, tc))
                    vop(lambda e, csl4=csl4: e.tensor_mul(wc, csl4, tc))
                    for rs in range(4):
                        idx = hr * 4 + rs
                        mus = WK[:, idx:idx + 1]
                        sqs = WK[:, 16 + idx:17 + idx]
                        vop(lambda e, rs=rs, idx=idx, mus=mus: e.tensor_scalar(
                            D_owb[:, idx, :], OrawR[:, rs, 0:128], wc[:, rs:rs + 1], 0.0,
                            op0=ALU.mult, op1=ALU.add, accum_out=mus))
                        vop(lambda e, idx=idx, sqs=sqs: e.scalar_tensor_tensor(
                            WK[:, 128:256], D_owb[:, idx, :], 1.0, D_owb[:, idx, :],
                            op0=ALU.mult, op1=ALU.mult, accum_out=sqs))
                nm16, var16, rstd16, e16 = WK[:, 32:48], WK[:, 48:64], WK[:, 64:80], WK[:, 80:96]
                vop(lambda e: e.tensor_scalar(nm16, WK[:, 0:16], -1.0 / 128.0, None, op0=ALU.mult))
                vop(lambda e: e.tensor_tensor(e16, WK[:, 0:16], WK[:, 0:16], op=ALU.mult))
                vop(lambda e: e.tensor_scalar(e16, e16, -1.0 / 16384.0, None, op0=ALU.mult))
                vop(lambda e: e.scalar_tensor_tensor(
                    var16, WK[:, 16:32], 1.0 / 128.0, e16, op0=ALU.mult, op1=ALU.add))
                vv3 = vop(lambda e: e.tensor_scalar(var16, var16, 1e-5, None, op0=ALU.add))
                s2 = sop(lambda e: e.activation(rstd16, var16, AF.Ln), waits=[("VV", vv3)])
                s3 = sop(lambda e: e.activation(rstd16, rstd16, AF.Exp, scale=-0.5))
                gh_ss[pp] = s4 = sop(lambda e, pp=pp: e.activation(
                    SG_all, D_gh2[pp][:, :, :].rearrange("p q d -> p (q d)"), AF.Silu),
                    waits=[lgh, ("VV", pg.cnt["VV"])])
                pp2 = (ghp - 1) % 2
                first_p2 = True
                ucell = {"first": True}
                for hr in range(4):
                    h = hq + hr
                    if h >= _dH:
                        continue
                    for rs in range(4):
                        idx = hr * 4 + rs
                        ctr = WK[:, 256:384]
                        vop(lambda e, idx=idx: e.tensor_scalar(
                            ctr, D_owb[:, idx, :], nm16[:, idx:idx + 1], None, op0=ALU.add),
                            waits=[("SS", s3)] if first_p2 else [])
                        v13 = vop(lambda e, idx=idx, rs=rs, hr=hr: e.scalar_tensor_tensor(
                            gat_all[:, idx * 128:(idx + 1) * 128], ctr, rstd16[:, idx:idx + 1],
                            SG_all[:, rs * 512 + hr * 128: rs * 512 + (hr + 1) * 128],
                            op0=ALU.mult, op1=ALU.mult),
                            waits=[("SS", s4)] + ([("MM", tr_last)] if first_p2 else []))
                        first_p2 = False

                        def t_go(idx=idx, hr=hr, rs=rs, pp2=pp2, v13=v13, ucell=ucell):
                            nonlocal tr_last
                            sl4 = idx % 4
                            pt = P[6][:, sl4 * 64:(sl4 + 1) * 64].bitcast(BF16)
                            pmt4[sl4] = tr_last = tr(
                                pt, gat_all[:, idx * 128:(idx + 1) * 128], SMB_id,
                                waits=[("VV", v13), ("VV", max(vg2_4))])
                            wfl = []
                            if ucell["first"]:
                                ucell["first"] = False
                                wfl = gacc_fl[pp2]
                                gacc_fl[pp2] = []
                            vg2_4[sl4] = vop(lambda e, pt=pt, hr=hr, rs=rs, pp2=pp2: e.tensor_copy(
                                D_gacc[pp2][:, hr * 512 + rs * 128: hr * 512 + (rs + 1) * 128], pt),
                                waits=[("MM", pmt4[sl4])] + wfl)
                        tail_q.append(t_go)

                    def f_go(h=h, g=g, hr=hr, pp2=pp2):
                        gacc_fl[pp2].append(dma(
                            "sync", "LY", gT_s[h * 128:(h + 1) * 128, g * 512:(g + 1) * 512],
                            D_gacc[pp2][:, hr * 512:(hr + 1) * 512],
                            waits=[("VV", vg2_4[(hr * 4 + 3) % 4])]))
                    tail_q.append(f_go)

        while tail_q:
            tail_q.pop(0)()
        d_done = pg.snap()

        if stop == 'd0':
            continue
        D_gstat = [Dg[:, i * 2048:(i + 1) * 2048].rearrange("p (g s) -> p g s", s=128)
                   for i in range(2)]
        gst_mm = [0, 0]
        pc_v = 0
        xf_l2 = ZT
        prev_l1 = ZT
        prev_l2 = ZT
        for tt in range(NT):
            pp = tt % 2
            lg2 = dma("sync", "LY", D_gstat[pp],
                      gT_s[:, tt * 128:(tt + 1) * 128].rearrange("(g p) s -> p g s", p=128),
                      waits=[("MM", gst_mm[pp])] + pg.wsnap(d_done))
            lx = dma("sync", "LY", WK[:, 1024:2048], x4[b][tt * 128:(tt + 1) * 128, :])
            mlast = 0
            for dgt in range(16):
                for h2 in range(2):
                    w = ([lg2, lwo, ("VV", pc_v)] + pg.wsnap(d_done)) if dgt == 0 else ()
                    mlast = mm(P[4 + h2][:, 0:512], D_gstat[pp][:, dgt, :],
                               E_wo[:, dgt, h2 * 512:(h2 + 1) * 512], dgt == 0, dgt == 15,
                               waits=w)
            gst_mm[pp] = mlast
            rbuf = WK[:, 0:1024]
            vop(lambda e: e.tensor_add(rbuf[:, 0:512], P[4][:, 0:512], REPF[:, 0:512]),
                waits=[("MM", mlast), prev_l1] + pg.wsnap(d_done))
            pc_v = vop(lambda e: e.tensor_add(rbuf[:, 512:1024], P[5][:, 0:512], REPF[:, 512:1024]))
            e3 = vop(lambda e: e.tensor_add(rbuf, rbuf, WK[:, 1024:2048]), waits=[lx])
            nm, var, rstd, sqs = SMc(7), SMc(8), SMc(9), SMc(10)
            vop(lambda e, nm=nm: e.reduce_sum(nm, rbuf, axis=AX.X, negate=True), waits=[("VV", e3)])
            vop(lambda e, nm=nm: e.tensor_scalar(nm, nm, 1.0 / 1024.0, None, op0=ALU.mult))
            v3 = vop(lambda e, nm=nm: e.tensor_scalar(rbuf, rbuf, nm, None, op0=ALU.add))
            s1 = vop(lambda e, sqs=sqs: e.scalar_tensor_tensor(
                WK[:, 1024:2048], rbuf, 1.0, rbuf, op0=ALU.mult, op1=ALU.mult, accum_out=sqs),
                waits=[("VV", v3)])
            v4 = vop(lambda e, var=var, sqs=sqs: e.tensor_scalar(var, sqs, 1.0 / 1024.0, 1e-5,
                                                                 op0=ALU.mult, op1=ALU.add))
            s2 = sop(lambda e, rstd=rstd, var=var: e.activation(rstd, var, AF.Ln), waits=[("VV", v4)])
            s3 = sop(lambda e, rstd=rstd: e.activation(rstd, rstd, AF.Exp, scale=-0.5))
            v5 = vop(lambda e, rstd=rstd: e.tensor_scalar(rbuf, rbuf, rstd, None, op0=ALU.mult),
                     waits=[("SS", s3)])
            v6 = vop(lambda e: e.tensor_copy(D_xfb[:], rbuf), waits=[prev_l2])
            prev_l1 = dma("sync", "LY", xf32_s[b][tt * 128:(tt + 1) * 128, :], rbuf,
                          waits=[("VV", v5)])
            prev_l2 = xf_l2 = dma("sync", "LY",
                                  xfb_s[b * S + tt * 128: b * S + (tt + 1) * 128, :], D_xfb,
                                  waits=[("VV", v6)])
        if b + 1 < nstep_run:
            pre_w[b + 1] = (
                dma("gpsimd", "LG", wq_sb, wqp.rearrange("(dt p) o -> p dt o", p=128),
                    waits=[("MM", max(gst_mm))]),
                dma("gpsimd", "LG", wk_sb, wkp.rearrange("(dt p) o -> p dt o", p=128)))
        xfb_bar = pg.qbar("LY")
        xfT_mm = [0, 0]
        mm_d = pg.cnt["MM"]
        for dt in range(8):
            pp = dt % 2
            lt = dma("scalar", "LS", D_xfT[pp], xfb_s[b * S:(b + 1) * S, dt * 128:(dt + 1) * 128],
                     waits=xfb_bar + [("MM", xfT_mm[pp] if dt >= 2 else mm_d)], transpose=True)
            for sl in range(4):
                xfT_mm[pp] = mm(P[sl][0:8, 0:512], SMB_rw[:, dt, :],
                                D_xfT[pp][:, sl * 512:(sl + 1) * 512], dt == 0, dt == 7,
                                waits=[lt, lrw])
        mrt = pg.cnt["MM"]
        lgTrow = WK[0:8, 0:2048]
        lgv = 0
        for sl in range(4):
            lgv = vop(lambda e, sl=sl: e.tensor_copy(lgTrow[:, sl * 512:(sl + 1) * 512],
                                                     P[sl][0:8, 0:512]),
                      waits=[("MM", mrt)])
        lgsb = D_lgf[:, 0:128].rearrange("p (tt e) -> p tt e", e=8)
        lgev = 0
        for tt in range(NT):
            mt = tr(P[6][:, 0:8], lgTrow[:, tt * 128:(tt + 1) * 128], SM_id8[0:8, :],
                    waits=[("VV", lgv), ("VV", lgev)])
            lgev = vop(lambda e, tt=tt: e.tensor_copy(lgsb[:, tt, :], P[6][:, 0:8]),
                       waits=[("MM", mt)])
        l3 = dma("sync", "LY", lg_s[b], lgsb.rearrange("p tt e -> p (tt e)"), waits=[("VV", lgev)])
        if stop == 'ef0':
            continue
        gop(lambda e, b=b: e.collective_compute(
            "AllGather", ALU.bypass, replica_groups=[list(range(NCORE))],
            ins=[lg_s[b].opt()],
            outs=[lgAG[b * NCORE * 128:(b + 1) * NCORE * 128, :].opt()]),
            waits=[l3], inc=("CC", 1))
        gop(lambda e, b=b: e.collective_compute(
            "AllGather", ALU.bypass, replica_groups=[list(range(NCORE))],
            ins=[xfb_s[b * S:(b + 1) * S, :].opt()],
            outs=[xfAG[b * NCORE * S:(b + 1) * NCORE * S, :].opt()]),
            waits=xfb_bar, inc=("CC", 1))
        ag_n += 2
        prev_step = pg.snap()

    ag_attn = ag_n
    attn_snap = pg.snap()

    def _finish_early():
        sl_ = dma("sync", "LY", out_c[0][0:128, :], REPF[:], waits=pg.wsnap(pg.snap()))
        pg.emit("sync", lambda e: e.nop(), waits=[sl_])
        pg.realize()
        return nc

    if stop is not None and stop != "attn" and (
            stop in ("p0", "attn0", "a0", "b0", "c0", "d0", "ef0")
            or stop.startswith("dh") or stop.startswith("dm") or stop in ("dep", "dskip", "dld", "dsa", "dva")):
        return _finish_early()
    if "xf" in dbg:
        dma("sync", "LY", dbg["xf"], xf32_s.rearrange("b s d -> (b s) d"), waits=pg.wsnap(attn_snap))
    if "logits" in dbg:
        dma("sync", "LY", dbg["logits"].rearrange("b p x -> (b p) x"), lg_s.rearrange("b p x -> (b p) x"), waits=pg.wsnap(attn_snap))
    if stop == "attn":
        return _finish_early()

    L = C[:, 0:8192].bitcast(F32).rearrange("p (t e) -> p t e", e=8)
    ind1 = C[:, 8192:12288].rearrange("p (t e) -> p t e", e=8)
    ind2 = C[:, 12288:16384].rearrange("p (t e) -> p t e", e=8)
    lL_toks = []
    for bb in range(NSTEP):
        for rr in range(NCORE):
            lL_toks.append(dma("sync", "LY", L[:, (bb * 8 + rr) * 16:(bb * 8 + rr + 1) * 16, :],
                     lgAG[(bb * 8 + rr) * 128:(bb * 8 + rr + 1) * 128, :].rearrange(
                         "p (tw e) -> p tw e", e=8),
                     waits=([("CC", max(ag_attn - 1, 0))] + pg.wsnap(attn_snap, ("CC",))) if (bb == 0 and rr == 0) else ()))
    Dx = Dg[:].bitcast(F32)
    m1, m2 = Dx[:, 0:512], Dx[:, 512:1024]
    i1f, i2f = Dx[:, 1024:1536], Dx[:, 1536:2048]
    wt1, wt2 = Dx[:, 2048:2560], Dx[:, 2560:3072]
    scr = Dx[:, 3072:7168]
    scr2v = scr.rearrange("p (t e) -> p t e", e=8)
    se = Dx[:, 7168:7680]
    bc = lambda ap: ap.rearrange("p (t a) -> p t a", a=1).broadcast_to([128, 512, 8])
    i8bc = SM_i8.rearrange("p (a e) -> p a e", a=1).broadcast_to([128, 512, 8])
    rii8bc = SM_rii8.rearrange("p (a e) -> p a e", a=1).broadcast_to([128, 512, 8])
    ohbc = SM_oh.rearrange("p (a e) -> p a e", a=1).broadcast_to([128, 512, 8])
    vop(lambda e: e.reduce_max(m1, L, axis=AX.X), waits=lL_toks + pg.wsnap(attn_snap, ("CC",)))
    vop(lambda e: e.tensor_tensor(scr2v, L, bc(m1), op=ALU.is_equal))
    vop(lambda e: e.tensor_mul(scr2v, scr2v, rii8bc))
    vop(lambda e: e.reduce_max(i1f, scr2v, axis=AX.X))
    vop(lambda e: e.tensor_tensor(scr2v, L, bc(m1), op=ALU.is_equal))
    vop(lambda e: e.tensor_scalar(scr2v, scr2v, 1e30, None, op0=ALU.mult))
    vop(lambda e: e.tensor_tensor(scr2v, L, scr2v, op=ALU.subtract))
    vop(lambda e: e.reduce_max(m2, scr2v, axis=AX.X))
    vop(lambda e: e.tensor_tensor(scr2v, scr2v, bc(m2), op=ALU.is_equal))
    vop(lambda e: e.tensor_mul(scr2v, scr2v, rii8bc))
    vop(lambda e: e.reduce_max(i2f, scr2v, axis=AX.X))
    vop(lambda e: e.tensor_scalar(i1f, i1f, -1.0, 99.0, op0=ALU.mult, op1=ALU.add))
    vop(lambda e: e.tensor_scalar(i2f, i2f, -1.0, 99.0, op0=ALU.mult, op1=ALU.add))
    v14 = vop(lambda e: e.tensor_tensor(scr2v, L, bc(m1), op=ALU.subtract))
    s1 = sop(lambda e: e.activation(scr, scr, AF.Exp), waits=[("VV", v14)])
    vop(lambda e: e.reduce_sum(se, scr2v, axis=AX.X), waits=[("SS", s1)])
    vop(lambda e: e.reciprocal(wt1, se))
    v17 = vop(lambda e: e.tensor_tensor(wt2, m2, m1, op=ALU.subtract))
    s2 = sop(lambda e: e.activation(wt2, wt2, AF.Exp), waits=[("VV", v17)])
    vop(lambda e: e.tensor_mul(wt2, wt2, wt1), waits=[("SS", s2)])
    vop(lambda e: e.tensor_tensor(ind1, bc(i1f), i8bc, op=ALU.is_equal))
    ind_done = vop(lambda e: e.tensor_tensor(ind2, bc(i2f), i8bc, op=ALU.is_equal))

    ranks = A[:].bitcast(F32).rearrange("p (t e) -> p t e", e=16)
    rk_ev = 0
    cs_ev = 0
    cs_toks = []
    for Gi in range(4):
        mmax = 0
        for trel in range(128):
            tile = Gi * 128 + trel
            bank = trel // 32
            col = (trel % 32) * 16
            for kk, indv in enumerate((ind1, ind2)):
                w = ([("VV", ind_done), ("VV", rk_ev)] + pg.wsnap(attn_snap, ("CC",))) if (trel == 0 and kk == 0) else ()
                mmax = mm(P[bank][:, col + kk * 8: col + kk * 8 + 8], SMB_su,
                          indv[:, tile, :], True, True, waits=w)
        for q4 in range(4):
            rk_ev = vop(lambda e, Gi=Gi, q4=q4: e.tensor_copy(
                ranks[:, Gi * 128 + q4 * 32:Gi * 128 + (q4 + 1) * 32, :],
                P[q4][:, 0:512].rearrange("p (t e) -> p t e", e=16)),
                waits=[("MM", mmax)])
        for h16 in range(4):
            mc = 0
            for tr2 in range(32):
                tile = Gi * 128 + h16 * 32 + tr2
                for kk, indv in enumerate((ind1, ind2)):
                    w = [("VV", cs_ev)] if (tr2 == 0 and kk == 0) else ()
                    mc = mm(P[4][0:1, tr2 * 16 + kk * 8: tr2 * 16 + kk * 8 + 8],
                            SMB_onesc, indv[:, tile, :], True, True, waits=w)
            cs_ev = vop(lambda e, h16=h16: e.tensor_copy(WK[0:1, h16 * 512:(h16 + 1) * 512],
                                                         P[4][0:1, 0:512]),
                        waits=[("MM", mc), (cs_toks[-1] if cs_toks else ("MM", 0))])
        cs_toks.append(dma("sync", "LY", cstage[Gi * 2048:(Gi + 1) * 2048].rearrange("(a b) -> a b", a=1),
                           WK[0:1, 0:2048], waits=[("VV", cs_ev)]))
    cs2 = REPB[:, 0:64]
    lc2 = dma("gpsimd", "LG", cs2.rearrange("p (t1 ke) -> p t1 ke", ke=16),
              cstage.rearrange("(t1 p ke) -> p t1 ke", p=128, ke=16),
              waits=cs_toks + pg.wsnap(attn_snap, ("CC",)))
    mb = mm(P[5][:, 0:64], SMB_su, cs2, True, False, waits=[lc2])
    mtb = mm(P[4][0:1, 0:64], SMB_onesc, cs2, True, True)
    tb, bp = WK2[0:1, 64:128], WK2[0:1, 128:192]
    vtb = vop(lambda e: e.tensor_copy(tb, P[4][0:1, 0:64]), waits=[("MM", mtb)])
    vop(lambda e: e.memset(bp[:, 0:16], 0.0))
    vop(lambda e: e.tensor_copy(bp[:, 16:32], tb[:, 0:16]), waits=[("VV", vtb)])
    vop(lambda e: e.tensor_add(bp[:, 32:48], bp[:, 16:32], tb[:, 16:32]))
    vb3 = vop(lambda e: e.tensor_add(bp[:, 48:64], bp[:, 32:48], tb[:, 32:48]))
    mb2 = mm(P[5][:, 0:64], ONESF[:], bp, False, True, waits=[("VV", vb3)])
    vbase = vop(lambda e: e.tensor_copy(WK2[:, 192:256], P[5][:, 0:64]), waits=[("MM", mb2)])
    lbs = dma("sync", "LY", bstage.rearrange("(t1 p ke) -> p t1 ke", p=128, ke=16),
              WK2[:, 192:256].rearrange("p (t1 ke) -> p t1 ke", ke=16), waits=[("VV", vbase)])
    radd_ev = 0
    prev_lbr2 = ("MM", 0)
    for Gi in range(4):
        lbr2 = dma("sync", "LY", WK2[0:1, 0:2048],
                   bstage[Gi * 2048:(Gi + 1) * 2048].rearrange("(a b) -> a b", a=1),
                   waits=[lbs, prev_lbr2, ("MM", pg.cnt["MM"])])
        prev_lbr2 = lbr2
        for q4 in range(4):
            mq = mm(P[q4][:, 0:512], ONESF[:], WK2[0:1, q4 * 512:(q4 + 1) * 512],
                    True, True, waits=[lbr2, ("VV", rk_ev), ("VV", radd_ev)])
            radd_ev = vop(lambda e, Gi=Gi, q4=q4: e.tensor_add(
                A[:].bitcast(F32)[:, Gi * 2048 + q4 * 512:Gi * 2048 + (q4 + 1) * 512],
                A[:].bitcast(F32)[:, Gi * 2048 + q4 * 512:Gi * 2048 + (q4 + 1) * 512],
                P[q4][:, 0:512]), waits=[("MM", mq)])
    ranks_done = radd_ev

    kept_k, vv_, pp_, jj_ = WK2[:, 0:512], WK2[:, 512:1024], WK2[:, 1024:1536], WK2[:, 1536:2048]
    yr, wfin = se, m2
    stash = WK[:, 0:2048].rearrange("p (t f) -> p t f", f=4)
    yrw_ev = 0
    for kk, (indv, iv, wv_) in enumerate(((ind1, i1f, wt1), (ind2, i2f, wt2))):
        rk = Dx[:, 7680:8192]
        vop(lambda e, indv=indv, kk=kk: e.tensor_tensor(
            scr2v, ranks[:, :, kk * 8:(kk + 1) * 8], indv, op=ALU.mult),
            waits=[("VV", ranks_done), ("VV", yrw_ev)])
        vop(lambda e, rk=rk: e.reduce_sum(rk, scr2v, axis=AX.X))
        vop(lambda e, rk=rk: e.tensor_scalar(kept_k, rk, 8192.0, None, op0=ALU.is_lt))
        vop(lambda e, rk=rk, kk=kk: e.tensor_scalar(vv_, rk, float(kk * CAP), None, op0=ALU.add))
        vi32 = scr[:, 0:512].bitcast(I32)
        pi32 = scr[:, 512:1024].bitcast(I32)
        vop(lambda e, vi32=vi32: e.tensor_copy(vi32, vv_))
        vop(lambda e, vi32=vi32, pi32=pi32: e.tensor_scalar(pi32, vi32, 2047, None, op0=ALU.bitwise_and))
        vop(lambda e, pi32=pi32: e.tensor_copy(pp_, pi32))
        vop(lambda e, vi32=vi32, pi32=pi32: e.tensor_scalar(pi32, vi32, 11, None, op0=ALU.arith_shift_right))
        vop(lambda e, pi32=pi32: e.tensor_copy(jj_, pi32))
        vop(lambda e: e.tensor_scalar(yr, jj_, 16384.0, None, op0=ALU.mult))
        vop(lambda e, iv=iv: e.scalar_tensor_tensor(yr, iv, 2048.0, yr, op0=ALU.mult, op1=ALU.add))
        vop(lambda e: e.tensor_add(yr, yr, pp_))
        vop(lambda e: e.tensor_mul(yr, yr, kept_k))
        vop(lambda e, wv_=wv_: e.tensor_mul(wfin, wv_, kept_k))
        vop(lambda e, kk=kk: e.tensor_copy(stash[:, 0:512, kk:kk + 1],
                                           yr.rearrange("p (t a) -> p t a", a=1)))
        yrw_ev = vop(lambda e, kk=kk: e.tensor_copy(stash[:, 0:512, 2 + kk:3 + kk],
                                                    wfin.rearrange("p (t a) -> p t a", a=1)))
    lyrw = dma("sync", "LY", yrw_s.rearrange("(p t) f -> p t f", p=128), stash,
               waits=[("VV", yrw_ev)])

    i128bc4 = SM_i128.rearrange("p (a r) -> p a r", a=1).broadcast_to([128, 4, 128])
    PTb = E[:, 0:512].rearrange("p (t r) -> p t r", r=128)
    CLO, CHI = E[:, 1024:1536], E[:, 1536:2048]
    CT = WK2[:].bitcast(BF16)
    prev_kk = ("MM", 0)
    sent_v = vop(lambda e: e.memset(SMc(15)[0:1, :], 999999.0), waits=[("VV", yrw_ev)])
    col_toks = []
    for kk, indv in enumerate((ind1, ind2)):
        roe, sel = Dx[:, 7680:8192], Dx[:, 0:512]
        va = vop(lambda e, kk=kk: e.tensor_tensor(
            scr2v, ranks[:, :, kk * 8:(kk + 1) * 8], ohbc, op=ALU.mult),
            waits=[("VV", yrw_ev), lyrw, prev_kk])
        vop(lambda e, roe=roe: e.reduce_sum(roe, scr2v, axis=AX.X))
        vop(lambda e, indv=indv: e.tensor_tensor(scr2v, indv, ohbc, op=ALU.mult))
        selv = vop(lambda e, sel=sel: e.reduce_sum(sel, scr2v, axis=AX.X))
        lrow = dma("sync", "LY", rowstage[kk * 1024:kk * 1024 + 512].rearrange("(a b) -> a b", a=1),
                   roe[0:1, :], waits=[("VV", selv)])
        lsen = dma("sync", "LY", rowstage[kk * 1024 + 512:kk * 1024 + 513].rearrange("(a b) -> a b", a=1),
                   SMc(15)[0:1, :], waits=[("VV", sent_v)])
        for j in range(4):
            col_toks.append(dma("sync", "LY", SM_base[kk][:, j:j + 1],
                                rowstage[kk * 1024 + j * 128:kk * 1024 + (j + 1) * 128].rearrange("(p a) -> p a", a=1),
                                waits=[lrow]))
            col_toks.append(dma("sync", "LY", SM_basep1[kk][:, j:j + 1],
                                rowstage[kk * 1024 + j * 128 + 1:kk * 1024 + (j + 1) * 128 + 1].rearrange("(p a) -> p a", a=1),
                                waits=[lrow, lsen]))
        vop(lambda e, kk=kk: e.tensor_sub(SM_cnt[kk], SM_basep1[kk], SM_base[kk]),
            waits=col_toks[-8:])
        rof = Dx[:, 512:1024]
        mK = mm(P[6][:, 0:512], ONESF[:], roe[0:1, 0:512], True, True, waits=[("VV", selv)])
        vf = vop(lambda e, roe=roe, rof=rof: e.tensor_tensor(rof, roe, P[6][:, 0:512], op=ALU.subtract),
                 waits=[("MM", mK)])
        rofb, selb = C[:, 0:512], C[:, 512:1024]
        vop(lambda e, rof=rof: e.tensor_copy(rofb, rof), waits=[("VV", vf)])
        vh2 = vop(lambda e, sel=sel: e.tensor_copy(selb, sel))
        pt_ev = 0
        clo_ev = 0
        for g4 in range(128):
            rofs = rofb.rearrange("p (t a) -> p t a", a=1)[:, g4 * 4:(g4 + 1) * 4, :].broadcast_to([128, 4, 128])
            sels = selb.rearrange("p (t a) -> p t a", a=1)[:, g4 * 4:(g4 + 1) * 4, :].broadcast_to([128, 4, 128])
            vop(lambda e, rofs=rofs: e.tensor_tensor(PTb, i128bc4, rofs, op=ALU.is_equal),
                waits=[("VV", vh2), ("MM", pg.cnt["MM"])])
            pt_ev = vop(lambda e, sels=sels: e.tensor_tensor(PTb, PTb, sels, op=ALU.mult))
            mc2 = 0
            for t4 in range(4):
                tile = g4 * 4 + t4
                col = (tile % 32) * 4
                mc2 = mm(P[7][:, col:col + 2], PTb[:, t4, :], IDC[:, tile * 2:tile * 2 + 2],
                         True, True, waits=[("VV", pt_ev), ("VV", clo_ev if (tile % 32) == 0 else 0)])
            if g4 % 8 == 7:
                f0 = (g4 // 8) * 32
                pv = P[7][:, 0:128].rearrange("p (t f) -> p t f", f=4)
                clo_ev = vop(lambda e, f0=f0, pv=pv: e.tensor_copy(
                    CLO[:, f0:f0 + 32].rearrange("p (t a) -> p t a", a=1),
                    pv[:, 0:32, 0:1]), waits=[("MM", mc2)])
                clo_ev = vop(lambda e, f0=f0, pv=pv: e.tensor_copy(
                    CHI[:, f0:f0 + 32].rearrange("p (t a) -> p t a", a=1),
                    pv[:, 0:32, 1:2]))
        ct_prev = 0
        for g2 in range(4):
            for plane, srcb in enumerate((CLO, CHI)):
                mt = tr(P[6][:, 0:64].bitcast(BF16), srcb[:, g2 * 128:(g2 + 1) * 128], SMB_id,
                        waits=[("VV", clo_ev), ("VV", ct_prev)])
                ct_prev = vop(lambda e, g2=g2, plane=plane: e.tensor_copy(
                    CT[:, (plane * 4 + g2) * 128:(plane * 4 + g2 + 1) * 128],
                    P[6][:, 0:64].bitcast(BF16)), waits=[("MM", mt)])
        ctc = WK[:, 0:512].rearrange("p (g r) -> p g r", r=128)
        ctlo = CT[:, 0:512].rearrange("p (g r) -> p g r", r=128)
        cthi = CT[:, 512:1024].rearrange("p (g r) -> p g r", r=128)
        vcm = vop(lambda e: e.scalar_tensor_tensor(ctc, cthi, 256.0, ctlo, op0=ALU.mult, op1=ALU.add),
                  waits=[("VV", ct_prev), lyrw])
        cti = WK[:, 512:1024].bitcast(I32).rearrange("p (g r) -> p g r", r=128)
        vci = vop(lambda e: e.tensor_copy(cti, ctc), waits=[("VV", vcm)])
        prev_kk = dma("sync", "LY", ctflat[kk].rearrange("(g p r) a -> p g (r a)", p=128, r=128),
                      cti, waits=[("VV", vci)])
        col_toks.append(prev_kk)
    idx_done = pg.snap()
    if stop == "index":
        return _finish_early()

    w1lo = Bg[:].rearrange("p (dt f) -> p dt f", dt=4)
    w1hi = E[:].rearrange("p (dt f) -> p dt f", dt=4)
    w2sb = C[:].rearrange("p (ft d) -> p ft d", ft=16)
    w2sb2 = A[:].rearrange("p (ft d) -> p ft d", ft=16)
    lw1a = dma("gpsimd", "LG", w1lo, w1c[0:512, :].rearrange("(dt p) f -> p dt f", p=128),
               waits=pg.wsnap(idx_done, ("CC",)))
    lw1b = dma("gpsimd", "LG", w1hi, w1c[512:1024, :].rearrange("(dt p) f -> p dt f", p=128),
               waits=pg.wsnap(idx_done, ("CC",)))
    lw2a = dma("gpsimd", "LG", w2sb, w2c[0:2048, :].rearrange("(ft p) d -> p ft d", p=128),
               waits=pg.wsnap(idx_done, ("CC",)))
    lw2b = dma("gpsimd", "LG", w2sb2, w2c[2048:4096, :].rearrange("(ft p) d -> p ft d", p=128),
               waits=pg.wsnap(idx_done, ("CC",)))

    def w1slice(dt, f0, f1):
        return (w1lo if dt < 4 else w1hi)[:, dt % 4, f0:f1]

    def w2slice(ft, d0, d1):
        return (w2sb if ft < 16 else w2sb2)[:, ft % 16, d0:d1]

    hT_sb = Dg[:].rearrange("p (ft s) -> p ft s", ft=32)
    xeT = [Fg[:, i * 4096:(i + 1) * 4096].rearrange("p (dt s) -> p dt s", dt=8) for i in range(2)]
    gb = W2b[:, 0:2048]
    NCH = 32
    gat_l = [ZT, ZT, ZT, ZT]
    yag_toks = []
    xe_mm = [0, 0]
    xet_marks = {}
    FS = dict(ctm=0, cti=0, h_ev=0, y_ev=0, hT_mm=0,
              h_evs=[0, 0, 0], y_evs=[0, 0, 0])

    def ffn_gather(ch):
        kk = ch // 16
        j0 = (ch % 16) * 4
        pp = ch % 2
        cst_toks = []
        for q in range(4):
            jslot = j0 + q
            icol, icol2 = q * 2, q * 2 + 1
            tj = WK[:, 0:128]
            vtj = vop(lambda e, jslot=jslot, tj=tj: e.tensor_scalar(
                tj, SM_i128, float(jslot * 128), None, op0=ALU.add),
                waits=(col_toks if (ch == 0 and q == 0) else [("MM", FS['ctm'])]))
            tjb = tj.rearrange("p (a s) -> p a s", a=1).broadcast_to([128, 4, 128])
            bbc = SM_base[kk].rearrange("p (t a) -> p t a", a=1).broadcast_to([128, 4, 128])
            bp1c = SM_basep1[kk].rearrange("p (t a) -> p t a", a=1).broadcast_to([128, 4, 128])
            cntc = SM_cnt[kk].rearrange("p (t a) -> p t a", a=1).broadcast_to([128, 4, 128])
            B4 = W2b[:, 2048:2560].rearrange("p (t s) -> p t s", s=128)
            C4 = W2b[:, 2560:3072].rearrange("p (t s) -> p t s", s=128)
            vb4 = vop(lambda e, tjb=tjb, bbc=bbc, B4=B4: e.tensor_tensor(B4, tjb, bbc, op=ALU.is_ge))
            vc4 = vop(lambda e, tjb=tjb, bp1c=bp1c, C4=C4: e.tensor_tensor(C4, tjb, bp1c, op=ALU.is_ge))
            vc5 = vop(lambda e, C4=C4, cntc=cntc: e.tensor_tensor(C4, C4, cntc, op=ALU.mult))
            for t1 in range(4):
                mB = mm(P[4][:, 0:1], B4[:, t1, :], SMB_onesc, t1 == 0, t1 == 3,
                        waits=([("VV", vc5), ("VV", FS['cti'])] if t1 == 0 else ()))
                FS['ctm'] = mm(P[5][:, 0:1], C4[:, t1, :], SMB_onesc, t1 == 0, t1 == 3)
            cidx = WK[:, 128:129]
            vop(lambda e, cidx=cidx: e.tensor_scalar(cidx, P[4][:, 0:1], 128.0, -128.0,
                                                     op0=ALU.mult, op1=ALU.add),
                waits=[("MM", FS['ctm'])])
            vop(lambda e, cidx=cidx: e.scalar_tensor_tensor(
                cidx, SM_iotac, 1.0, cidx, op0=ALU.mult, op1=ALU.add))
            vop(lambda e, cidx=cidx, jslot=jslot: e.tensor_scalar(
                cidx, cidx, float(jslot * 128), None, op0=ALU.add))
            vop(lambda e, cidx=cidx: e.tensor_tensor(cidx, cidx, P[5][:, 0:1], op=ALU.subtract))
            vop(lambda e, cidx=cidx: e.tensor_scalar(cidx, cidx, 0.0, 65535.0,
                                                     op0=ALU.max, op1=ALU.min))
            FS['cti'] = vop(lambda e, cidx=cidx, icol=icol: e.tensor_copy(ISL[:, icol:icol + 1], cidx),
                            waits=[gat_l[2 + q % 2]])
            gq1 = gop(lambda e, icol=icol, icol2=icol2, kk=kk: e.indirect_dma_start(
                out=ISL[:, icol2:icol2 + 1], out_offset=None, in_=ctflat[kk],
                in_offset=IndirectOffsetOnAxis(ISL[:, icol:icol + 1], 0)),
                waits=[("VV", FS['cti'])])
            gq = gop(lambda e, icol2=icol2, q=q: e.indirect_dma_start(
                out=gb[:, (q % 2) * 1024:(q % 2) * 1024 + 1024],
                out_offset=None, in_=xfAG,
                in_offset=IndirectOffsetOnAxis(ISL[:, icol2:icol2 + 1], 0)),
                waits=[gq1, ("CC", ag_attn), gat_l[q % 2]])
            gat_l[2 + q % 2] = gq
            gat_l[q % 2] = dma("sync", "LY", comp_s[ch * 512 + q * 128:ch * 512 + (q + 1) * 128, :],
                               gb[:, (q % 2) * 1024:(q % 2) * 1024 + 1024], waits=[gq])
            cst_toks.append(gat_l[q % 2])
        xm = []
        for dt in range(8):
            xm.append(dma("scalar", "LS", xeT[pp][:, dt, :],
                          comp_s[ch * 512:(ch + 1) * 512, dt * 128:(dt + 1) * 128],
                          waits=cst_toks + [("MM", xe_mm[pp])],
                          transpose=True))
        xet_marks[ch] = xm

    ST1B = (0, 1, 6)
    ST2B = (2, 3, 7)

    def ffn_compute(ch):
        nonlocal ag_n
        pp = ch % 2
        for fs in range(32):
            bk = fs % 3
            psp = P[ST1B[bk]]
            for dt in range(8):
                w = (xet_marks[ch] + [lw1a, lw1b, ("SS", FS['h_evs'][bk])]) if dt == 0 else ()
                m = mm(psp[:, 0:512], w1slice(dt, fs * 128, (fs + 1) * 128),
                       xeT[pp][:, dt, :], dt == 0, dt == 7, waits=w)
            xe_mm[pp] = m
            FS['h_evs'][bk] = FS['h_ev'] = sop(lambda e, fs=fs, psp=psp: e.activation(
                hT_sb[:, fs, :], psp[:, 0:512], AF.Gelu_apprx_tanh),
                waits=[("MM", m), ("MM", FS['hT_mm'] if fs < 2 else 0)])
        yst_l = [ZT, ZT]
        for st in range(4):
            yp = st % 2
            for dh in range(2):
                bk = (st * 2 + dh) % 3
                psy = P[ST2B[bk]]
                for fs in range(32):
                    w = ([("SS", FS['h_ev']), lw2a, lw2b, ("VV", FS['y_evs'][bk]),
                          yst_l[yp]] if fs == 0 else ())
                    m = mm(psy[:, 0:512], hT_sb[:, fs, st * 128:(st + 1) * 128],
                           w2slice(fs, dh * 512, (dh + 1) * 512), fs == 0, fs == 31, waits=w)
                FS['hT_mm'] = m
                FS['y_evs'][bk] = FS['y_ev'] = vop(lambda e, dh=dh, psy=psy, yp=yp: e.tensor_copy(
                    REPB[:, yp * 1024 + dh * 512: yp * 1024 + (dh + 1) * 512],
                    psy[:, 0:512]), waits=[("MM", m)])
            yst_l[yp] = dma("sync", "LY",
                            ycmp_s[ch * 512 + st * 128:ch * 512 + (st + 1) * 128, :],
                            REPB[:, yp * 1024:(yp + 1) * 1024], waits=[("VV", FS['y_ev'])])
            yag_toks.append(yst_l[yp])
        if ch % 4 == 3:
            jag = ch // 4
            gop(lambda e, jag=jag: e.collective_compute(
                "AllGather", ALU.bypass, replica_groups=[list(range(NCORE))],
                ins=[ycmp_s[jag * 2048:(jag + 1) * 2048, :].opt()],
                outs=[yAG[jag * NCORE * 2048:(jag + 1) * NCORE * 2048, :].opt()]),
                waits=list(yag_toks), inc=("CC", 1))
            ag_n += 1
            yag_toks.clear()

    IDCi = IDC[:].bitcast(I32)
    fpre = {}

    def pre_stage(jt):
        rw = WK2[:, 1536 + 4 * jt:1540 + 4 * jt]
        g1 = gop(lambda e, jt=jt, rw=rw: e.indirect_dma_start(
            out=rw, out_offset=None, in_=yrw_s,
            in_offset=IndirectOffsetOnAxis(OWN[:, jt:jt + 1], 0)),
            waits=[lyrw] + (pg.wsnap(idx_done, ("CC",)) if jt == 0 else []))
        r1v = vop(lambda e, rw=rw, jt=jt: e.tensor_copy(IDCi[:, 2 * jt:2 * jt + 1], rw[:, 0:1]),
                  waits=[g1] + (pg.wsnap(idx_done, ("CC",)) if jt == 0 else []))
        r2v = vop(lambda e, rw=rw, jt=jt: e.tensor_copy(IDCi[:, 2 * jt + 1:2 * jt + 2], rw[:, 1:2]))
        fpre[jt] = (r1v, r2v)

    ffn_gather(0)
    for ch in range(NCH):
        if ch + 1 < NCH:
            ffn_gather(ch + 1)
        ffn_compute(ch)
        pre_stage(2 * ch)
        pre_stage(2 * ch + 1)
    y_ev = FS['y_ev']
    if "ycmp" in dbg:
        dma("sync", "LY", dbg["ycmp"], ycmp_s, waits=pg.qbar("LY") + [("VV", y_ev)])
    ffn_snap = pg.snap()
    if stop == "ffn":
        return _finish_early()

    lfb = dma("sync", "LY", WK2[0:1, 0:1024], fbi.rearrange("(a d) -> a d", a=1),
              waits=pg.wsnap(ffn_snap, ("CC",)))
    ovf = vop(lambda e: e.memset(WK2[0:1, 1024:1152], 1.0), waits=pg.wsnap(ffn_snap, ("CC",)))
    repf_ev = 0
    for sl in range(2):
        m = mm(P[6][:, 0:512], WK2[0:1, 1024:1152], WK2[0:1, sl * 512:(sl + 1) * 512],
               True, True, waits=[lfb, ("VV", ovf), ("VV", repf_ev)])
        repf_ev = vop(lambda e, sl=sl: e.tensor_copy(REPF[:, sl * 512:(sl + 1) * 512], P[6][:, 0:512]),
                      waits=[("MM", m)])
    NP = 4
    SKEW = 2
    Af = A[:].bitcast(F32)
    Bf = Bg[:].bitcast(F32)
    ffw = [("MM", pg.cnt["MM"]), ("VV", pg.cnt["VV"]), ("SS", pg.cnt["SS"])]
    st_l = [ZT] * NP
    fmarks = {}

    def fin_gather(jt):
        b, tt = divmod(jt, NT)
        p = jt % NP
        first = [] if jt >= NP else ffw
        yb = Dg[:, p * 2048:(p + 1) * 2048]
        xb = Af[:, p * 1024:(p + 1) * 1024]
        g2 = gop(lambda e, yb=yb, jt=jt: e.indirect_dma_start(
            out=yb[:, 0:1024], out_offset=None, in_=yAG,
            in_offset=IndirectOffsetOnAxis(IDCi[:, 2 * jt:2 * jt + 1], 0)),
            waits=[("VV", fpre[jt][0]), ("CC", ag_n), st_l[p]] + first)
        g3 = gop(lambda e, yb=yb, jt=jt: e.indirect_dma_start(
            out=yb[:, 1024:2048], out_offset=None, in_=yAG,
            in_offset=IndirectOffsetOnAxis(IDCi[:, 2 * jt + 1:2 * jt + 2], 0)),
            waits=[("VV", fpre[jt][1])])
        lx2 = dma("sync", "LY", xb, xf32_s[b][tt * 128:(tt + 1) * 128, :],
                  waits=[st_l[p]] + first)
        fmarks[jt] = (g2, g3, lx2)

    def fin_compute(jt):
        b, tt = divmod(jt, NT)
        p = jt % NP
        g2, g3, lx2 = fmarks.pop(jt)
        rw = WK2[:, 1536 + 4 * jt:1540 + 4 * jt]
        yb = Dg[:, p * 2048:(p + 1) * 2048]
        xb = Af[:, p * 1024:(p + 1) * 1024]
        acc = Bf[:, p * 1024:(p + 1) * 1024]
        a1 = vop(lambda e, acc=acc, yb=yb, rw=rw: e.scalar_tensor_tensor(
            acc, yb[:, 0:1024], rw[:, 2:3], REPF[:], op0=ALU.mult, op1=ALU.add),
            waits=[g2, g3, ("VV", repf_ev)])
        a2 = vop(lambda e, acc=acc, yb=yb, rw=rw: e.scalar_tensor_tensor(
            acc, yb[:, 1024:2048], rw[:, 3:4], acc, op0=ALU.mult, op1=ALU.add))
        a3 = vop(lambda e, acc=acc, xb=xb: e.tensor_add(acc, acc, xb), waits=[lx2])
        nm, var, rstd, sqs = SMc(p * 4), SMc(p * 4 + 1), SMc(p * 4 + 2), SMc(p * 4 + 3)
        vop(lambda e, nm=nm, acc=acc: e.reduce_sum(nm, acc, axis=AX.X, negate=True),
            waits=[("VV", a3)])
        vop(lambda e, nm=nm: e.tensor_scalar(nm, nm, 1.0 / 1024.0, None, op0=ALU.mult))
        v3 = vop(lambda e, nm=nm, acc=acc: e.tensor_scalar(acc, acc, nm, None, op0=ALU.add))
        s1 = vop(lambda e, sqs=sqs, acc=acc, xb=xb: e.scalar_tensor_tensor(
            xb, acc, 1.0, acc, op0=ALU.mult, op1=ALU.mult, accum_out=sqs),
            waits=[("VV", v3)])
        v4 = vop(lambda e, var=var, sqs=sqs: e.tensor_scalar(var, sqs, 1.0 / 1024.0, 1e-5,
                                                             op0=ALU.mult, op1=ALU.add))
        s2 = sop(lambda e, rstd=rstd, var=var: e.activation(rstd, var, AF.Ln), waits=[("VV", v4)])
        s3 = sop(lambda e, rstd=rstd: e.activation(rstd, rstd, AF.Exp, scale=-0.5))
        v5 = vop(lambda e, rstd=rstd, acc=acc: e.tensor_scalar(acc, acc, rstd, None, op0=ALU.mult),
                 waits=[("SS", s3)])
        st_l[p] = dma("sync", "LY", out_c[b][tt * 128:(tt + 1) * 128, :], acc, waits=[("VV", v5)])

    NTILE = NSTEP * NT
    for jt in range(NTILE):
        fin_gather(jt)
        if jt >= SKEW:
            fin_compute(jt - SKEW)
    for jt in range(NTILE - SKEW, NTILE):
        fin_compute(jt)
    pg.emit("sync", lambda e: e.nop(), waits=list(st_l))

    pg.realize()
    return nc


_NC_CACHE = None


def kernel(_stop=None, _dbg=None, **inputs):
    global _NC_CACHE
    perm = np.concatenate([np.arange(0, 64, 2), np.arange(1, 64, 2)])
    PERM = np.concatenate([h * 64 + perm for h in range(H)])
    f32 = np.float32
    Wq = np.ascontiguousarray(inputs["L0_attn_Wq"][:, PERM], dtype=f32)
    bq = np.ascontiguousarray(np.asarray(inputs["L0_attn_bq"], dtype=f32)[PERM].reshape(8, 128).T)
    Wk = np.ascontiguousarray(inputs["L0_attn_Wk"][:, PERM], dtype=f32)
    bk = np.ascontiguousarray(np.asarray(inputs["L0_attn_bk"], dtype=f32)[PERM].reshape(8, 128).T)
    x = np.asarray(inputs["L0_attn_x"], dtype=f32)
    if _stop is None and _dbg is None:
        if _NC_CACHE is None:
            _NC_CACHE = build_program()
        nc = _NC_CACHE
    else:
        nc = build_program(stop=_stop, dbg_keys=_dbg or ())
    in_maps = []
    for c in range(NCORE):
        batches = [c + 8 * bb for bb in range(NSTEP)]
        oh = np.zeros((128, 8), f32)
        oh[:, c] = 1.0
        tglob = np.concatenate([(np.arange(S, dtype=np.int64) + bb * S) for bb in batches])
        ownpos = ((tglob % 128) * 512 + tglob // 128).astype(np.int32)
        own = np.ascontiguousarray(ownpos.reshape(64, 128).T)
        in_maps.append({
            "x4": np.ascontiguousarray(x[batches]),
            "wqp": Wq, "bqp": bq, "wkp": Wk, "bkp": bk,
            "wvi": np.ascontiguousarray(inputs["L0_attn_Wv"], dtype=f32),
            "bvi": np.ascontiguousarray(inputs["L0_attn_bv"], dtype=f32),
            "wgi": np.ascontiguousarray(inputs["L0_attn_Wg"], dtype=f32),
            "bgi": np.ascontiguousarray(inputs["L0_attn_bg"], dtype=f32),
            "woi": np.ascontiguousarray(inputs["L0_attn_Wo"], dtype=f32),
            "boi": np.ascontiguousarray(inputs["L0_attn_bo"], dtype=f32),
            "rwi": np.ascontiguousarray(inputs["L0_ffn_router_w"], dtype=f32),
            "w1c": np.ascontiguousarray(inputs["L0_ffn_w1"][c], dtype=f32),
            "w2c": np.ascontiguousarray(inputs["L0_ffn_w2"][c], dtype=f32),
            "fbi": np.ascontiguousarray(np.asarray(inputs["L0_ffn_bias"], dtype=f32).reshape(D)),
            "onehot": oh,
            "ownrows": own,
        })
    res = run_bass_kernel_spmd(nc, in_maps, core_ids=list(range(NCORE)))
    if _stop is not None or _dbg:
        return res
    out = np.empty((B, S, D), np.float32)
    for c in range(NCORE):
        oc = res.results[c]["out_c"]
        for bb in range(NSTEP):
            out[c + 8 * bb] = oc[bb]
    return out
